# version 18
# baseline (speedup 1.0000x reference)
import os
import math
import contextlib
import numpy as np
import concourse.bass as bass
import concourse.mybir as mybir
from concourse.bass_utils import run_bass_kernel_spmd

F32 = mybir.dt.float32
BF16 = mybir.dt.bfloat16
FP8 = mybir.dt.float8e5
ALU = mybir.AluOpType
ACT = mybir.ActivationFunctionType
AX = mybir.AxisListType

B, T, D = 8, 2048, 1024
NDT = 8
CK = 128
LN2 = math.log(2.0)

ENGS = ("sync", "tensor", "vector", "scalar", "gpsimd")
ACT_CONSTS = [LN2, -LN2, 2 * LN2, -2 * LN2, 7 * LN2, -7 * LN2, 1e-8, -math.log(16.0), LN2 / 2.0, 7 * LN2 / 2.0,
              -1.0, 0.5, 2.0, 1e-6, 1e-5, 1e-12, 0.03125, 32.0, 0.0, 1.0]


class Sem:
    __slots__ = ("name", "h", "n")

    def __init__(self, name):
        self.name, self.h, self.n = name, None, 0


class KB:
    def __init__(self):
        self.q = {e: [] for e in ENGS}
        self.sems = []

    def sem(self, name):
        s = Sem(name)
        self.sems.append(s)
        return s

    def raw(self, eng, fn):
        self.q[eng].append(fn)

    def op(self, eng, fn, sem=None, d=1):
        if sem is None:
            self.q[eng].append(fn)
            return None
        sem.n += d
        def run(e, fn=fn, sem=sem, d=d):
            fn(e).then_inc(sem.h, d)
        self.q[eng].append(run)
        return sem.n

    def dma(self, eng, out, in_, sem, accum=None):
        kw = {}
        if accum is not None:
            kw["accum_op"] = accum
        return self.op(eng, lambda e, o=out, i=in_, kw=kw: e.dma_start(out=o, in_=i, **kw),
                       sem=sem, d=16)

    def wait(self, eng, sem, val):
        self.maxwait = getattr(self, "maxwait", {})
        self.maxwait[sem.name] = max(self.maxwait.get(sem.name, 0), val)
        self.q[eng].append(lambda e, s=sem, v=val: e.wait_ge(s.h, v))

    def drain(self, eng):
        self.q[eng].append(lambda e: e.drain())

    def inc(self, eng, sem, d=1):
        sem.n += d
        def run(e, s=sem, d=d):
            e.drain()
            e.sem_inc(s.h, d)
        self.q[eng].append(run)
        return sem.n


class NS:
    pass


def ts_op(k, eng, out, in_, s1, op0, s2=None, op1=None, sem=None, accum=None):
    def fn(e, out=out, in_=in_, s1=s1, s2=s2, op0=op0, op1=op1, accum=accum):
        kw = {}
        if accum is not None:
            kw["accum_out"] = accum
        if op1 is not None:
            return e.tensor_scalar(out, in_, s1, s2, op0, op1, **kw)
        return e.tensor_scalar(out, in_, s1, None, op0, **kw)
    return k.op(eng, fn, sem=sem)


def tt_op(k, eng, out, a, b, op, sem=None):
    return k.op(eng, lambda e, out=out, a=a, b=b, op=op: e.tensor_tensor(out, a, b, op),
                sem=sem)


def stt_op(k, eng, out, in0, scalar, op0, in1, op1, sem=None):
    return k.op(eng, lambda e, out=out, in0=in0, s=scalar, op0=op0, in1=in1, op1=op1:
                e.scalar_tensor_tensor(out, in0, s, in1, op0, op1), sem=sem)


def act_op(k, out, in_, func, bias=None, scale=None, accum=None, sem=None):
    def fn(e, out=out, in_=in_, func=func, bias=bias, scale=scale, accum=accum):
        kw = {}
        if bias is not None:
            kw["bias"] = bias
        if scale is not None:
            kw["scale"] = scale
        if accum is not None:
            kw["accum_out"] = accum
        return e.activation(out, in_, func, **kw)
    return k.op("scalar", fn, sem=sem)


def copy_op(k, eng, out, in_, sem=None):
    if eng == "scalar":
        return k.op(eng, lambda e, o=out, i=in_: e.copy(o, i), sem=sem)
    return k.op(eng, lambda e, o=out, i=in_: e.tensor_copy(o, i), sem=sem)


def mm(k, out, lhsT, rhs, start, stop, sem=None):
    return k.op("tensor",
                lambda e, o=out, l=lhsT, r=rhs, st=start, sp=stop:
                e.matmul(o, l, r, start=st, stop=sp), sem=sem)


def tr_op(k, out, in_, ident, sem=None):
    return k.op("tensor", lambda e, o=out, i=in_, d=ident: e.transpose(o, i, d), sem=sem)



def emit_magic(k, c):
    C = c.cols[:, 68:69]
    Tm = c.cols[:, 69:70]
    k.raw("vector", lambda e: e.memset(C, 1.0))
    k.drain("vector")
    for _ in range(64):
        ts_op(k, "vector", Tm, C, 1.0, ALU.add)
        k.drain("vector")
        tt_op(k, "vector", Tm, Tm, C, ALU.subtract)
        k.drain("vector")
        ts_op(k, "vector", Tm, Tm, 0.5, ALU.is_lt, -1.0, ALU.mult)
        k.drain("vector")
        ts_op(k, "vector", Tm, Tm, 2.0, ALU.add)
        k.drain("vector")
        tt_op(k, "vector", C, C, Tm, ALU.mult)
        k.drain("vector")
    ts_op(k, "vector", C, C, 0.75, ALU.mult)
    k.drain("vector")
    c.tok_magic = k.inc("vector", c.sVE)


def q_chain(k, c, eng, out, in_, M, emax_neg, W, ties_even=None, relu_bias=None,
            bias_col=None, nonneg=False, sem=None, first_sem=None,
            first_on_scalar=False):
    semE = c.sVE if eng == "vector" else c.sGP
    MAG = c.cols[:, 68:69]
    k.drain(eng)
    if first_on_scalar:
        if relu_bias is not None:
            c.first_tok = act_op(k, W(0), in_, ACT.Relu, bias=relu_bias,
                                 sem=first_sem)
            nonneg = True
        elif bias_col is not None:
            c.first_tok = act_op(k, W(0), in_, ACT.Copy, bias=bias_col,
                                 sem=first_sem)
        else:
            c.first_tok = copy_op(k, "scalar", W(0), in_, sem=first_sem)
        tfs = k.inc("scalar", c.sSC)
        k.wait(eng, c.sSC, tfs)
    elif relu_bias is not None:
        c.first_tok = ts_op(k, eng, W(0), in_, relu_bias, ALU.add, 0.0, ALU.max,
                            sem=first_sem)
        nonneg = True
    elif bias_col is not None:
        c.first_tok = ts_op(k, eng, W(0), in_, bias_col, ALU.add, sem=first_sem)
    else:
        c.first_tok = copy_op(k, eng, W(0), in_, sem=first_sem)
    k.drain(eng)
    signed = not nonneg
    if signed:
        t0e = k.inc(eng, semE)
        k.wait("scalar", semE, t0e)
        act_op(k, W(5), W(0), ACT.Sign)
        ta = act_op(k, W(1), W(0), ACT.Abs, sem=c.sSC)
        k.wait(eng, c.sSC, ta)
        ts_op(k, eng, W(1), W(1), 1e-38, ALU.max)
    else:
        ts_op(k, eng, W(1), W(0), 1e-38, ALU.max)
    t1 = k.inc(eng, semE)
    k.wait("scalar", semE, t1)
    t2 = act_op(k, W(2), W(1), ACT.Ln, sem=c.sSC)
    k.wait(eng, c.sSC, t2)
    ts_op(k, eng, W(2), W(2), -1.0 / LN2, ALU.mult, 0.5, ALU.add)
    k.drain(eng)
    ts_op(k, eng, W(2), W(2), MAG, ALU.add)
    k.drain(eng)
    ts_op(k, eng, W(2), W(2), MAG, ALU.subtract)
    k.drain(eng)
    ts_op(k, eng, W(2), W(2), float(emax_neg), ALU.min)
    t3 = k.inc(eng, semE)
    k.wait("scalar", semE, t3)
    split = emax_neg > 100
    if split:
        act_op(k, W(3), W(2), ACT.Exp, scale=LN2 / 2.0, bias=float(M) * LN2 / 2.0)
    else:
        act_op(k, W(3), W(2), ACT.Exp, scale=LN2, bias=float(M) * LN2)
    t4 = act_op(k, W(4), W(2), ACT.Exp, scale=-LN2, bias=-float(M) * LN2,
                sem=c.sSC)
    k.wait(eng, c.sSC, t4)
    tt_op(k, eng, W(2), W(1), W(3), ALU.mult)
    k.drain(eng)
    if split:
        tt_op(k, eng, W(2), W(2), W(3), ALU.mult)
        k.drain(eng)
    ts_op(k, eng, W(2), W(2), MAG, ALU.add)
    k.drain(eng)
    ts_op(k, eng, W(2), W(2), MAG, ALU.subtract)
    k.drain(eng)
    tt_op(k, eng, W(2), W(2), W(4), ALU.mult)
    k.drain(eng)
    if signed:
        tok = tt_op(k, eng, out, W(2), W(5), ALU.mult, sem=sem)
    else:
        tok = copy_op(k, eng, out, W(2), sem=sem)
    k.drain(eng)
    return tok



def build_program(debug=False, probe=False):
    nc = bass.Bass("TRN2", target_bir_lowering=False)
    k = KB()
    c = NS()
    c.nc = nc
    c.debug = debug

    def din(name, shape, dt=F32):
        return nc.dram_tensor(name, list(shape), dt, kind="ExternalInput").ap()

    c.x_fm = din("x_fm", (D, T))
    c.out_fm = nc.dram_tensor("out_fm", [D, T], F32, kind="ExternalOutput").ap()
    if debug:
        c.dbg = {}
        for nm in ("dbg_x1", "dbg_s0", "dbg_xn0", "dbg_spre",
                   "dbg_xn1", "dbg_s1", "dbg_x2",
                   "dbg_xn2", "dbg_s2", "dbg_x3",
                   "dbg_xn3", "dbg_s3", "dbg_x4"):
            c.dbg[nm] = nc.dram_tensor(nm, [D, T], F32, kind="ExternalOutput").ap()

    c.IDT = din("c_ident", (16, 16))
    c.IDTB = din("c_identb", (128, 128), BF16)
    c.ONESROW = din("c_onesrow", (65, 128))
    c.ONESCOLB = din("c_onescolb", (128, 1), BF16)
    c.MBIG = din("c_mbig", (128, 128))
    c.REPL2 = din("c_repl2", (16, 8 * 128))
    c.ACTC = din("c_actc", (128, 20))
    c.DROP = din("dropscale", (D, T), BF16)

    c.L0WqT = din("L0WqT", (D, D)); c.L0WkT = din("L0WkT", (D, D))
    c.L0WvT = din("L0WvT", (D, D)); c.L0WoT = din("L0WoT", (D, D))
    c.L0WwT = din("L0WwT", (D, 16)); c.L0WeT = din("L0WeT", (D, 16))
    c.L0bw = din("L0bw", (16, 1)); c.L0be = din("L0be", (16, 1))
    c.L0W1T = din("L0W1T", (D, 4096)); c.L0W2T = din("L0W2T", (4096, D))
    c.L0b1 = din("L0b1", (128, 32)); c.L0b2 = din("L0b2", (128, 8))
    c.L1Wq = din("L1Wq", (D, D)); c.L1Wk = din("L1Wk", (D, D)); c.L1Wv = din("L1Wv", (D, D))
    c.OMGT2 = din("c_omgt2", (128, 256), BF16)
    c.NEGH = din("c_negh", (128, 1))
    c.L1rw = din("L1rw", (D, 8))
    c.L1w1 = din("L1w1", (8, D, 2816)); c.L1v1 = din("L1v1", (8, D, 2816))
    c.L1w2 = din("L1w2", (8, 2816, D)); c.L1bias = din("L1bias", (128, 8))
    c.L2WqT = din("L2WqT", (D, D)); c.L2WkT = din("L2WkT", (D, D))
    c.L2WvT = din("L2WvT", (D, D)); c.L2WuT = din("L2WuT", (D, D))
    c.L2se = din("L2se", (4096, D))
    c.L2w1 = din("L2w1", (D, 4096)); c.L2w2 = din("L2w2", (4096, D))
    c.L2gx = din("L2gx", (128, 8)); c.L2bu = din("L2bu", (128, 8))
    c.L2gs = din("L2gs", (1, D))
    c.IDTF = din("c_identf", (128, 128))
    c.L3WinT = din("L3WinT", (D, 4096)); c.L3WoutT = din("L3WoutT", (2048, D))
    c.L3WxT = din("L3WxT", (2048, 96)); c.L3WdtT = din("L3WdtT", (64, 2048))
    c.L3cw = din("L3cw", (128, 64)); c.L3cb = din("L3cb", (128, 16))
    c.L3D = din("L3D", (128, 16)); c.L3AL = din("L3AL", (128, 256))
    c.L3bdt = din("L3bdt", (128, 16))
    c.L3gwT = din("L3gwT", (D, 32))
    c.L3egT = din("L3egT", (32, D, 512)); c.L3euT = din("L3euT", (32, D, 512))
    c.L3edT = din("L3edT", (32, 512, D))
    c.L3sgT = din("L3sgT", (D, 512)); c.L3suT = din("L3suT", (D, 512))
    c.L3sdT = din("L3sdT", (512, D))

    c.qW1s = nc.dram_tensor("qW1s", [32, 128, 1024], FP8, kind="Internal").ap()
    c.qW2s = nc.dram_tensor("qW2s", [8, 128, 4096], FP8, kind="Internal").ap()
    c.WbfA = nc.dram_tensor("WbfA", [4, D, D], BF16, kind="Internal").ap()
    c.qxs = nc.dram_tensor("qxs", [NDT, 128, T], FP8, kind="Internal").ap()
    c.L1wb = nc.dram_tensor("L1wb", [8, 2, D, 2816], BF16, kind="Internal").ap()
    c.L1w2b = nc.dram_tensor("L1w2b", [8, 2816, D], BF16, kind="Internal").ap()
    c.ctrd = nc.dram_tensor("ctrd", [16, 8, 128], F32, kind="Internal").ap()
    c.qfms = nc.dram_tensor("qfms", [D, T], BF16, kind="Internal").ap()
    c.kfms = nc.dram_tensor("kfms", [D, T], BF16, kind="Internal").ap()
    c.vtms = nc.dram_tensor("vtms", [T, D], BF16, kind="Internal").ap()
    c.yacc = nc.dram_tensor("yacc", [8, 128, T], F32, kind="Internal").ap()
    c.l2w1b = nc.dram_tensor("l2w1b", [D, 4096], BF16, kind="Internal").ap()
    c.l2w2b = nc.dram_tensor("l2w2b", [4096, D], BF16, kind="Internal").ap()
    c.senTb = nc.dram_tensor("senTb", [D, 4096], BF16, kind="Internal").ap()
    c.slotsTb = nc.dram_tensor("slotsTb", [D, 4096], BF16, kind="Internal").ap()
    c.procb = nc.dram_tensor("procb", [4096, D], BF16, kind="Internal").ap()
    c.expb = nc.dram_tensor("expb", [4096, T], BF16, kind="Internal").ap()
    dk3 = "ExternalOutput" if (debug or probe) else "Internal"
    c.dbg_dt = nc.dram_tensor("dbg_dt", [2048, T], F32, kind=dk3).ap()
    c.dbg_da = nc.dram_tensor("dbg_da", [2048, T], F32, kind=dk3).ap()
    c.dbg_ys = nc.dram_tensor("dbg_ys", [2048, T], F32, kind=dk3).ap()
    c.uzb = nc.dram_tensor("uzb", [4096, T], BF16, kind=dk3).ap()
    c.ucfb = nc.dram_tensor("ucfb", [2048, T], BF16, kind=dk3).ap()
    c.y3b = nc.dram_tensor("y3b", [2048, T], BF16, kind=dk3).ap()
    c.bcd = nc.dram_tensor("bcd", [32, T], F32, kind=dk3).ap()
    c.l3wi = nc.dram_tensor("l3wi", [D, 4096], BF16, kind="Internal").ap()
    c.l3wo = nc.dram_tensor("l3wo", [2048, D], BF16, kind="Internal").ap()
    c.l3wx = nc.dram_tensor("l3wx", [2048, 96], BF16, kind="Internal").ap()
    c.l3wdt = nc.dram_tensor("l3wdt", [64, 2048], BF16, kind="Internal").ap()
    c.l3eg = nc.dram_tensor("l3eg", [32, D, 512], BF16, kind="Internal").ap()
    c.l3eu = nc.dram_tensor("l3eu", [32, D, 512], BF16, kind="Internal").ap()
    c.l3ed = nc.dram_tensor("l3ed", [32, 512, D], BF16, kind="Internal").ap()
    c.l3sg = nc.dram_tensor("l3sg", [D, 512], BF16, kind="Internal").ap()
    c.l3su = nc.dram_tensor("l3su", [D, 512], BF16, kind="Internal").ap()
    c.l3sd = nc.dram_tensor("l3sd", [512, D], BF16, kind="Internal").ap()
    c.l3gw = nc.dram_tensor("l3gw", [D, 32], BF16, kind="Internal").ap()
    c.ctr32 = nc.dram_tensor("ctr32", [16, 32, 128], F32, kind="Internal").ap()

    with contextlib.ExitStack() as st:
        def sb(name, shape, dt=F32):
            return st.enter_context(nc.sbuf_tensor(name, list(shape), dt))

        c.sb = sb
        c.xres = sb("xres", (128, NDT * T))
        c.xnbf = sb("xnbf", (128, NDT * T), BF16)
        c.sres = sb("sres", (128, NDT * T), BF16)
        c.aQ = sb("aQ", (128, NDT * 256), BF16)
        c.aK = sb("aK", (128, NDT * 256), BF16)
        c.aKtm = sb("aKtm", (128, 1024), BF16)
        c.aVtm = sb("aVtm", (128, 2 * 16 * 65), BF16)
        c.F32 = sb("f32ws", (128, 2304))
        c.G32 = sb("g32ws", (128, 2048))
        c.wbf = [sb(f"wbf{i}", (128, 1024), BF16) for i in range(2)]
        c.rows = sb("rows", (1, 1024))
        c.Tst = sb("Tst", (128, 8 * 65))
        c.cols = sb("cols", (128, 70))
        c.cIDT = sb("cIDT", (16, 16))
        c.cIDTB = sb("cIDTB", (128, 128), BF16)
        c.cONESROW = sb("cONESROW", (65, 128))
        c.cONESCOLB = sb("cONESCOLB", (128, 1), BF16)
        c.cACTC = sb("cACTC", (128, 20))
        c.bfS = sb("bfS", (128, 384), BF16)
        c.bfT = sb("bfT", (128, 256), BF16)
        c.st0 = contextlib.ExitStack()
        def sb0(name, shape, dt=F32):
            return c.st0.enter_context(nc.sbuf_tensor(name, list(shape), dt))
        c.qxb = sb0("qxb", (128, 2048), FP8)
        c.hq = sb0("hq", (128, 32 * 256), FP8)
        c.w8 = [sb0(f"w8{i}", (128, 1024), FP8) for i in range(2)]
        c.gates = sb0("gates", (16, T))
        c.wgbf = sb0("wgbf", (16, T), BF16)
        c.crows = sb0("crows", (1, 512))
        c.sm16 = sb0("sm16", (16, 416))
        c.efr = sb0("efr", (128, 16))
        c.kdT = sb0("kdT", (128, 32))
        c.cMBIG = sb0("cMBIG", (128, 128))
        c.cREPL2 = sb0("cREPL2", (16, 8 * 128))

        c.ps = [st.enter_context(nc.psum_tensor(f"ps{i}", [128, 512], F32))
                for i in range(7)]
        c.ps7b = st.enter_context(nc.psum_tensor("ps7b", [128, 512], BF16))

        for i, v in enumerate(ACT_CONSTS):
            nc.const_aps.register(F32, v, c.cACTC[:, i:i + 1])
        for nm in ("sX", "sW", "sMM", "sVE", "sSC", "sGP", "sD2", "sD3", "sOUT"):
            setattr(c, nm, k.sem(nm))
        for s in k.sems:
            s.h = st.enter_context(nc.semaphore(s.name))

        plan(k, c)
        for s in k.sems:
            mw = getattr(k, "maxwait", {}).get(s.name, 0)
            assert mw <= s.n, f"sem {s.name}: waits up to {mw} but only {s.n} incs"

        with nc.Block() as blk:
            for eng in ENGS:
                getattr(blk, eng)(lambda e, q=list(k.q[eng]): [f(e) for f in q])

    return nc


def wait_free(k, c, eng, key):
    ent = c.bank_free.get(key)
    if ent:
        sem, tok = ent
        k.wait(eng, getattr(c, sem), tok)


def plan(k, c):
    c.bank_free = {}
    for dt in range(NDT):
        k.dma("sync", c.xres[:, dt * T:(dt + 1) * T], c.x_fm[dt * 128:(dt + 1) * 128, :], c.sX)
    for dst, src in ((c.cIDT, c.IDT), (c.cIDTB, c.IDTB), (c.cONESROW, c.ONESROW),
                     (c.cONESCOLB, c.ONESCOLB), (c.cMBIG, c.MBIG), (c.cREPL2, c.REPL2),
                     (c.cACTC, c.ACTC)):
        k.dma("sync", dst[:], src, c.sX)
    tX = c.sX.n
    k.wait("scalar", c.sX, tX)

    k.raw("vector", lambda e: e.memset(
        c.aVtm[:].rearrange("p (j v) -> p j v", v=65)[:, :, 64:65], 1.0))
    k.raw("vector", lambda e: e.memset(c.cols[:, 67:68], 1.0))
    k.drain("vector")
    emit_magic(k, c)
    k.wait("gpsimd", c.sVE, c.tok_magic)
    k.wait("scalar", c.sVE, c.tok_magic)
    gen = l0_weight_prep_gen(k, c)
    c.pump = lambda n=1: [next(gen, None) for _ in range(n)]
    c.pump(2)
    layernorm(k, c, c.sX, tX, make_qx=True)
    if c.debug:
        dump_dbg(k, c, "dbg_xn0", src="xnbf")
    l0_attn(k, c)
    if c.debug:
        dump_dbg(k, c, "dbg_s0", src="sres")
    c.pump(1000)
    l0_ffn_combine(k, c)
    if c.debug:
        dump_dbg(k, c, "dbg_x1", src="xres")

    c.st0.close()
    c.wbg = [c.sb(f"wbg{i}", (128, 1024), BF16) for i in range(2)]
    c.colsB = c.sb("colsB", (128, 32))
    c.crow5 = c.sb("crow5", (1, 512))
    c.cIDF = c.sb("cIDF", (128, 128))
    c.st1 = contextlib.ExitStack()
    def sb1(name, shape, dt=F32):
        return c.st1.enter_context(c.nc.sbuf_tensor(name, list(shape), dt))
    c.hA = sb1("hA", (128, 6144), BF16)
    c.cOMGT2 = sb1("cOMGT2", (128, 256), BF16)
    c.cNEGH = sb1("cNEGH", (128, 1))
    c.rwb = sb1("rwb", (128, 64), BF16)
    gen1 = l1_cast_gen(k, c)
    c.pump = lambda n=1: [next(gen1, None) for _ in range(n)]
    layernorm(k, c, c.sVE, c.tok_l0)
    if c.debug:
        dump_dbg(k, c, "dbg_xn1", src="xnbf")
    proj_qkv(k, c, "tok_l1qkv", l1_consts=True)
    l1_attn_heads(k, c)
    if c.debug:
        dump_dbg(k, c, "dbg_s1", src="sres")
    l1_router(k, c)
    l1_ffn_combine(k, c)
    if c.debug:
        dump_dbg(k, c, "dbg_x2", src="xres")

    gen2 = l2_cast_gen(k, c)
    gen3c = l3_cast_gen(k, c)
    def pump_l2l3(n=1):
        for _ in range(n):
            if not hasattr(c, "tok_l2w"):
                next(gen2, None)
            else:
                next(gen3c, None)
    c.pump = pump_l2l3
    layernorm(k, c, c.sVE, c.tok_l1)
    if c.debug:
        dump_dbg(k, c, "dbg_xn2", src="xnbf")
    proj_qkv(k, c, "tok_l2qkv")
    gen2s = l2_sen_gen(k, c)
    c.pump2 = lambda n=1: [next(gen2s, None) for _ in range(n)]
    c.pump2(1)
    l2_attn_heads(k, c)
    if c.debug:
        dump_dbg(k, c, "dbg_s2", src="sres")
    l2_xnr(k, c)
    l2_slot_ffn(k, c)
    if c.debug:
        dump_dbg(k, c, "dbg_x3", src="xres")

    c.st1.close()
    c.M32 = c.sb("M32", (128, 1025))
    c.MF = c.sb("MF", (128, 1536))
    c.hC = c.sb("hC", (128, 2048), BF16)
    c.brow = c.sb("brow", (1, 1024))
    c.crow = c.sb("crow", (1, 1024))
    c.cA = c.sb("cA", (128, 32))
    c.colsC = c.sb("colsC", (128, 96))
    c.mwb = c.sb("mwb", (128, 224), BF16)
    c.gwb = c.sb("gwb", (128, 256), BF16)
    layernorm(k, c, c.sVE, c.tok_l2)
    if c.debug:
        dump_dbg(k, c, "dbg_xn3", src="xnbf")
    l3_pre(k, c)
    l3_router(k, c)
    gen3m = l3_mamba_gen(k, c)
    c.pump2 = lambda n=1: [next(gen3m, None) for _ in range(n)]
    while not hasattr(c, "tok_mamba"):
        c.pump2(1)
    l3_moe(k, c)
    l3_wout_combine(k, c)
    if c.debug:
        dump_dbg(k, c, "dbg_s3", src="sres")
        dump_dbg(k, c, "dbg_x4", src="xres")

    k.wait("gpsimd", c.sVE, c.sVE.n)
    k.wait("gpsimd", c.sSC, c.sSC.n)
    k.wait("gpsimd", c.sMM, c.sMM.n)
    for dt in range(NDT):
        k.dma("gpsimd", c.out_fm[dt * 128:(dt + 1) * 128, :],
              c.xres[:, dt * T:(dt + 1) * T], c.sOUT)
    k.wait("gpsimd", c.sOUT, c.sOUT.n)


def dump_dbg(k, c, name, src):
    ten = getattr(c, src)
    k.wait("gpsimd", c.sVE, c.sVE.n)
    k.wait("gpsimd", c.sSC, c.sSC.n)
    k.wait("gpsimd", c.sMM, c.sMM.n)
    k.wait("gpsimd", c.sD3, c.sD3.n)
    k.wait("sync", c.sW, c.sW.n)
    k.wait("gpsimd", c.sW, c.sW.n)
    if ten.dtype != F32:
        for dt in range(NDT):
            k.drain("gpsimd")
            t = copy_op(k, "gpsimd", c.F32[:, 0:2048], ten[:, dt * T:(dt + 1) * T], sem=c.sGP)
            k.wait("gpsimd", c.sGP, t)
            td = k.dma("gpsimd", c.dbg[name][dt * 128:(dt + 1) * 128, :], c.F32[:, 0:2048], c.sD3)
            k.wait("gpsimd", c.sD3, td)
    else:
        for dt in range(NDT):
            k.dma("gpsimd", c.dbg[name][dt * 128:(dt + 1) * 128, :],
                  ten[:, dt * T:(dt + 1) * T], c.sD3)
        k.wait("gpsimd", c.sD3, c.sD3.n)
    tokd = k.inc("gpsimd", c.sGP)
    for eng in ("vector", "scalar", "tensor", "sync"):
        k.wait(eng, c.sGP, tokd)



def l0_weight_prep_gen(k, c):
    WG = lambda i: c.G32[:, i * 256:(i + 1) * 256]
    WS = lambda i: c.G32[:, i * 40:(i + 1) * 40]
    t = k.dma("gpsimd", c.G32[:, 1792:1824], c.L0b1, c.sD2)
    t = k.dma("gpsimd", c.G32[:, 1824:1832], c.L0b2, c.sD2)
    k.wait("gpsimd", c.sD2, t)
    q_chain(k, c, "gpsimd", c.cols[:, 0:32], c.G32[:, 1792:1824], 2, 14,
            lambda i: c.G32[:, i * 32:(i + 1) * 32], ties_even=False, sem=c.sGP)
    q_chain(k, c, "gpsimd", c.cols[:, 32:40], c.G32[:, 1824:1832], 2, 14,
            lambda i: c.G32[:, i * 8:(i + 1) * 8], ties_even=False, sem=c.sGP)
    c.tok_bq = c.sGP.n
    yield
    for wi, wsrc in enumerate((c.L0WqT, c.L0WkT, c.L0WvT, c.L0WoT)):
        td = k.dma("gpsimd", c.WbfA[wi], wsrc, c.sD2)
        yield
    k.wait("gpsimd", c.sD2, td)
    c.tok_wcast = c.sD2.n
    yield
    WGS = lambda i: c.G32[:, i * 256:(i + 1) * 256]
    for kt in range(8):
        for fc4 in range(4):
            for sub in range(4):
                fc = fc4 * 4 + sub
                t = k.dma("gpsimd", c.G32[:, 1792:2048],
                          c.L0W1T[kt * 128:(kt + 1) * 128, fc * 256:(fc + 1) * 256], c.sD2)
                k.wait("gpsimd", c.sD2, t)
                q_chain(k, c, "gpsimd", c.w8[0][:, 0:256], c.G32[:, 1792:2048], 2, 14,
                        WGS, ties_even=False, sem=c.sGP)
                k.dma("gpsimd", c.qW1s[2 * fc, :, kt * 128:(kt + 1) * 128],
                      c.w8[0][:, 0:128], c.sD2)
                td = k.dma("gpsimd", c.qW1s[2 * fc + 1, :, kt * 128:(kt + 1) * 128],
                           c.w8[0][:, 128:256], c.sD2)
                k.wait("gpsimd", c.sD2, td)
            yield
    for ft in range(32):
        for oc in range(4):
            t = k.dma("gpsimd", c.G32[:, 1792:2048],
                      c.L0W2T[ft * 128:(ft + 1) * 128, oc * 256:(oc + 1) * 256], c.sD2)
            k.wait("gpsimd", c.sD2, t)
            q_chain(k, c, "gpsimd", c.w8[0][:, 0:256], c.G32[:, 1792:2048], 2, 14,
                    WGS, ties_even=False, sem=c.sGP)
            k.dma("gpsimd", c.qW2s[2 * oc, :, ft * 128:(ft + 1) * 128],
                  c.w8[0][:, 0:128], c.sD2)
            td = k.dma("gpsimd", c.qW2s[2 * oc + 1, :, ft * 128:(ft + 1) * 128],
                       c.w8[0][:, 128:256], c.sD2)
            k.wait("gpsimd", c.sD2, td)
        yield
    c.tok_wprep = c.sD2.n
    while True:
        yield



def layernorm(k, c, sem, tX, make_qx=False):
    k.wait("vector", sem, tX)
    pump = getattr(c, "pump", lambda n=1: None)
    k.raw("vector", lambda e: e.memset(c.cols[:, 66:67], 1.0 / D))
    tones = k.inc("vector", c.sVE)
    k.wait("tensor", c.sVE, tones)
    k.wait("tensor", sem, tX)
    k.wait("scalar", sem, tX)
    W = lambda i: c.F32[:, i * 256:(i + 1) * 256]
    for tc in range(4):
        for dt in range(NDT):
            mm(k, c.ps[0][0:1, :], c.cols[:, 66:67],
               c.xres[:, dt * T + tc * 512: dt * T + (tc + 1) * 512],
               start=(dt == 0), stop=(dt == NDT - 1))
        k.inc("tensor", c.sMM)
        buf = c.F32[:, 1792:2304]
        for dt in range(NDT):
            tsq = act_op(k, buf, c.xres[:, dt * T + tc * 512: dt * T + (tc + 1) * 512],
                         ACT.Square, sem=c.sSC)
            k.wait("tensor", c.sSC, tsq)
            mm(k, c.ps[1][0:1, :], c.cols[:, 66:67], buf,
               start=(dt == 0), stop=(dt == NDT - 1))
            tm2 = k.inc("tensor", c.sMM)
            if dt < NDT - 1:
                k.wait("scalar", c.sMM, tm2)
        tmm2 = c.sMM.n
        k.wait("vector", c.sMM, tmm2)
        ts_op(k, "vector", c.rows[0:1, 0:512], c.ps[0][0:1, :], 1.0, ALU.mult)
        k.drain("vector")
        tt_op(k, "vector", c.rows[0:1, 512:1024], c.rows[0:1, 0:512],
              c.rows[0:1, 0:512], ALU.mult)
        k.drain("vector")
        stt_op(k, "vector", c.rows[0:1, 512:1024], c.ps[1][0:1, :], 1.0, ALU.mult,
               c.rows[0:1, 512:1024], ALU.subtract)
        k.drain("vector")
        tmu = ts_op(k, "vector", c.rows[0:1, 512:1024], c.rows[0:1, 512:1024],
                    1e-5, ALU.add, sem=c.sVE)
        k.wait("tensor", c.sVE, tmu)
        mm(k, c.ps[2][:, :], c.cONESROW[0:1, :], c.rows[0:1, 0:512], start=True, stop=True)
        tmr = k.inc("tensor", c.sMM)
        k.wait("vector", c.sMM, tmr)
        k.raw("vector", lambda e: e.reciprocal(c.rows[0:1, 0:512], c.rows[0:1, 512:1024]))
        trec = k.inc("vector", c.sVE)
        k.wait("scalar", c.sVE, trec)
        tsq2 = act_op(k, c.rows[0:1, 512:1024], c.rows[0:1, 0:512], ACT.Sqrt, sem=c.sSC)
        k.wait("tensor", c.sSC, tsq2)
        mm(k, c.ps[3][:, :], c.cONESROW[0:1, :], c.rows[0:1, 512:1024], start=True, stop=True)
        trep = k.inc("tensor", c.sMM)
        k.wait("vector", c.sMM, trep)
        for dt in range(NDT):
            sl = slice(dt * T + tc * 512, dt * T + (tc + 1) * 512)
            xn = c.F32[:, 1792:2304]
            tt_op(k, "vector", xn, c.xres[:, sl], c.ps[2][:, :], ALU.subtract)
            k.drain("vector")
            tv = tt_op(k, "vector", xn, xn, c.ps[3][:, :], ALU.mult, sem=c.sVE)
            k.wait("scalar", c.sVE, tv)
            tb = copy_op(k, "scalar", c.xnbf[:, sl], xn, sem=c.sSC)
            if not make_qx:
                k.drain("scalar")
            if make_qx:
                for half in range(2):
                    hs = slice(1792 + half * 256, 1792 + (half + 1) * 256)
                    tq = q_chain(k, c, "vector", c.qxb[:, half * 256:(half + 1) * 256],
                                 c.F32[:, hs], 2, 14, W, ties_even=False, sem=c.sVE)
                k.wait("sync", c.sVE, tq)
                k.wait("sync", c.sSC, tb)
                tqd = k.dma("sync", c.qxs[dt, :, tc * 512:(tc + 1) * 512],
                            c.qxb[:, 0:512], c.sW)
                k.wait("sync", c.sW, tqd)
                k.wait("vector", c.sW, tqd)
            k.wait("vector", c.sSC, tb)
        tdone = k.inc("vector", c.sVE)
        k.wait("tensor", c.sVE, tdone)
        pump(3)
    c.tok_ln = k.inc("vector", c.sVE)
    c.tok_lnsc = k.inc("scalar", c.sSC)



def l0_attn(k, c):
    k.wait("sync", c.sVE, c.tok_ln)
    for dt in range(NDT):
        k.dma("sync", c.F32[:, 1024 + dt * 32:1024 + dt * 32 + 16],
              c.L0WwT[dt * 128:(dt + 1) * 128, :], c.sW)
        k.dma("sync", c.F32[:, 1024 + dt * 32 + 16:1024 + dt * 32 + 32],
              c.L0WeT[dt * 128:(dt + 1) * 128, :], c.sW)
    k.dma("sync", c.cols[0:16, 64:65], c.L0bw, c.sW)
    tW = k.dma("sync", c.cols[0:16, 65:66], c.L0be, c.sW)
    k.wait("sync", c.sW, tW)
    k.wait("scalar", c.sW, tW)
    k.wait("scalar", c.sSC, c.tok_lnsc)
    tgw = copy_op(k, "scalar", c.bfS[:, 0:256], c.F32[:, 1024:1280], sem=c.sSC)
    k.wait("tensor", c.sSC, tgw)
    k.wait("tensor", c.sVE, c.tok_ln)
    for tc in range(4):
        for dt in range(NDT):
            sl = slice(dt * T + tc * 512, dt * T + (tc + 1) * 512)
            mm(k, c.ps[4][0:16, :], c.bfS[:, dt * 32:dt * 32 + 16], c.xnbf[:, sl],
               start=(dt == 0), stop=(dt == NDT - 1))
            mm(k, c.ps[5][0:16, :], c.bfS[:, dt * 32 + 16:dt * 32 + 32], c.xnbf[:, sl],
               start=(dt == 0), stop=(dt == NDT - 1))
        tg = k.inc("tensor", c.sMM)
        k.wait("scalar", c.sMM, tg)
        tsl = slice(tc * 512, (tc + 1) * 512)
        act_op(k, c.wgbf[:, tsl], c.ps[4][0:16, :], ACT.Sigmoid,
               bias=c.cols[0:16, 64:65])
        te = act_op(k, c.F32[0:16, 512:1024], c.ps[5][0:16, :], ACT.Sigmoid,
                    bias=c.cols[0:16, 65:66], sem=c.sSC)
        k.wait("vector", c.sSC, te)
        tv = ts_op(k, "vector", c.gates[0:16, tsl], c.F32[0:16, 512:1024],
                   -0.95, ALU.mult, 0.95, ALU.add, sem=c.sVE)
        k.wait("scalar", c.sVE, tv)
        tl = act_op(k, c.gates[0:16, tsl], c.gates[0:16, tsl], ACT.Ln, bias=1e-8,
                    sem=c.sSC)
        k.wait("tensor", c.sSC, tl)
        k.wait("vector", c.sSC, tl)
    k.raw("vector", lambda e: e.memset(c.F32[0:16, 0:512], 1.0))
    k.drain("vector")
    for tc in range(4):
        init = 0.0 if tc == 0 else c.gates[0:16, tc * 512 - 1:tc * 512]
        k.raw("vector", lambda e, tc=tc, init=init: e.tensor_tensor_scan(
            c.gates[0:16, tc * 512:(tc + 1) * 512], c.F32[0:16, 0:512],
            c.gates[0:16, tc * 512:(tc + 1) * 512], init, ALU.mult, ALU.add))
        k.drain("vector")
    k.raw("vector", lambda e: e.memset(c.Tst[:], 0.0))
    tzero = k.inc("vector", c.sVE)
    tlwc = tzero
    k.wait("sync", c.sVE, tlwc)
    k.wait("tensor", c.sVE, tzero)
    k.wait("tensor", c.sD2, c.tok_wcast)
    k.wait("sync", c.sD2, c.tok_wcast)
    c.crow_mm_tok = 0

    for tg in range(8):
        g0 = tg * 256
        for q4 in range(4):
            for kt in range(8):
                bsel = kt % 2
                k.dma("sync", c.wbf[bsel][:, 0:256],
                      c.WbfA[0, kt * 128:(kt + 1) * 128, q4 * 256:(q4 + 1) * 256], c.sW)
                tq2 = k.dma("sync", c.wbf[bsel][:, 256:512],
                            c.WbfA[1, kt * 128:(kt + 1) * 128, q4 * 256:(q4 + 1) * 256], c.sW)
                k.wait("sync", c.sW, tq2)
                k.wait("tensor", c.sW, tq2)
                for dq in range(2):
                    mm(k, c.ps[dq][:, 0:256], c.wbf[bsel][:, dq * 128:(dq + 1) * 128],
                       c.xnbf[:, kt * T + g0: kt * T + g0 + 256],
                       start=(kt == 0), stop=(kt == 7))
                    mm(k, c.ps[2 + dq][:, 0:256],
                       c.wbf[bsel][:, 256 + dq * 128:256 + (dq + 1) * 128],
                       c.xnbf[:, kt * T + g0: kt * T + g0 + 256],
                       start=(kt == 0), stop=(kt == 7))
                tmm = k.inc("tensor", c.sMM)
                k.wait("sync", c.sMM, tmm)
            tmm = c.sMM.n
            k.wait("vector", c.sMM, tmm)
            k.wait("scalar", c.sMM, tmm)
            for dq in range(2):
                dtq = q4 * 2 + dq
                for pbank, arena in ((dq, c.aQ), (2 + dq, c.aK)):
                    ps = c.ps[pbank]
                    ts_op(k, "vector", c.F32[:, 0:256], ps[:, 0:256], 0.0, ALU.min)
                    tmn = k.inc("vector", c.sVE)
                    k.wait("scalar", c.sVE, tmn)
                    texp = act_op(k, c.F32[:, 256:512], c.F32[:, 0:256], ACT.Exp, sem=c.sSC)
                    k.wait("vector", c.sSC, texp)
                    stt_op(k, "vector", arena[:, dtq * 256:(dtq + 1) * 256],
                           ps[:, 0:256], 0.0, ALU.max, c.F32[:, 256:512], ALU.add)
                    k.drain("vector")
            tphi = k.inc("vector", c.sVE)
            k.wait("tensor", c.sVE, tphi)
        for kt in range(8):
            bsel = kt % 2
            tvw = k.dma("sync", c.wbf[bsel][:, 0:1024],
                        c.WbfA[2, kt * 128:(kt + 1) * 128, :], c.sW)
            k.wait("sync", c.sW, tvw)
            k.wait("tensor", c.sW, tvw)
            for sub in range(2):
                xsl = c.xnbf[:, kt * T + g0 + sub * 128: kt * T + g0 + (sub + 1) * 128]
                mm(k, c.ps[sub * 2][:, :], xsl, c.wbf[bsel][:, 0:512],
                   start=(kt == 0), stop=(kt == 7))
                mm(k, c.ps[sub * 2 + 1][:, :], xsl, c.wbf[bsel][:, 512:1024],
                   start=(kt == 0), stop=(kt == 7))
            tmm = k.inc("tensor", c.sMM)
            k.wait("sync", c.sMM, tmm)
        tmm = c.sMM.n
        k.wait("scalar", c.sMM, tmm)
        for sub in range(2):
            for hb in range(2):
                vbase = sub * 1040 + hb * 8 * 65
                dst = c.aVtm[:, vbase:vbase + 520].rearrange(
                    "p (j v) -> p j v", v=65)[:, :, 0:64]
                copy_op(k, "scalar", dst, c.ps[sub * 2 + hb][:, :], sem=c.sSC)
        tvm = c.sSC.n
        k.wait("tensor", c.sSC, tvm)
        for ci in range(2):
            l0_attn_chunk(k, c, tg, ci)
            c.pump(1)
    if c.debug:
        dump_dbg(k, c, "dbg_spre", src="sres")
    for tc in range(4):
        for og in range(2):
            for dv in range(8):
                bsel = dv % 2
                tw = k.dma("sync", c.wbf[bsel][:, 0:512],
                           c.WbfA[3, dv * 128:(dv + 1) * 128,
                                  og * 512:(og + 1) * 512], c.sW)
                k.wait("sync", c.sW, tw)
                k.wait("tensor", c.sW, tw)
                for oo in range(4):
                    mm(k, c.ps[oo][:, :], c.wbf[bsel][:, oo * 128:(oo + 1) * 128],
                       c.sres[:, dv * T + tc * 512: dv * T + (tc + 1) * 512],
                       start=(dv == 0), stop=(dv == 7))
                tmm = k.inc("tensor", c.sMM)
                k.wait("sync", c.sMM, tmm)
            tmm = c.sMM.n
            if og == 0:
                k.wait("scalar", c.sMM, tmm)
                for oo in range(4):
                    copy_op(k, "scalar", c.F32[:, oo * 512:(oo + 1) * 512],
                            c.ps[oo][:, :], sem=c.sSC)
                tstg = c.sSC.n
                k.wait("tensor", c.sSC, tstg)
        k.wait("scalar", c.sMM, tmm)
        for oo in range(4):
            copy_op(k, "scalar",
                    c.sres[:, (4 + oo) * T + tc * 512: (4 + oo) * T + (tc + 1) * 512],
                    c.ps[oo][:, :], sem=c.sSC)
        tev2 = c.sSC.n
        k.wait("vector", c.sSC, tev2)
        for oo in range(4):
            tt_op(k, "vector",
                  c.sres[:, oo * T + tc * 512: oo * T + (tc + 1) * 512],
                  c.F32[:, oo * 512:(oo + 1) * 512],
                  c.F32[:, oo * 512:(oo + 1) * 512], ALU.min, sem=c.sVE)
        tev3 = c.sVE.n
        k.drain("vector")
        k.wait("tensor", c.sVE, tev3)
        k.wait("tensor", c.sSC, tev2)
    c.tok_attn = k.inc("vector", c.sVE)
    k.wait("scalar", c.sVE, c.tok_attn)
    c.tok_attn_sc = k.inc("scalar", c.sSC)


def l0_attn_chunk(k, c, tg, ci):
    ck = tg * 2 + ci
    t0 = ck * CK
    for dtq in range(8):
        mmt = tr_op(k, c.ps7b[:, (dtq % 4) * 128:(dtq % 4 + 1) * 128],
                    c.aK[:, dtq * 256 + ci * 128: dtq * 256 + (ci + 1) * 128],
                    c.cIDTB[:], sem=c.sMM)
        k.wait("scalar", c.sMM, mmt)
        tkc = copy_op(k, "scalar",
                      c.aKtm[:, dtq * 128:(dtq + 1) * 128],
                      c.ps7b[:, (dtq % 4) * 128:(dtq % 4 + 1) * 128], sem=c.sSC)
        k.wait("tensor", c.sSC, tkc)
    tlnw = act_op(k, c.sm16[:, 288:416], c.wgbf[0:16, t0:t0 + CK], ACT.Ln, sem=c.sSC)
    k.wait("vector", c.sSC, tlnw)
    tt_op(k, "vector", c.sm16[:, 288:416], c.sm16[:, 288:416],
          c.gates[0:16, t0:t0 + CK], ALU.subtract)
    k.drain("vector")
    cendcol = c.gates[0:16, t0 + CK - 1:t0 + CK]
    if ck > 0:
        cstcol = c.gates[0:16, t0 - 1:t0]
        ts_op(k, "vector", c.sm16[:, 0:128], c.sm16[:, 288:416], cendcol, ALU.add)
        k.drain("vector")
        ts_op(k, "vector", c.sm16[:, 128:256], c.gates[0:16, t0:t0 + CK],
              cstcol, ALU.subtract)
        k.drain("vector")
        ts_op(k, "vector", c.sm16[:, 256:257], cendcol, cstcol, ALU.subtract)
    else:
        ts_op(k, "vector", c.sm16[:, 0:128], c.sm16[:, 288:416], cendcol, ALU.add)
        k.drain("vector")
        copy_op(k, "vector", c.sm16[:, 128:256], c.gates[0:16, t0:t0 + CK])
        k.drain("vector")
        copy_op(k, "vector", c.sm16[:, 256:257], cendcol)
    tprep = k.inc("vector", c.sVE)
    k.wait("scalar", c.sVE, tprep)
    taexp = act_op(k, c.sm16[:, 128:256], c.sm16[:, 128:256], ACT.Exp, sem=c.sSC)
    k.wait("tensor", c.sVE, tprep)
    k.wait("tensor", c.sVE, c.bank_free.get("p6v", 0))
    k.wait("tensor", c.sSC, c.bank_free.get("p6s", 0))
    tr_op(k, c.ps[6][:, 0:16], c.sm16[0:16, 288:416], c.cIDT[:])
    tr_op(k, c.ps[6][:, 16:32], c.sm16[0:16, 0:128], c.cIDT[:])
    ttr3 = tr_op(k, c.ps[6][0:1, 32:48], c.sm16[0:16, 256:257], c.cIDT[:],
                 sem=c.sMM)
    k.wait("vector", c.sMM, ttr3)
    tqT = copy_op(k, "vector", c.kdT[:, 16:32], c.ps[6][:, 0:16], sem=c.sVE)
    k.drain("vector")
    k.wait("scalar", c.sVE, tqT)
    tkdT = act_op(k, c.kdT[:, 0:16], c.ps[6][:, 16:32], ACT.Exp, sem=c.sSC)
    tef0 = copy_op(k, "scalar", c.F32[0:1, 1345:1361], c.ps[6][0:1, 32:48], sem=c.sSC)
    c.bank_free["p6v"] = tqT
    c.bank_free["p6s"] = tef0
    k.wait("tensor", c.sSC, tef0)
    mm(k, c.ps[6][0:128, 48:64], c.cONESROW[0:1, 0:128], c.F32[0:1, 1345:1361],
       start=True, stop=True)
    tefm = k.inc("tensor", c.sMM)
    k.wait("scalar", c.sMM, tefm)
    tefr = act_op(k, c.efr[:, :], c.ps[6][0:128, 48:64], ACT.Exp, sem=c.sSC)
    k.wait("tensor", c.sSC, taexp)
    for dt in range(NDT):
        mm(k, c.ps[6][:, 128:256], c.cREPL2[:, dt * 128:(dt + 1) * 128],
           c.sm16[0:16, 128:256], start=True, stop=True)
        tar = k.inc("tensor", c.sMM)
        k.wait("vector", c.sMM, tar)
        tqa = tt_op(k, "vector", c.F32[:, dt * 128:(dt + 1) * 128],
                    c.aQ[:, dt * 256 + ci * 128: dt * 256 + (ci + 1) * 128],
                    c.ps[6][:, 128:256], ALU.mult, sem=c.sVE)
        k.drain("vector")
        k.wait("tensor", c.sVE, tqa)
    c.bank_free["p6s"] = tefr
    c.bank_free["p6v"] = c.sVE.n
    k.wait("vector", c.sSC, tefr)
    k.wait("vector", c.sSC, tkdT)

    for h in range(16):
        if h % 4 == 0:
            k.wait("sync", c.sMM, c.crow_mm_tok)
            for hh in range(h, h + 4):
                tcrow = k.dma("sync", c.crows[0:1, (hh % 4) * 128:(hh % 4 + 1) * 128],
                              c.gates[hh:hh + 1, t0:t0 + CK], c.sW)
            k.wait("sync", c.sW, tcrow)
            k.wait("tensor", c.sW, tcrow)
        dth = h // 2
        base = 64 * (h % 2)
        vsl = c.aVtm[:, ci * 1040 + h * 65: ci * 1040 + h * 65 + 65]
        ksl = c.aKtm[:, h * 64:(h + 1) * 64]
        k.wait("tensor", c.sVE, c.bank_free.get("p0", 0))
        k.wait("tensor", c.sVE, c.bank_free.get("p1", 0))
        mm(k, c.ps[0][:, 0:128],
           c.aK[base:base + 64, dth * 256 + ci * 128: dth * 256 + (ci + 1) * 128],
           c.aQ[base:base + 64, dth * 256 + ci * 128: dth * 256 + (ci + 1) * 128],
           start=True, stop=True)
        mm(k, c.ps[1][:, 0:128], c.cONESROW[0:1, :],
           c.crows[0:1, (h % 4) * 128:(h % 4 + 1) * 128], start=True, stop=True)
        tsm = k.inc("tensor", c.sMM)
        c.crow_mm_tok = tsm
        tkd = ts_op(k, "vector", c.bfS[:, 128:192], ksl, c.kdT[:, h:h + 1], ALU.mult,
                    sem=c.sVE)
        k.wait("vector", c.sMM, tsm)
        stt_op(k, "vector", c.F32[:, 1024:1152], c.ps[1][:, 0:128],
               c.kdT[:, 16 + h:17 + h], ALU.add, c.cMBIG[:], ALU.add)
        tmc = k.inc("vector", c.sVE)
        c.bank_free["p1"] = tmc
        k.wait("scalar", c.sVE, tmc)
        tE = act_op(k, c.F32[:, 1152:1280], c.F32[:, 1024:1152], ACT.Exp, sem=c.sSC)
        k.wait("vector", c.sSC, tE)
        teff = tt_op(k, "vector", c.bfS[:, 0:128], c.ps[0][:, 0:128],
                     c.F32[:, 1152:1280], ALU.mult, sem=c.sVE)
        k.drain("vector")
        c.bank_free["p0"] = teff
        k.wait("tensor", c.sVE, teff)
        k.wait("tensor", c.sVE, c.bank_free.get("p2", 0))
        mm(k, c.ps[2][0:65, 0:128], vsl, c.bfS[:, 0:128], start=True, stop=False)
        mm(k, c.ps[2][0:65, 0:128], c.Tst[base:base + 64, dth * 65:(dth + 1) * 65],
           c.F32[base:base + 64, dth * 128:(dth + 1) * 128],
           start=False, stop=True)
        tO = k.inc("tensor", c.sMM)
        k.wait("tensor", c.sVE, tkd)
        k.wait("tensor", c.sVE, c.bank_free.get("p4", 0))
        mm(k, c.ps[4][base:base + 64, 0:65], c.bfS[:, 128:192], vsl,
           start=True, stop=True)
        tWs = k.inc("tensor", c.sMM)
        k.wait("vector", c.sMM, tO)
        ts_op(k, "vector", c.F32[64:65, 1024:1152], c.ps[2][64:65, 0:128], 1e-6, ALU.add)
        k.drain("vector")
        k.raw("vector", lambda e: e.reciprocal(c.F32[64:65, 1152:1280], c.F32[64:65, 1024:1152]))
        trc = k.inc("vector", c.sVE)
        k.wait("tensor", c.sVE, trc)
        k.wait("tensor", c.sVE, c.bank_free.get("p3", 0))
        mm(k, c.ps[3][0:64, 0:128], c.cONESROW[64:65, 0:64], c.F32[64:65, 1152:1280],
           start=True, stop=True)
        trp = k.inc("tensor", c.sMM)
        k.wait("scalar", c.sMM, trp)
        trcp = copy_op(k, "scalar", c.F32[0:64, 1024:1152], c.ps[3][0:64, 0:128],
                       sem=c.sSC)
        k.wait("vector", c.sSC, trcp)
        slot = (h % 2) * 128
        k.wait("vector", c.sW, c.bank_free.get(f"soutdma{h % 2}", 0))
        tso = tt_op(k, "vector", c.bfT[0:64, slot:slot + 128],
                    c.ps[2][0:64, 0:128], c.F32[0:64, 1024:1152], ALU.mult, sem=c.sVE)
        k.drain("vector")
        c.bank_free["p2"] = tso
        c.bank_free["p3"] = tso
        k.wait("sync", c.sVE, tso)
        tsd = k.dma("sync", c.sres[base:base + 64, dth * T + t0: dth * T + t0 + CK],
                    c.bfT[0:64, slot:slot + 128], c.sW)
        k.wait("sync", c.sW, tsd)
        c.bank_free[f"soutdma{h % 2}"] = tsd
        k.wait("vector", c.sMM, tWs)
        stt_op(k, "vector", c.F32[base:base + 64, 1280:1345],
               c.Tst[base:base + 64, dth * 65:(dth + 1) * 65],
               c.efr[base:base + 64, h:h + 1], ALU.mult,
               c.ps[4][base:base + 64, 0:65], ALU.add)
        k.drain("vector")
        tTu = copy_op(k, "vector", c.Tst[base:base + 64, dth * 65:(dth + 1) * 65],
                      c.F32[base:base + 64, 1280:1345], sem=c.sVE)
        k.drain("vector")
        c.bank_free["p4"] = tTu
        k.wait("tensor", c.sVE, tTu)
    k.wait("sync", c.sW, c.sW.n)


def l0_ffn_combine(k, c):
    k.wait("sync", c.sD2, c.tok_wprep)
    WV = lambda i: c.F32[:, i * 256:(i + 1) * 256]
    WG = lambda i: c.G32[:, i * 256:(i + 1) * 256]
    hfree = {}
    yfree = {}
    qx_mm_tok = 0
    for tf in range(8):
        k.wait("sync", c.sMM, qx_mm_tok)
        for kt in range(8):
            tqx = k.dma("sync", c.qxb[:, kt * 256:(kt + 1) * 256],
                        c.qxs[kt, :, tf * 256:(tf + 1) * 256], c.sW)
        k.wait("sync", c.sW, tqx)
        k.wait("tensor", c.sW, tqx)
        for ft in range(32):
            bsel = ft % 2
            k.wait("sync", c.sMM, c.bank_free.get(f"w8{bsel}", 0))
            tw = k.dma("sync", c.w8[bsel][:, 0:1024], c.qW1s[ft][:, 0:1024], c.sW)
            k.wait("sync", c.sW, tw)
            k.wait("tensor", c.sW, tw)
            ent = hfree.get(ft % 2)
            if ent:
                k.wait("tensor", getattr(c, ent[0]), ent[1])
            pb = c.ps[ft % 2]
            for kt in range(8):
                mm(k, pb[:, 0:256], c.w8[bsel][:, kt * 128:(kt + 1) * 128],
                   c.qxb[:, kt * 256:(kt + 1) * 256],
                   start=(kt == 0), stop=(kt == 7))
            th = k.inc("tensor", c.sMM)
            c.bank_free[f"w8{bsel}"] = th
            eng = "vector" if ft % 3 != 2 else "gpsimd"
            semE = "sVE" if eng == "vector" else "sGP"
            Wc = WV if eng == "vector" else WG
            hrm = (c.G32[:, 1792:2048] if eng == "vector" else c.F32[:, 2048:2304])
            k.wait(eng, c.sMM, th)
            if eng == "gpsimd":
                k.wait("scalar", c.sMM, th)
            q_chain(k, c, eng, hrm, pb[:, 0:256], 7, 126, Wc, ties_even=False,
                    relu_bias=c.cols[:, ft:ft + 1],
                    first_sem=c.sVE if eng == "vector" else c.sSC,
                    first_on_scalar=(eng == "gpsimd"))
            hfree[ft % 2] = (("sVE" if eng == "vector" else "sSC"), c.first_tok)
            q_chain(k, c, eng, c.hq[:, ft * 256:(ft + 1) * 256], hrm, 2, 14, Wc,
                    ties_even=True, nonneg=True,
                    sem=c.sVE if eng == "vector" else c.sGP)
        qx_mm_tok = c.sMM.n
        c.bank_free["yw80"] = c.bank_free.get("w80", 0)
        c.bank_free["yw81"] = c.bank_free.get("w81", 0)
        tvq = k.inc("vector", c.sVE)
        tgq = k.inc("gpsimd", c.sGP)
        k.wait("tensor", c.sVE, tvq)
        k.wait("tensor", c.sGP, tgq)
        for ot in range(8):
            k.wait("sync", c.sVE, c.bank_free.get("drop", 0))
            td = k.dma("sync", c.bfS[:, 128:384],
                       c.DROP[ot * 128:(ot + 1) * 128, tf * 256:(tf + 1) * 256], c.sW)
            k.wait("sync", c.sW, td)
            ent = yfree.get(ot % 2)
            if ent:
                k.wait("tensor", getattr(c, ent[0]), ent[1])
            pb = c.ps[2 + ot % 2]
            for qtr in range(4):
                qb = qtr % 2
                k.wait("sync", c.sMM, c.bank_free.get(f"yw8{qb}", 0))
                tw = k.dma("sync", c.w8[qb][:, 0:1024],
                           c.qW2s[ot][:, qtr * 1024:(qtr + 1) * 1024], c.sW)
                k.wait("sync", c.sW, tw)
                k.wait("tensor", c.sW, tw)
                if qtr == 0:
                    k.wait("tensor", c.sW, td)
                for fi in range(8):
                    ft = qtr * 8 + fi
                    mm(k, pb[:, 0:256], c.w8[qb][:, fi * 128:(fi + 1) * 128],
                       c.hq[:, ft * 256:(ft + 1) * 256],
                       start=(ft == 0), stop=(ft == 31))
                tqm = k.inc("tensor", c.sMM)
                c.bank_free[f"yw8{qb}"] = tqm
            ty = c.sMM.n
            k.wait("vector", c.sMM, ty)
            k.wait("vector", c.sW, td)
            q_chain(k, c, "vector", c.F32[:, 1792:2048], pb[:, 0:256], 7, 126, WV,
                    ties_even=False, bias_col=c.cols[:, 32 + ot:33 + ot],
                    first_sem=c.sVE)
            yfree[ot % 2] = ("sVE", c.first_tok)
            tt_op(k, "vector", c.F32[:, 1792:2048], c.F32[:, 1792:2048],
                  c.bfS[:, 128:384], ALU.mult)
            tdm = k.inc("vector", c.sVE)
            c.bank_free["drop"] = tdm
            k.wait("scalar", c.sVE, tdm)
            tsi = act_op(k, c.bfT[:, 0:256], c.F32[:, 1792:2048], ACT.Silu, sem=c.sSC)
            k.wait("vector", c.sSC, tsi)
            sl = slice(ot * T + tf * 256, ot * T + (tf + 1) * 256)
            tt_op(k, "vector", c.F32[:, 1792:2048], c.sres[:, sl], c.bfT[:, 0:256],
                  ALU.mult)
            k.drain("vector")
            tt_op(k, "vector", c.xres[:, sl], c.xres[:, sl], c.F32[:, 1792:2048],
                  ALU.add, sem=c.sVE)
            k.drain("vector")
        c.bank_free["w80"] = c.sMM.n
        c.bank_free["w81"] = c.sMM.n
    c.tok_l0 = k.inc("vector", c.sVE)





def pump_until(c, attr):
    while not hasattr(c, attr):
        c.pump(1)


def l1_cast_gen(k, c):
    for wi, wsrc in enumerate((c.L1Wq, c.L1Wk, c.L1Wv)):
        t = k.dma("gpsimd", c.WbfA[wi], wsrc, c.sD2)
        yield
    k.wait("gpsimd", c.sD2, t)
    c.tok_l1qkv = c.sD2.n
    yield
    for e in range(8):
        for ph, wsrc in ((0, c.L1w1), (1, c.L1v1)):
            k.dma("gpsimd", c.L1wb[e, ph], wsrc[e], c.sD2)
            yield
        t2 = k.dma("gpsimd", c.L1w2b[e], c.L1w2[e], c.sD2)
        k.wait("gpsimd", c.sD2, t2)
        setattr(c, f"tok_l1e{e}", c.sD2.n)
        yield
    while True:
        yield


def proj_qkv(k, c, tok_attr, l1_consts=False):
    pump_until(c, tok_attr)
    k.wait("sync", c.sD2, getattr(c, tok_attr))
    k.wait("sync", c.sVE, c.tok_ln)
    k.wait("tensor", c.sVE, c.tok_ln)
    if l1_consts:
        k.dma("sync", c.cOMGT2[:], c.OMGT2, c.sW)
        tcn = k.dma("sync", c.cNEGH[:], c.NEGH, c.sW)
        k.wait("sync", c.sW, tcn)
        k.wait("tensor", c.sW, tcn)
        k.wait("scalar", c.sW, tcn)
        k.wait("vector", c.sW, tcn)
    evac = lambda i: (c.aQ[:, i * 512:(i + 1) * 512] if i < 4
                      else c.aK[:, (i - 4) * 512:(i - 3) * 512])
    wtok = {}
    tsc_prev = 0
    twq_prev = 0
    for pi, dstm in ((0, c.qfms), (1, c.kfms)):
        for tc in range(4):
            for og in range(2):
                for dt in range(8):
                    bsel = dt % 2
                    k.wait("sync", c.sMM, wtok.get(bsel, 0))
                    tw = k.dma("sync", c.wbf[bsel][:, 0:1024],
                               c.WbfA[pi, dt * 128:(dt + 1) * 128, :], c.sW)
                    k.wait("sync", c.sW, tw)
                    k.wait("tensor", c.sW, tw)
                    if dt == 0:
                        k.wait("tensor", c.sSC, tsc_prev)
                    for oi in range(4):
                        ot = og * 4 + oi
                        mm(k, c.ps[oi][:, :],
                           c.wbf[bsel][:, ot * 128:(ot + 1) * 128],
                           c.xnbf[:, dt * T + tc * 512: dt * T + (tc + 1) * 512],
                           start=(dt == 0), stop=(dt == 7))
                    wtok[bsel] = k.inc("tensor", c.sMM)
                tmm = c.sMM.n
                k.wait("scalar", c.sMM, tmm)
                k.wait("scalar", c.sW, twq_prev)
                for oi in range(4):
                    copy_op(k, "scalar", evac(og * 4 + oi), c.ps[oi][:, :], sem=c.sSC)
                tsc_prev = c.sSC.n
                k.wait("sync", c.sSC, tsc_prev)
                for oi in range(4):
                    ot = og * 4 + oi
                    k.dma("sync",
                          dstm[ot * 128:(ot + 1) * 128, tc * 512:(tc + 1) * 512],
                          evac(ot), c.sW)
                twq_prev = c.sW.n
                k.wait("sync", c.sW, twq_prev)
                c.pump(1)
    for tch in range(4):
        for oh in range(2):
            for dt in range(8):
                bsel = dt % 2
                k.wait("sync", c.sMM, wtok.get(bsel, 0))
                tw = k.dma("sync", c.wbf[bsel][:, 0:1024],
                           c.WbfA[2, dt * 128:(dt + 1) * 128, :], c.sW)
                k.wait("sync", c.sW, tw)
                k.wait("tensor", c.sW, tw)
                if dt == 0:
                    k.wait("tensor", c.sSC, tsc_prev)
                for tt2 in range(4):
                    mm(k, c.ps[tt2][:, :],
                       c.xnbf[:, dt * T + tch * 512 + tt2 * 128:
                              dt * T + tch * 512 + (tt2 + 1) * 128],
                       c.wbf[bsel][:, oh * 512:(oh + 1) * 512],
                       start=(dt == 0), stop=(dt == 7))
                wtok[bsel] = k.inc("tensor", c.sMM)
            tmm = c.sMM.n
            k.wait("scalar", c.sMM, tmm)
            k.wait("scalar", c.sW, twq_prev)
            for tt2 in range(4):
                copy_op(k, "scalar", evac(tt2), c.ps[tt2][:, :], sem=c.sSC)
            tsc_prev = c.sSC.n
            k.wait("sync", c.sSC, tsc_prev)
            for tt2 in range(4):
                k.dma("sync", c.vtms[tch * 512 + tt2 * 128: tch * 512 + (tt2 + 1) * 128,
                                     oh * 512:(oh + 1) * 512],
                      evac(tt2), c.sW)
            twq_prev = c.sW.n
            k.wait("sync", c.sW, twq_prev)
            c.pump(1)
    c.tok_l1proj = twq_prev
    k.wait("tensor", c.sSC, tsc_prev)


def l1_attn_heads(k, c):
    NL16 = -math.log(16.0)
    tok_mm_prev = c.sMM.n
    tok_sc_prev = c.sSC.n
    tok_ve_prev = c.sVE.n
    tok_s_prev = 0
    for h in range(16):
        base = 64 * (h % 2)
        bsl = slice(base, base + 64)
        k.wait("sync", c.sMM, tok_mm_prev)
        k.wait("sync", c.sSC, tok_sc_prev)
        k.wait("sync", c.sVE, tok_ve_prev)
        k.wait("sync", c.sW, tok_s_prev)
        k.wait("vector", c.sMM, tok_mm_prev)
        k.wait("vector", c.sSC, tok_sc_prev)
        k.dma("sync", c.aK[bsl, 0:2048], c.kfms[64 * h:64 * h + 64, :], c.sW)
        k.dma("sync", c.aQ[bsl, 0:2048], c.qfms[64 * h:64 * h + 64, :], c.sW)
        vdst = c.aVtm[:, 0:1040].rearrange("p (j v) -> p j v", v=65)[:, :, 0:64]
        vsrc = c.vtms.rearrange("(n p) o -> p n o", p=128)[:, :, 64 * h:64 * h + 64]
        tv = k.dma("sync", vdst, vsrc, c.sW)
        k.wait("sync", c.sW, tv)
        for eng in ("tensor", "vector", "scalar"):
            k.wait(eng, c.sW, tv)
        pkslot = lambda nt: c.hA[:, (nt % 2) * 256:(nt % 2) * 256 + 256]
        pktok = {}
        tpk_prev = 0
        for nt in range(16):
            nsl = slice(nt * 128, (nt + 1) * 128)
            tq2 = tt_op(k, "vector", c.F32[bsl, 0:128], c.aK[bsl, nsl],
                        c.aK[bsl, nsl], ALU.mult, sem=c.sVE)
            k.wait("tensor", c.sVE, tq2)
            mm(k, c.ps[0][:, 0:256], c.aK[bsl, nsl], c.cOMGT2[bsl, :],
               start=True, stop=True)
            mm(k, c.ps[1][:, 0:1], c.F32[bsl, 0:128], c.cNEGH[bsl, :],
               start=True, stop=True)
            tpm = k.inc("tensor", c.sMM)
            k.wait("vector", c.sMM, tpm)
            k.wait("vector", c.sSC, tpk_prev)
            tnk = ts_op(k, "vector", c.F32[:, 2048:2049], c.ps[1][:, 0:1],
                        NL16, ALU.add, sem=c.sVE)
            k.wait("scalar", c.sVE, tnk)
            k.wait("scalar", c.sMM, tpm)
            ent = pktok.get(nt % 2)
            if ent:
                k.wait("scalar", c.sMM, ent)
            tpk = act_op(k, pkslot(nt), c.ps[0][:, 0:256], ACT.Exp,
                         bias=c.F32[:, 2048:2049], sem=c.sSC)
            tpk_prev = tpk
            k.wait("tensor", c.sSC, tpk)
            for mt in range(2):
                mm(k, c.ps[2 + mt][:, 0:65],
                   pkslot(nt)[:, mt * 128:(mt + 1) * 128],
                   c.aVtm[:, nt * 65:nt * 65 + 65],
                   start=(nt == 0), stop=(nt == 15))
            pktok[nt % 2] = k.inc("tensor", c.sMM)
        tkv = c.sMM.n
        k.wait("scalar", c.sMM, tkv)
        copy_op(k, "scalar", c.Tst[:, 0:65], c.ps[2][:, 0:65])
        tkve = copy_op(k, "scalar", c.Tst[:, 65:130], c.ps[3][:, 0:65], sem=c.sSC)
        k.wait("tensor", c.sSC, tkve)
        k.wait("vector", c.sMM, tkv)
        tep_prev = 0
        for tch in range(4):
            tsl = slice(tch * 512, (tch + 1) * 512)
            tq2 = tt_op(k, "vector", c.F32[bsl, 0:512], c.aQ[bsl, tsl],
                        c.aQ[bsl, tsl], ALU.mult, sem=c.sVE)
            k.wait("tensor", c.sVE, tq2)
            k.wait("tensor", c.sSC, tep_prev)
            for mt in range(2):
                mm(k, c.ps[mt][:, :], c.cOMGT2[bsl, mt * 128:(mt + 1) * 128],
                   c.aQ[bsl, tsl], start=True, stop=True)
            mm(k, c.ps[4][0:1, :], c.cNEGH[bsl, :], c.F32[bsl, 0:512],
               start=True, stop=True)
            tpm = k.inc("tensor", c.sMM)
            k.wait("scalar", c.sMM, tpm)
            ten = act_op(k, c.F32[0:1, 1536:2048], c.ps[4][0:1, :], ACT.Exp,
                         bias=c.cACTC[0:1, 7:8], sem=c.sSC)
            act_op(k, c.F32[:, 512:1024], c.ps[0][:, :], ACT.Exp)
            tep = act_op(k, c.F32[:, 1024:1536], c.ps[1][:, :], ACT.Exp, sem=c.sSC)
            tep_prev = tep
            k.wait("tensor", c.sSC, ten)
            mm(k, c.ps[5][:, :], c.cONESROW[0:1, 0:128], c.F32[0:1, 1536:2048],
               start=True, stop=True)
            trep = k.inc("tensor", c.sMM)
            k.wait("vector", c.sMM, trep)
            k.wait("vector", c.sSC, tep)
            tt_op(k, "vector", c.F32[:, 512:1024], c.F32[:, 512:1024],
                  c.ps[5][:, :], ALU.mult)
            k.drain("vector")
            tpq = tt_op(k, "vector", c.F32[:, 1024:1536], c.F32[:, 1024:1536],
                        c.ps[5][:, :], ALU.mult, sem=c.sVE)
            k.wait("tensor", c.sVE, tpq)
            mm(k, c.ps[6][0:65, :], c.Tst[:, 0:65], c.F32[:, 512:1024],
               start=True, stop=False)
            mm(k, c.ps[6][0:65, :], c.Tst[:, 65:130], c.F32[:, 1024:1536],
               start=False, stop=True)
            tout = k.inc("tensor", c.sMM)
            k.wait("scalar", c.sMM, tout)
            tden = act_op(k, c.F32[64:65, 1536:2048], c.ps[6][64:65, :], ACT.Copy,
                          bias=1e-6, sem=c.sSC)
            k.wait("vector", c.sSC, tden)
            k.raw("vector", lambda e, o=c.F32[64:65, 1024:1536],
                  i=c.F32[64:65, 1536:2048]: e.reciprocal(o, i))
            trc = k.inc("vector", c.sVE)
            k.wait("tensor", c.sVE, trc)
            mm(k, c.ps[5][:, :], c.cONESROW[64:65, 0:128], c.F32[64:65, 1024:1536],
               start=True, stop=True)
            trp = k.inc("tensor", c.sMM)
            k.wait("scalar", c.sMM, trp)
            trv = copy_op(k, "scalar", c.F32[0:64, 1536:2048], c.ps[5][0:64, :],
                          sem=c.sSC)
            k.wait("vector", c.sSC, trv)
            tso = tt_op(k, "vector", c.aK[0:64, tsl], c.ps[6][0:64, :],
                        c.F32[0:64, 1536:2048], ALU.mult, sem=c.sVE)
            k.wait("sync", c.sVE, tso)
            tsd = k.dma("sync", c.sres[bsl, (h // 2) * T + tch * 512:
                                       (h // 2) * T + (tch + 1) * 512],
                        c.aK[0:64, tsl], c.sW)
            k.wait("sync", c.sW, tsd)
            c.pump(2)
        tok_mm_prev = c.sMM.n
        tok_sc_prev = c.sSC.n
        tok_ve_prev = c.sVE.n
        tok_s_prev = c.sW.n
    c.tok_l1s = tok_s_prev
    c.tok_l1attn = (tok_mm_prev, tok_sc_prev, tok_ve_prev)


def l1_router(k, c):
    tm0, ts0, tv0 = c.tok_l1attn
    k.wait("sync", c.sMM, tm0)
    t1 = k.dma("sync", c.F32[:, 0:64].rearrange("p (a b) -> p a b", b=8),
               c.L1rw.rearrange("(a p) b -> p a b", p=128), c.sW)
    tb = k.dma("sync", c.colsB[:, 0:8], c.L1bias, c.sW)
    k.wait("sync", c.sW, tb)
    k.wait("vector", c.sW, tb)
    k.wait("vector", c.sMM, tm0)
    k.wait("vector", c.sSC, ts0)
    tr = copy_op(k, "vector", c.rwb[:, 0:64], c.F32[:, 0:64], sem=c.sVE)
    k.wait("tensor", c.sVE, tr)
    k.wait("tensor", c.sSC, ts0)
    k.wait("scalar", c.sMM, tm0)
    k.wait("scalar", c.sVE, tv0)
    tdma_prev = 0
    for tt in range(16):
        for dt in range(8):
            mm(k, c.ps[0][:, 0:8], c.xnbf[:, dt * T + tt * 128:dt * T + (tt + 1) * 128],
               c.rwb[:, dt * 8:(dt + 1) * 8], start=(dt == 0), stop=(dt == 7))
        tlg = k.inc("tensor", c.sMM)
        k.wait("vector", c.sMM, tlg)
        k.raw("vector", lambda e, o=c.F32[:, 64:65], i=c.ps[0][:, 0:8]:
              e.reduce_max(o, i, axis=AX.X))
        k.drain("vector")
        tneg = ts_op(k, "vector", c.F32[:, 64:65], c.F32[:, 64:65], -1.0,
                     ALU.mult, sem=c.sVE)
        k.wait("scalar", c.sVE, tneg)
        texp = act_op(k, c.F32[:, 72:80], c.ps[0][:, 0:8], ACT.Exp,
                      bias=c.F32[:, 64:65], accum=c.F32[:, 65:66], sem=c.sSC)
        k.wait("vector", c.sSC, texp)
        k.raw("vector", lambda e, o=c.F32[:, 66:67], i=c.F32[:, 72:80]:
              e.reduce_max(o, i, axis=AX.X))
        k.drain("vector")
        stt_op(k, "vector", c.F32[:, 80:88], c.F32[:, 72:80], c.F32[:, 66:67],
               ALU.is_lt, c.F32[:, 72:80], ALU.mult)
        k.drain("vector")
        k.raw("vector", lambda e, o=c.F32[:, 67:68], i=c.F32[:, 80:88]:
              e.reduce_max(o, i, axis=AX.X))
        k.drain("vector")
        ts_op(k, "vector", c.F32[:, 80:88], c.F32[:, 72:80], c.F32[:, 67:68],
              ALU.is_ge)
        k.drain("vector")
        tt_op(k, "vector", c.F32[:, 80:88], c.F32[:, 80:88], c.F32[:, 72:80],
              ALU.mult)
        k.drain("vector")
        k.raw("vector", lambda e, o=c.F32[:, 68:69], i=c.F32[:, 65:66]:
              e.reciprocal(o, i))
        k.drain("vector")
        tcg = ts_op(k, "vector", c.bfS[:, 0:8], c.F32[:, 80:88], c.F32[:, 68:69],
                    ALU.mult, sem=c.sVE)
        k.wait("tensor", c.sVE, tcg)
        tr2 = tr_op(k, c.ps7b[0:8, 0:128], c.bfS[:, 0:8], c.cIDTB[:], sem=c.sMM)
        k.wait("scalar", c.sMM, tr2)
        k.wait("scalar", c.sW, tdma_prev)
        tev = copy_op(k, "scalar", c.F32[0:8, 96:224], c.ps7b[0:8, 0:128],
                      sem=c.sSC)
        k.wait("sync", c.sSC, tev)
        tdma_prev = k.dma("sync", c.ctrd[tt], c.F32[0:8, 96:224], c.sW)
        k.wait("sync", c.sW, tdma_prev)
    c.tok_router = tdma_prev


def hslot(c, ft):
    if ft < 12:
        return c.hA[:, ft * 512:(ft + 1) * 512]
    if ft < 16:
        return c.aQ[:, (ft - 12) * 512:(ft - 11) * 512]
    if ft < 20:
        return c.aK[:, (ft - 16) * 512:(ft - 15) * 512]
    return c.aKtm[:, (ft - 20) * 512:(ft - 19) * 512]


def l1_ffn_combine(k, c):
    ytok = {}
    wtok = {}
    gtok = {}
    stok = 0
    for e in range(8):
        pump_until(c, f"tok_l1e{e}")
        k.wait("sync", c.sD2, getattr(c, f"tok_l1e{e}"))
        for tf in range(4):
            tfs = slice(tf * 512, (tf + 1) * 512)
            k.wait("sync", c.sMM, gtok.get("crow", 0))
            tcr = k.dma("sync", c.crow5[0:1, 0:512].rearrange("p (a b) -> p a b", b=128),
                        c.ctrd[tf * 4:(tf + 1) * 4, e:e + 1, :], c.sW)
            k.wait("sync", c.sW, tcr)
            k.wait("tensor", c.sW, tcr)
            k.wait("tensor", c.sSC, stok)
            mm(k, c.ps[6][:, :], c.cONESROW[0:1, 0:128], c.crow5[0:1, 0:512],
               start=True, stop=True)
            tcm = k.inc("tensor", c.sMM)
            gtok["crow"] = tcm
            k.wait("scalar", c.sMM, tcm)
            k.wait("scalar", c.sVE, gtok.get("thh_last", 0))
            stok = copy_op(k, "scalar", c.F32[:, 1024:1536], c.ps[6][:, :], sem=c.sSC)
            k.wait("vector", c.sSC, stok)
            for ft in range(22):
                bsel = ft % 2
                k.wait("sync", c.sMM, wtok.get(f"w{bsel}", 0))
                tw1 = k.dma("sync", c.wbf[bsel][:, 0:1024].rearrange(
                    "p (a b) -> p a b", b=128),
                    c.L1wb[e, 0].rearrange("(a p) f -> p a f", p=128)[:, :, ft * 128:(ft + 1) * 128],
                    c.sW)
                tw2 = k.dma("sync", c.wbg[bsel][:, 0:1024].rearrange(
                    "p (a b) -> p a b", b=128),
                    c.L1wb[e, 1].rearrange("(a p) f -> p a f", p=128)[:, :, ft * 128:(ft + 1) * 128],
                    c.sW)
                k.wait("sync", c.sW, tw2)
                k.wait("tensor", c.sW, tw2)
                pb1 = c.ps[(ft % 2) * 2]
                pb2 = c.ps[(ft % 2) * 2 + 1]
                ent = gtok.get(ft % 2)
                if ent:
                    k.wait("tensor", c.sVE, ent[0])
                    k.wait("tensor", c.sSC, ent[1])
                k.wait("tensor", c.sMM, ytok.get(f"hs{ft}", 0))
                for kt in range(8):
                    ksl = slice(kt * T + tf * 512, kt * T + (tf + 1) * 512)
                    mm(k, pb1[:, :], c.wbf[bsel][:, kt * 128:(kt + 1) * 128],
                       c.xnbf[:, ksl], start=(kt == 0), stop=(kt == 7))
                    mm(k, pb2[:, :], c.wbg[bsel][:, kt * 128:(kt + 1) * 128],
                       c.xnbf[:, ksl], start=(kt == 0), stop=(kt == 7))
                th = k.inc("tensor", c.sMM)
                wtok[f"w{bsel}"] = th
                k.wait("scalar", c.sMM, th)
                k.wait("scalar", c.sVE, gtok.get("thh_last", 0))
                tg1 = act_op(k, c.F32[:, 0:512], pb1[:, :], ACT.Gelu, sem=c.sSC)
                k.wait("vector", c.sSC, tg1)
                k.wait("vector", c.sMM, th)
                tt_op(k, "vector", c.F32[:, 512:1024], c.F32[:, 0:512], pb2[:, :],
                      ALU.mult)
                k.drain("vector")
                thh = tt_op(k, "vector", hslot(c, ft), c.F32[:, 512:1024],
                            c.F32[:, 1024:1536], ALU.mult, sem=c.sVE)
                gtok[ft % 2] = (thh, tg1)
                gtok["thh_last"] = thh
                c.pump(1)
            thp = k.inc("vector", c.sVE)
            k.wait("tensor", c.sVE, thp)
            for ot in range(8):
                rk = f"y{ot}_{tf}"
                if e > 0:
                    k.wait("sync", c.sW, ytok.get(rk + "w", 0))
                    k.wait("sync", c.sVE, ytok.get("yin", 0))
                    tyi = k.dma("sync", c.F32[:, 1536:2048],
                                c.yacc[ot][:, tfs], c.sW)
                    k.wait("sync", c.sW, tyi)
                    k.wait("vector", c.sW, tyi)
                pb = c.ps[4 + ot % 2]
                entp = ytok.get(f"pb{ot % 2}")
                if entp is not None:
                    k.wait("tensor", entp[0], entp[1])
                for cc in range(3):
                    nft = 8 if cc < 2 else 6
                    qb = cc % 2
                    k.wait("sync", c.sMM, wtok.get(f"w{qb}", 0))
                    tw = k.dma("sync", c.wbf[qb][:, 0:nft * 128].rearrange(
                        "p (a b) -> p a b", b=128),
                        c.L1w2b[e].rearrange("(a p) f -> p a f", p=128)[
                            :, cc * 8:cc * 8 + nft, ot * 128:(ot + 1) * 128],
                        c.sW)
                    k.wait("sync", c.sW, tw)
                    k.wait("tensor", c.sW, tw)
                    for fi in range(nft):
                        ft = cc * 8 + fi
                        mm(k, pb[:, :], c.wbf[qb][:, fi * 128:(fi + 1) * 128],
                           hslot(c, ft), start=(ft == 0), stop=(ft == 21))
                    wtok[f"w{qb}"] = k.inc("tensor", c.sMM)
                ty = c.sMM.n
                for ft in range(22):
                    ytok[f"hs{ft}"] = ty
                ev = c.G32[:, (ot % 2) * 512:(ot % 2) * 512 + 512]
                if e == 0:
                    k.wait("scalar", c.sMM, ty)
                    k.wait("scalar", c.sW, ytok.get(f"evw{ot % 2}", 0))
                    tev = copy_op(k, "scalar", ev, pb[:, :], sem=c.sSC)
                    ytok[f"pb{ot % 2}"] = (c.sSC, tev)
                    k.wait("sync", c.sSC, tev)
                else:
                    k.wait("vector", c.sMM, ty)
                    k.wait("vector", c.sW, ytok.get(f"evw{ot % 2}", 0))
                    tev = tt_op(k, "vector", ev, c.F32[:, 1536:2048], pb[:, :],
                                ALU.add, sem=c.sVE)
                    ytok["yin"] = tev
                    ytok[f"pb{ot % 2}"] = (c.sVE, tev)
                    k.wait("sync", c.sVE, tev)
                tyw = k.dma("sync", c.yacc[ot][:, tfs], ev, c.sW)
                k.wait("sync", c.sW, tyw)
                ytok[rk + "w"] = tyw
                ytok[f"evw{ot % 2}"] = tyw
                c.pump(1)
    k.wait("vector", c.sW, c.tok_l1s)
    k.wait("scalar", c.sW, c.tok_l1s)
    for tf in range(4):
        tfs = slice(tf * 512, (tf + 1) * 512)
        for ot in range(8):
            k.wait("sync", c.sW, ytok.get(f"y{ot}_{tf}w", 0))
            k.wait("sync", c.sSC, ytok.get("cmb", 0))
            tyi = k.dma("sync", c.F32[:, 1536:2048], c.yacc[ot][:, tfs], c.sW)
            k.wait("sync", c.sW, tyi)
            k.wait("scalar", c.sW, tyi)
            k.wait("scalar", c.sVE, ytok.get("cmbv", 0))
            tsi = act_op(k, c.hA[:, 0:512], c.F32[:, 1536:2048], ACT.Silu,
                         bias=c.colsB[:, ot:ot + 1], sem=c.sSC)
            ytok["cmb"] = tsi
            k.wait("vector", c.sSC, tsi)
            sl = slice(ot * T + tf * 512, ot * T + (tf + 1) * 512)
            tt_op(k, "vector", c.F32[:, 512:1024], c.sres[:, sl], c.hA[:, 0:512],
                  ALU.mult)
            k.drain("vector")
            tcx = tt_op(k, "vector", c.xres[:, sl], c.xres[:, sl],
                        c.F32[:, 512:1024], ALU.add, sem=c.sVE)
            ytok["cmbv"] = tcx
    c.tok_l1 = k.inc("vector", c.sVE)




def l2_cast_gen(k, c):
    k.wait("gpsimd", c.sVE, c.tok_l1)
    t = None
    for wi, wsrc in enumerate((c.L2WqT, c.L2WkT, c.L2WvT, c.L2WuT)):
        t = k.dma("gpsimd", c.WbfA[wi], wsrc, c.sD2)
        yield
    k.wait("gpsimd", c.sD2, t)
    c.tok_l2qkv = c.sD2.n
    yield
    k.dma("gpsimd", c.l2w1b, c.L2w1, c.sD2)
    yield
    t = k.dma("gpsimd", c.l2w2b, c.L2w2, c.sD2)
    k.wait("gpsimd", c.sD2, t)
    c.tok_l2w = c.sD2.n
    while True:
        yield


def l2_sen_gen(k, c):
    tg = k.dma("sync", c.F32[0:1, 1024:2048], c.L2gs, c.sW)
    k.wait("sync", c.sW, tg)
    k.wait("tensor", c.sW, tg)
    mm(k, c.ps[4][:, :], c.cONESROW[0:1, 0:128], c.F32[0:1, 1024:1536],
       start=True, stop=True)
    mm(k, c.ps[5][:, :], c.cONESROW[0:1, 0:128], c.F32[0:1, 1536:2048],
       start=True, stop=True)
    tgr = k.inc("tensor", c.sMM)
    k.wait("scalar", c.sMM, tgr)
    copy_op(k, "scalar", c.F32[:, 1024:1536], c.ps[4][:, :])
    tge = copy_op(k, "scalar", c.F32[:, 1536:2048], c.ps[5][:, :], sem=c.sSC)
    k.wait("sync", c.sSC, tge)
    k.wait("vector", c.sSC, tge)
    yield
    tdma_prev = 0
    ttr_prev = 0
    tsc_prev = 0
    tsq_prev = 0
    for nt in range(32):
        k.wait("sync", c.sVE, tsc_prev)
        k.wait("sync", c.sSC, tsq_prev)
        tld = k.dma("sync", c.G32[:, 0:1024], c.L2se[nt * 128:(nt + 1) * 128, :], c.sW)
        k.wait("sync", c.sW, tld)
        k.wait("scalar", c.sW, tld)
        tsq = act_op(k, c.G32[:, 1024:2048], c.G32[:, 0:1024], ACT.Square,
                     accum=c.F32[:, 2048:2049], sem=c.sSC)
        tsq_prev = tsq
        k.wait("scalar", c.sSC, tsq)
        k.drain("scalar")
        tsr = act_op(k, c.F32[:, 2049:2050], c.F32[:, 2048:2049], ACT.Sqrt, sem=c.sSC)
        k.wait("vector", c.sSC, tsr)
        ts_op(k, "vector", c.F32[:, 2050:2051], c.F32[:, 2049:2050], 1e-12, ALU.max)
        k.drain("vector")
        k.raw("vector", lambda e, o=c.F32[:, 2051:2052], i=c.F32[:, 2050:2051]:
              e.reciprocal(o, i))
        k.drain("vector")
        ts_op(k, "vector", c.F32[:, 2051:2052], c.F32[:, 2051:2052], 32.0, ALU.mult)
        k.drain("vector")
        k.wait("vector", c.sMM, ttr_prev)
        tsc = stt_op(k, "vector", c.wbf[1][:, 0:1024], c.G32[:, 0:1024],
                     c.F32[:, 2051:2052], ALU.mult, c.F32[:, 1024:2048], ALU.mult,
                     sem=c.sVE)
        tsc_prev = tsc
        k.wait("tensor", c.sVE, tsc)
        for wv in range(2):
            for dq in range(4):
                dt = wv * 4 + dq
                tr_op(k, c.ps7b[:, dq * 128:(dq + 1) * 128],
                      c.wbf[1][:, dt * 128:(dt + 1) * 128], c.cIDTB[:])
            ttr = k.inc("tensor", c.sMM)
            k.wait("vector", c.sMM, ttr)
            k.wait("vector", c.sW, tdma_prev)
            for dq in range(4):
                dt = wv * 4 + dq
                copy_op(k, "vector", c.wbf[0][:, dt * 128:(dt + 1) * 128],
                        c.ps7b[:, dq * 128:(dq + 1) * 128])
            tev = k.inc("vector", c.sVE)
            k.wait("tensor", c.sVE, tev)
        ttr_prev = c.sMM.n
        k.wait("sync", c.sVE, c.sVE.n)
        tdma_prev = k.dma("sync",
                          c.senTb.rearrange("(a p) n -> p a n", p=128)[:, :, nt * 128:(nt + 1) * 128],
                          c.wbf[0][:, 0:1024].rearrange("p (a b) -> p a b", b=128),
                          c.sW)
        k.wait("sync", c.sW, tdma_prev)
        yield
    c.tok_sen = tdma_prev
    c.tok_sen_ve = c.sVE.n
    while True:
        yield


def l2_attn_heads(k, c):
    pump2 = c.pump2
    tok_mm_prev = c.sMM.n
    tok_sc_prev = c.sSC.n
    tok_ve_prev = c.sVE.n
    tok_s_prev = 0
    for h in range(16):
        base = 64 * (h % 2)
        bsl = slice(base, base + 64)
        k.wait("sync", c.sMM, tok_mm_prev)
        k.wait("sync", c.sSC, tok_sc_prev)
        k.wait("sync", c.sVE, tok_ve_prev)
        k.wait("sync", c.sW, tok_s_prev)
        k.wait("vector", c.sMM, tok_mm_prev)
        k.dma("sync", c.aK[bsl, 0:2048], c.kfms[64 * h:64 * h + 64, :], c.sW)
        k.dma("sync", c.aQ[bsl, 0:2048], c.qfms[64 * h:64 * h + 64, :], c.sW)
        vdst = c.aVtm[:, 0:1040].rearrange("p (j v) -> p j v", v=65)[:, :, 0:64]
        vsrc = c.vtms.rearrange("(n p) o -> p n o", p=128)[:, :, 64 * h:64 * h + 64]
        tv = k.dma("sync", vdst, vsrc, c.sW)
        k.wait("sync", c.sW, tv)
        for eng in ("tensor", "vector", "scalar"):
            k.wait(eng, c.sW, tv)
        tso_prev = 0
        for tch in range(4):
            tsl = slice(tch * 512, (tch + 1) * 512)
            ptok = {}
            tpk_prev = 0
            k.wait("tensor", c.sVE, tso_prev)
            for jt in range(16):
                ent = ptok.get(jt % 2)
                if ent:
                    k.wait("tensor", c.sSC, ent)
                mm(k, c.ps[jt % 2][:, :], c.aK[bsl, jt * 128:(jt + 1) * 128],
                   c.aQ[bsl, tsl], start=True, stop=True)
                tpm = k.inc("tensor", c.sMM)
                k.wait("scalar", c.sMM, tpm)
                k.wait("scalar", c.sMM, ptok.get(f"pv{jt % 2}", 0))
                tpk = act_op(k, c.hA[:, (jt % 2) * 512:(jt % 2) * 512 + 512],
                             c.ps[jt % 2][:, :], ACT.Exp, scale=0.03125, sem=c.sSC)
                ptok[jt % 2] = tpk
                k.wait("tensor", c.sSC, tpk)
                mm(k, c.ps[2][0:65, :], c.aVtm[:, jt * 65:jt * 65 + 65],
                   c.hA[:, (jt % 2) * 512:(jt % 2) * 512 + 512],
                   start=(jt == 0), stop=(jt == 15))
                ptok[f"pv{jt % 2}"] = k.inc("tensor", c.sMM)
            tout = c.sMM.n
            k.wait("scalar", c.sMM, tout)
            tden = act_op(k, c.F32[64:65, 0:512], c.ps[2][64:65, :], ACT.Copy,
                          sem=c.sSC)
            k.wait("vector", c.sSC, tden)
            k.raw("vector", lambda e, o=c.F32[64:65, 512:1024],
                  i=c.F32[64:65, 0:512]: e.reciprocal(o, i))
            trc = k.inc("vector", c.sVE)
            k.wait("tensor", c.sVE, trc)
            mm(k, c.ps[3][:, :], c.cONESROW[64:65, 0:128], c.F32[64:65, 512:1024],
               start=True, stop=True)
            trp = k.inc("tensor", c.sMM)
            k.wait("scalar", c.sMM, trp)
            trv = copy_op(k, "scalar", c.F32[0:64, 0:512], c.ps[3][0:64, :],
                          sem=c.sSC)
            k.wait("vector", c.sSC, trv)
            k.wait("vector", c.sW, c.bank_free.get("l2so", 0))
            tso = tt_op(k, "vector", c.hA[0:64, 1024:1536], c.ps[2][0:64, :],
                        c.F32[0:64, 0:512], ALU.mult, sem=c.sVE)
            tso_prev = tso
            k.wait("sync", c.sVE, tso)
            tsd = k.dma("sync", c.sres[bsl, (h // 2) * T + tch * 512:
                                       (h // 2) * T + (tch + 1) * 512],
                        c.hA[0:64, 1024:1536], c.sW)
            c.bank_free["l2so"] = tsd
            k.wait("sync", c.sW, tsd)
            pump2(1)
        tok_mm_prev = c.sMM.n
        tok_sc_prev = c.sSC.n
        tok_ve_prev = c.sVE.n
        tok_s_prev = c.sW.n
    c.tok_l2pre = tok_s_prev
    while not hasattr(c, "tok_sen"):
        c.pump2(1)
    tbu = k.dma("sync", c.colsB[:, 24:32], c.L2bu, c.sW)
    k.wait("sync", c.sW, tbu)
    k.wait("scalar", c.sW, tbu)
    k.wait("sync", c.sVE, c.tok_sen_ve)
    k.wait("sync", c.sMM, tok_mm_prev)
    k.wait("tensor", c.sVE, tok_ve_prev)
    k.wait("tensor", c.sSC, tok_sc_prev)
    for tc in range(4):
        for og in range(2):
            for dv in range(8):
                bsel = dv % 2
                k.wait("sync", c.sMM, c.bank_free.get(f"l2wu{bsel}", 0))
                tw = k.dma("sync", c.wbf[bsel][:, 0:512],
                           c.WbfA[3, dv * 128:(dv + 1) * 128,
                                  og * 512:(og + 1) * 512], c.sW)
                k.wait("sync", c.sW, tw)
                k.wait("tensor", c.sW, tw)
                if dv == 0 and og == 0:
                    k.wait("tensor", c.sVE, c.bank_free.get("l2wu_ve", 0))
                    k.wait("tensor", c.sSC, c.bank_free.get("l2wu_sc", 0))
                for oo in range(4):
                    mm(k, c.ps[oo][:, :], c.wbf[bsel][:, oo * 128:(oo + 1) * 128],
                       c.sres[:, dv * T + tc * 512: dv * T + (tc + 1) * 512],
                       start=(dv == 0), stop=(dv == 7))
                tmm = k.inc("tensor", c.sMM)
                c.bank_free[f"l2wu{bsel}"] = tmm
            tmm = c.sMM.n
            if og == 0:
                k.wait("vector", c.sMM, tmm)
                for oo in range(4):
                    ts_op(k, "vector", c.F32[:, oo * 512:(oo + 1) * 512],
                          c.ps[oo][:, :], c.colsB[:, 24 + oo:25 + oo], ALU.add)
                    k.drain("vector")
                tstg = k.inc("vector", c.sVE)
                k.wait("tensor", c.sVE, tstg)
        k.wait("vector", c.sMM, tmm)
        for oo in range(4):
            ts_op(k, "vector",
                  c.sres[:, (4 + oo) * T + tc * 512: (4 + oo) * T + (tc + 1) * 512],
                  c.ps[oo][:, :], c.colsB[:, 28 + oo:29 + oo], ALU.add)
            k.drain("vector")
        for oo in range(4):
            tt_op(k, "vector",
                  c.sres[:, oo * T + tc * 512: oo * T + (tc + 1) * 512],
                  c.F32[:, oo * 512:(oo + 1) * 512],
                  c.F32[:, oo * 512:(oo + 1) * 512], ALU.min)
            k.drain("vector")
        tev3 = k.inc("vector", c.sVE)
        k.wait("tensor", c.sVE, tev3)
        c.bank_free["l2wu_ve"] = tev3
        c.bank_free["l2wu_sc"] = 0
    c.tok_l2s = k.inc("vector", c.sVE)
    k.wait("scalar", c.sVE, c.tok_l2s)
    c.tok_l2s_sc = k.inc("scalar", c.sSC)


def l2_xnr(k, c):
    k.wait("sync", c.sVE, c.tok_l2s)
    tgx = k.dma("sync", c.colsB[:, 8:16], c.L2gx, c.sW)
    tcd = k.dma("sync", c.cIDF[:], c.IDTF, c.sW)
    k.wait("sync", c.sW, tcd)
    k.wait("vector", c.sW, tcd)
    k.wait("scalar", c.sVE, c.tok_l2s)
    k.wait("tensor", c.sVE, c.tok_l2s)
    k.wait("scalar", c.sSC, c.tok_l2s_sc)
    evac = lambda i: (c.aQ[:, i * 512:(i + 1) * 512] if i < 4
                      else c.aK[:, (i - 4) * 512:(i - 3) * 512])
    tve_prev = 0
    tw_prev = 0
    trr_prev = 0
    for tc in range(4):
        buf = c.F32[:, 1024:1536]
        for dt in range(8):
            sl = slice(dt * T + tc * 512, dt * T + (tc + 1) * 512)
            if dt == 0:
                k.wait("scalar", c.sVE, trr_prev)
            tsq = act_op(k, buf, c.xnbf[:, sl], ACT.Square, sem=c.sSC)
            k.wait("tensor", c.sSC, tsq)
            mm(k, c.ps[0][0:1, :], c.cols[:, 67:68], buf,
               start=(dt == 0), stop=(dt == 7))
            tm2 = k.inc("tensor", c.sMM)
            if dt < 7:
                k.wait("scalar", c.sMM, tm2)
        tmm = c.sMM.n
        k.wait("scalar", c.sMM, tmm)
        trw = act_op(k, c.F32[0:1, 1536:2048], c.ps[0][0:1, :], ACT.Sqrt, sem=c.sSC)
        k.wait("vector", c.sSC, trw)
        ts_op(k, "vector", c.F32[0:1, 1536:2048], c.F32[0:1, 1536:2048], 1e-12,
              ALU.max)
        k.drain("vector")
        k.raw("vector", lambda e, o=c.F32[0:1, 1024:1536],
              i=c.F32[0:1, 1536:2048]: e.reciprocal(o, i))
        k.drain("vector")
        trr = ts_op(k, "vector", c.F32[0:1, 1536:2048], c.F32[0:1, 1024:1536],
                    32.0, ALU.mult, sem=c.sVE)
        trr_prev = trr
        k.wait("tensor", c.sVE, trr)
        mm(k, c.ps[1][:, :], c.cONESROW[0:1, 0:128], c.F32[0:1, 1536:2048],
           start=True, stop=True)
        trp = k.inc("tensor", c.sMM)
        k.wait("vector", c.sMM, trp)
        for dt in range(8):
            sl = slice(dt * T + tc * 512, dt * T + (tc + 1) * 512)
            tt_op(k, "vector", c.F32[:, 0:512], c.xnbf[:, sl], c.ps[1][:, :],
                  ALU.mult)
            k.drain("vector")
            tvx = ts_op(k, "vector", evac(dt), c.F32[:, 0:512],
                        c.colsB[:, 8 + dt:9 + dt], ALU.mult, sem=c.sVE)
            k.drain("vector")
        tve = c.sVE.n
        k.wait("sync", c.sVE, tve)
        k.wait("tensor", c.sVE, tve)
        for dt in range(8):
            k.dma("sync", c.qfms[dt * 128:(dt + 1) * 128, tc * 512:(tc + 1) * 512],
                  evac(dt), c.sW)
        for tt4 in range(4):
            for wv in range(2):
                for dq in range(4):
                    dt = wv * 4 + dq
                    tr_op(k, c.ps7b[:, dq * 128:(dq + 1) * 128],
                          evac(dt)[:, tt4 * 128:(tt4 + 1) * 128], c.cIDTB[:])
                ttr = k.inc("tensor", c.sMM)
                k.wait("vector", c.sMM, ttr)
                k.wait("vector", c.sW, tw_prev)
                for dq in range(4):
                    dt = wv * 4 + dq
                    copy_op(k, "vector", c.wbf[0][:, dt * 128:(dt + 1) * 128],
                            c.ps7b[:, dq * 128:(dq + 1) * 128])
                tev = k.inc("vector", c.sVE)
                k.wait("tensor", c.sVE, tev)
            k.wait("sync", c.sVE, c.sVE.n)
            tw_prev = k.dma("sync",
                            c.vtms[tc * 512 + tt4 * 128: tc * 512 + (tt4 + 1) * 128, :],
                            c.wbf[0][:, 0:1024], c.sW)
            k.wait("sync", c.sW, tw_prev)
    c.tok_xnr = c.sW.n


def l2_slot_ffn(k, c):
    sslot = lambda i: (c.aK[:, i * 512:(i + 1) * 512] if i < 4 else
                       (c.aKtm[:, (i - 4) * 512:(i - 3) * 512] if i < 6 else
                        c.wbf[0][:, (i - 6) * 512:(i - 5) * 512]))
    hslot2 = lambda i: (c.hA[:, i * 512:(i + 1) * 512] if i < 12
                        else c.aQ[:, (i - 12) * 512:(i - 11) * 512])
    qfv = c.qfms.rearrange("(a p) t -> p a t", p=128)
    snv = c.senTb.rearrange("(a p) n -> p a n", p=128)
    w1v = c.l2w1b.rearrange("(a p) f -> p a f", p=128)
    w2v = c.l2w2b.rearrange("(a p) o -> p a o", p=128)
    stok = {}
    for ng in range(8):
        k.wait("sync", c.sMM, stok.get("sen", 0))
        for dt in range(8):
            tsn = k.dma("sync", sslot(dt),
                        c.senTb[dt * 128:(dt + 1) * 128, ng * 512:(ng + 1) * 512],
                        c.sW)
        k.wait("sync", c.sW, tsn)
        k.wait("tensor", c.sW, tsn)
        for tc in range(4):
            for dt in range(8):
                bs = dt % 2
                k.wait("sync", c.sMM, stok.get(f"x{bs}", 0))
                tx = k.dma("sync", c.wbg[bs][:, 0:512],
                           c.qfms[dt * 128:(dt + 1) * 128, tc * 512:(tc + 1) * 512],
                           c.sW)
                k.wait("sync", c.sW, tx)
                k.wait("tensor", c.sW, tx)
                if dt == 0:
                    k.wait("tensor", c.sVE, stok.get("rm", 0))
                for ni in range(4):
                    mm(k, c.ps[ni][:, :], sslot(dt)[:, ni * 128:(ni + 1) * 128],
                       c.wbg[bs][:, 0:512], start=(dt == 0), stop=(dt == 7))
                stok[f"x{bs}"] = k.inc("tensor", c.sMM)
            tmm = c.sMM.n
            k.wait("vector", c.sMM, tmm)
            for ni in range(4):
                dcol = c.F32[:, 2048 + ng * 4 + ni:2049 + ng * 4 + ni]
                if tc == 0:
                    k.raw("vector", lambda e, o=dcol, i=c.ps[ni][:, :]:
                          e.reduce_max(o, i, axis=AX.X))
                else:
                    k.raw("vector", lambda e, o=c.F32[:, 2080:2081],
                          i=c.ps[ni][:, :]: e.reduce_max(o, i, axis=AX.X))
                    k.drain("vector")
                    tt_op(k, "vector", dcol, dcol, c.F32[:, 2080:2081], ALU.max)
                k.drain("vector")
            stok["rm"] = k.inc("vector", c.sVE)
        stok["sen"] = c.sMM.n
        c.pump(1)
    tng = ts_op(k, "vector", c.F32[:, 2080:2112], c.F32[:, 2048:2080], -1.0,
                ALU.mult, sem=c.sVE)
    k.wait("tensor", c.sVE, tng)
    tdt = tr_op(k, c.ps[0][0:32, 0:128], c.F32[:, 2080:2112], c.cIDF[:], sem=c.sMM)
    k.wait("scalar", c.sMM, tdt)
    tde = copy_op(k, "scalar", c.F32[0:32, 2048:2176], c.ps[0][0:32, 0:128],
                  sem=c.sSC)
    k.wait("sync", c.sSC, tde)
    tdd = k.dma("sync", c.G32[32:64, 0:128], c.F32[0:32, 2048:2176], c.sW)
    k.wait("sync", c.sW, tdd)
    eslot = lambda i: (c.hA[:, i * 512:(i + 1) * 512] if i < 12
                       else c.aQ[:, (i - 12) * 512:(i - 11) * 512])
    vslot = lambda i: (c.aK[:, i * 512:(i + 1) * 512] if i < 4 else
                       (c.aKtm[:, (i - 4) * 512:(i - 3) * 512] if i < 6 else
                        c.wbf[0][:, (i - 6) * 512:(i - 5) * 512]))
    btok = {}
    for n4 in range(8):
        k.wait("sync", c.sMM, btok.get("crw", 0))
        tcr = k.dma("sync", c.crow5[0:1, 0:512].rearrange("p (a b) -> p a b", b=128),
                    c.G32[32 + 4 * n4:32 + 4 * (n4 + 1), 0:128], c.sW)
        k.wait("sync", c.sW, tcr)
        k.wait("tensor", c.sW, tcr)
        k.wait("tensor", c.sSC, btok.get("nrep", 0))
        mm(k, c.ps[2][:, :], c.cONESROW[0:1, 0:128], c.crow5[0:1, 0:512],
           start=True, stop=True)
        btok["crw"] = k.inc("tensor", c.sMM)
        k.wait("scalar", c.sMM, btok["crw"])
        k.wait("scalar", c.sVE, btok.get("ndrep_ve", 0))
        k.wait("scalar", c.sVE, btok.get("tadl", 0))
        tnr = copy_op(k, "scalar", c.F32[:, 1536:2048], c.ps[2][:, :], sem=c.sSC)
        btok["nrep"] = tnr
        k.wait("vector", c.sSC, tnr)
        k.wait("sync", c.sMM, btok.get("sen", 0))
        for dt in range(8):
            tsn = k.dma("sync", sslot(dt),
                        c.senTb[dt * 128:(dt + 1) * 128, n4 * 512:(n4 + 1) * 512],
                        c.sW)
        k.wait("sync", c.sW, tsn)
        k.wait("tensor", c.sW, tsn)
        for tt in range(16):
            bs = tt % 2
            k.wait("sync", c.sMM, btok.get(f"x{bs}", 0))
            tx = k.dma("sync", c.wbg[bs][:, 0:1024].rearrange("p (a b) -> p a b", b=128),
                       qfv[:, :, tt * 128:(tt + 1) * 128], c.sW)
            k.wait("sync", c.sW, tx)
            k.wait("tensor", c.sW, tx)
            pl = c.ps[tt % 2]
            ent = btok.get(f"pl{tt % 2}")
            if ent is not None:
                k.wait("tensor", c.sVE, ent)
            for dt in range(8):
                mm(k, pl[:, :], c.wbg[bs][:, dt * 128:(dt + 1) * 128], sslot(dt),
                   start=(dt == 0), stop=(dt == 7))
            btok[f"x{bs}"] = k.inc("tensor", c.sMM)
            tlg = btok[f"x{bs}"]
            k.wait("vector", c.sMM, tlg)
            cmx = c.F32[:, 2112 + tt:2113 + tt]
            if n4 == 0:
                k.raw("vector", lambda e, o=cmx, i=pl[:, :]:
                      e.reduce_max(o, i, axis=AX.X))
            else:
                k.raw("vector", lambda e, o=c.F32[:, 2080:2081], i=pl[:, :]:
                      e.reduce_max(o, i, axis=AX.X))
                k.drain("vector")
                tt_op(k, "vector", cmx, cmx, c.F32[:, 2080:2081], ALU.max)
            k.drain("vector")
            k.wait("vector", c.sSC, btok.get("exp", 0))
            tad = tt_op(k, "vector", c.F32[:, 1024:1536], pl[:, :],
                        c.F32[:, 1536:2048], ALU.add, sem=c.sVE)
            btok[f"pl{tt % 2}"] = tad
            btok["tadl"] = tad
            k.wait("scalar", c.sVE, tad)
            k.wait("scalar", c.sMM, btok.get(f"ekv{tt}", 0))
            tex = act_op(k, eslot(tt), c.F32[:, 1024:1536], ACT.Exp, sem=c.sSC)
            btok["exp"] = tex
            k.wait("tensor", c.sSC, tex)
            mm(k, c.ps[4][0:1, :], c.cONESCOLB[:], eslot(tt),
               start=(tt == 0), stop=(tt == 15))
            k.inc("tensor", c.sMM)
        tdn = c.sMM.n
        k.wait("scalar", c.sMM, tdn)
        tdre = act_op(k, c.F32[0:1, 512:1024], c.ps[4][0:1, :], ACT.Copy, sem=c.sSC)
        k.wait("vector", c.sSC, tdre)
        k.raw("vector", lambda e, o=c.F32[0:1, 0:512], i=c.F32[0:1, 512:1024]:
              e.reciprocal(o, i))
        trc = k.inc("vector", c.sVE)
        k.wait("tensor", c.sVE, trc)
        mm(k, c.ps[5][:, :], c.cONESROW[0:1, 0:128], c.F32[0:1, 0:512],
           start=True, stop=True)
        trp = k.inc("tensor", c.sMM)
        k.wait("scalar", c.sMM, trp)
        tdv = copy_op(k, "scalar", c.F32[:, 512:1024], c.ps[5][:, :], sem=c.sSC)
        k.wait("vector", c.sSC, tdv)
        for tt in range(16):
            bs = tt % 2
            k.wait("sync", c.sMM, btok.get(f"x{bs}", 0))
            tx = k.dma("sync", c.wbg[bs][:, 0:1024],
                       c.vtms[tt * 128:(tt + 1) * 128, :], c.sW)
            k.wait("sync", c.sW, tx)
            k.wait("tensor", c.sW, tx)
            if tt == 0:
                k.wait("tensor", c.sSC, tdv)
                k.wait("tensor", c.sVE, btok.get("sev", 0))
            for dt2 in range(7):
                mm(k, c.ps[dt2][:, :], c.wbg[bs][:, dt2 * 128:(dt2 + 1) * 128],
                   eslot(tt), start=(tt == 0), stop=(tt == 15))
            btok[f"x{bs}"] = k.inc("tensor", c.sMM)
        tsl1 = c.sMM.n
        k.wait("vector", c.sMM, tsl1)
        for dt2 in range(7):
            tt_op(k, "vector", vslot(dt2), c.ps[dt2][:, :], c.F32[:, 512:1024],
                  ALU.mult)
            k.drain("vector")
        tse = k.inc("vector", c.sVE)
        k.wait("tensor", c.sVE, tse)
        for tt in range(16):
            bs = tt % 2
            tx = k.dma("sync", c.bfT[:, (bs) * 128:(bs + 1) * 128],
                       c.vtms[tt * 128:(tt + 1) * 128, 896:1024], c.sW)
            k.wait("sync", c.sW, tx)
            k.wait("tensor", c.sW, tx)
            mm(k, c.ps[0][:, :], c.bfT[:, bs * 128:(bs + 1) * 128], eslot(tt),
               start=(tt == 0), stop=(tt == 15))
            tmb = k.inc("tensor", c.sMM)
            if tt < 15:
                k.wait("sync", c.sMM, tmb)
        tsl2 = c.sMM.n
        k.wait("vector", c.sMM, tsl2)
        tt_op(k, "vector", vslot(7), c.ps[0][:, :], c.F32[:, 512:1024], ALU.mult)
        k.drain("vector")
        tse2 = k.inc("vector", c.sVE)
        btok["sev"] = tse2
        btok["ndrep_ve"] = tse2
        btok["pl0"] = tse2
        btok["pl1"] = tse2
        k.wait("sync", c.sVE, tse2)
        for dt2 in range(8):
            tsw = k.dma("sync",
                        c.slotsTb[dt2 * 128:(dt2 + 1) * 128, n4 * 512:(n4 + 1) * 512],
                        vslot(dt2), c.sW)
        k.wait("sync", c.sW, tsw)
        for tt in range(16):
            btok[f"ekv{tt}"] = tsl2
        btok["sen"] = tsl2
        c.pump(1)
    k.wait("vector", c.sMM, c.sMM.n)
    tng = ts_op(k, "vector", c.F32[:, 2080:2096], c.F32[:, 2112:2128], -1.0,
                ALU.mult, sem=c.sVE)
    k.wait("tensor", c.sVE, tng)
    tdt = tr_op(k, c.ps[0][0:16, 0:128], c.F32[:, 2080:2096], c.cIDF[:], sem=c.sMM)
    k.wait("scalar", c.sMM, tdt)
    tde = copy_op(k, "scalar", c.F32[0:16, 2048:2176], c.ps[0][0:16, 0:128],
                  sem=c.sSC)
    k.wait("sync", c.sSC, tde)
    tcd = k.dma("sync", c.Tst[16:32, 0:128], c.F32[0:16, 2048:2176], c.sW)
    k.wait("sync", c.sW, tcd)
    ftok = {}
    pump_until(c, "tok_l2w")
    k.wait("sync", c.sD2, c.tok_l2w)
    for n4 in range(8):
        k.wait("sync", c.sMM, ftok.get("sl", 0))
        for dt in range(8):
            dst = (c.aK[:, dt * 512:(dt + 1) * 512] if dt < 4 else
                   (c.aKtm[:, (dt - 4) * 512:(dt - 3) * 512] if dt < 6 else
                    c.aVtm[:, (dt - 6) * 512:(dt - 5) * 512]))
            tsl = k.dma("sync", dst,
                        c.slotsTb[dt * 128:(dt + 1) * 128, n4 * 512:(n4 + 1) * 512],
                        c.sW)
        k.wait("sync", c.sW, tsl)
        k.wait("tensor", c.sW, tsl)
        slsl = lambda dt: (c.aK[:, dt * 512:(dt + 1) * 512] if dt < 4 else
                           (c.aKtm[:, (dt - 4) * 512:(dt - 3) * 512] if dt < 6 else
                            c.aVtm[:, (dt - 6) * 512:(dt - 5) * 512]))
        for fh in range(2):
            for key in [f"pa{cb}" for cb in range(7)] + ["pa7"]:
                ent = ftok.get(key)
                if ent is not None:
                    k.wait("tensor", ent[0], ent[1])
            for ft in range(16):
                bs = ft % 2
                k.wait("sync", c.sMM, ftok.get(f"w{bs}", 0))
                tw = k.dma("sync", c.wbg[bs][:, 0:1024].rearrange("p (a b) -> p a b", b=128),
                           w1v[:, :, fh * 2048 + ft * 128: fh * 2048 + (ft + 1) * 128],
                           c.sW)
                k.wait("sync", c.sW, tw)
                k.wait("tensor", c.sW, tw)
                ph = c.ps[ft % 2]
                k.wait("tensor", c.sSC, ftok.get(f"g{ft % 2}", 0))
                k.wait("tensor", c.sMM, ftok.get(f"hs{ft}", 0))
                for dt in range(8):
                    mm(k, ph[:, :], c.wbg[bs][:, dt * 128:(dt + 1) * 128], slsl(dt),
                       start=(dt == 0), stop=(dt == 7))
                ftok[f"w{bs}"] = k.inc("tensor", c.sMM)
                k.wait("scalar", c.sMM, ftok[f"w{bs}"])
                tg = act_op(k, hslot2(ft), ph[:, :], ACT.Gelu, sem=c.sSC)
                ftok[f"g{ft % 2}"] = tg
            thp = k.inc("scalar", c.sSC)
            k.wait("tensor", c.sSC, thp)
            for ft in range(16):
                bs = ft % 2
                k.wait("sync", c.sMM, ftok.get(f"w{bs}", 0))
                tw = k.dma("sync", c.wbg[bs][:, 0:1024],
                           c.l2w2b[fh * 2048 + ft * 128: fh * 2048 + (ft + 1) * 128, :],
                           c.sW)
                k.wait("sync", c.sW, tw)
                k.wait("tensor", c.sW, tw)
                if ft == 0:
                    for cb in range(7):
                        ent = ftok.get(f"pa{cb}")
                        if ent is not None:
                            k.wait("tensor", ent[0], ent[1])
                for cb in range(7):
                    nf, dh = cb // 2, cb % 2
                    mm(k, c.ps[cb][:, :], hslot2(ft)[:, nf * 128:(nf + 1) * 128],
                       c.wbg[bs][:, dh * 512:(dh + 1) * 512],
                       start=(ft == 0), stop=(ft == 15))
                ftok[f"w{bs}"] = k.inc("tensor", c.sMM)
            tpr = c.sMM.n
            for cb in range(7):
                nf, dh = cb // 2, cb % 2
                accsl = (c.G32[:, cb * 512:(cb + 1) * 512] if cb < 4
                         else c.F32[:, (cb - 4) * 512:(cb - 3) * 512])
                if fh == 0:
                    k.wait("scalar", c.sMM, tpr)
                    tev = copy_op(k, "scalar", accsl, c.ps[cb][:, :], sem=c.sSC)
                    ftok[f"pa{cb}"] = (c.sSC, tev)
                else:
                    k.wait("vector", c.sMM, tpr)
                    k.wait("vector", c.sW, ftok.get("pw", 0))
                    tev = tt_op(k, "vector", c.aVtm[:, 1024:1536], c.ps[cb][:, :],
                                accsl, ALU.add, sem=c.sVE)
                    ftok[f"pa{cb}"] = (c.sVE, tev)
                    k.wait("sync", c.sVE, tev)
                    tpw = k.dma("sync",
                                c.procb[n4 * 512 + nf * 128: n4 * 512 + (nf + 1) * 128,
                                        dh * 512:(dh + 1) * 512],
                                c.aVtm[:, 1024:1536], c.sW)
                    k.wait("sync", c.sW, tpw)
                    ftok["pw"] = tpw
            for ft in range(16):
                bs = ft % 2
                k.wait("sync", c.sMM, ftok.get(f"w{bs}", 0))
                tw = k.dma("sync", c.wbg[bs][:, 0:512],
                           c.l2w2b[fh * 2048 + ft * 128: fh * 2048 + (ft + 1) * 128,
                                   512:1024], c.sW)
                k.wait("sync", c.sW, tw)
                k.wait("tensor", c.sW, tw)
                if ft == 0:
                    for key in ("pa7", "pa0"):
                        ent = ftok.get(key)
                        if ent is not None:
                            k.wait("tensor", ent[0], ent[1])
                    k.wait("tensor", c.sSC, ftok.get("pa7s", 0))
                mm(k, c.ps[0][:, :], hslot2(ft)[:, 384:512], c.wbg[bs][:, 0:512],
                   start=(ft == 0), stop=(ft == 15))
                ftok[f"w{bs}"] = k.inc("tensor", c.sMM)
            tp7 = c.sMM.n
            if fh == 0:
                k.wait("scalar", c.sMM, tp7)
                tev = copy_op(k, "scalar", c.F32[:, 1536:2048], c.ps[0][:, :],
                              sem=c.sSC)
                ftok["pa7"] = (c.sSC, tev)
                ftok["pa7s"] = tev
            else:
                k.wait("vector", c.sMM, tp7)
                k.wait("vector", c.sW, ftok.get("pw", 0))
                tev = tt_op(k, "vector", c.aVtm[:, 1024:1536], c.ps[0][:, :],
                            c.F32[:, 1536:2048], ALU.add, sem=c.sVE)
                ftok["pa7"] = (c.sVE, tev)
                k.wait("sync", c.sVE, tev)
                tpw = k.dma("sync",
                            c.procb[n4 * 512 + 384: n4 * 512 + 512, 512:1024],
                            c.aVtm[:, 1024:1536], c.sW)
                k.wait("sync", c.sW, tpw)
                ftok["pw"] = tpw
            ftok["hs0"] = 0
        ftok["sl"] = c.sMM.n
        c.pump(1)
    c.tok_proc = ftok.get("pw", 0)
    ctok = {}
    for tc in range(4):
        k.wait("sync", c.sMM, ctok.get("crw", 0))
        tcr = k.dma("sync", c.crow5[0:1, 0:512].rearrange("p (a b) -> p a b", b=128),
                    c.Tst[16 + 4 * tc:16 + 4 * (tc + 1), 0:128], c.sW)
        k.wait("sync", c.sW, tcr)
        k.wait("tensor", c.sW, tcr)
        for nt4 in range(8):
            k.wait("sync", c.sMM, ctok.get("sen", 0))
            for dt in range(8):
                tsn = k.dma("sync", sslot(dt),
                            c.senTb[dt * 128:(dt + 1) * 128, nt4 * 512:(nt4 + 1) * 512],
                            c.sW)
            k.wait("sync", c.sW, tsn)
            k.wait("tensor", c.sW, tsn)
            for dt in range(8):
                bs = dt % 2
                k.wait("sync", c.sMM, ctok.get(f"x{bs}", 0))
                tx = k.dma("sync", c.wbg[bs][:, 0:512],
                           c.qfms[dt * 128:(dt + 1) * 128, tc * 512:(tc + 1) * 512],
                           c.sW)
                k.wait("sync", c.sW, tx)
                k.wait("tensor", c.sW, tx)
                if dt == 0:
                    for ni in range(4):
                        k.wait("tensor", c.sSC, ctok.get(f"es{ni}", 0))
                for ni in range(4):
                    mm(k, c.ps[ni][:, :], sslot(dt)[:, ni * 128:(ni + 1) * 128],
                       c.wbg[bs][:, 0:512], start=(dt == 0), stop=False)
                ctok[f"x{bs}"] = k.inc("tensor", c.sMM)
            for ni in range(4):
                mm(k, c.ps[ni][:, :], c.cONESROW[0:1, 0:128], c.crow5[0:1, 0:512],
                   start=False, stop=True)
            tlg = k.inc("tensor", c.sMM)
            k.wait("scalar", c.sMM, tlg)
            for ni in range(4):
                k.wait("scalar", c.sMM, ctok.get(f"ed{ni}", 0))
                k.wait("scalar", c.sW, ctok.get(f"ew{ni}", 0))
                tex = act_op(k, c.hA[:, ni * 512:(ni + 1) * 512], c.ps[ni][:, :],
                             ACT.Exp, sem=c.sSC)
                ctok[f"es{ni}"] = tex
                k.wait("tensor", c.sSC, tex)
                k.wait("sync", c.sSC, tex)
                mm(k, c.ps[4][0:1, :], c.cONESCOLB[:], c.hA[:, ni * 512:(ni + 1) * 512],
                   start=(nt4 == 0 and ni == 0), stop=(nt4 == 7 and ni == 3))
                ctok[f"ed{ni}"] = k.inc("tensor", c.sMM)
                ctok[f"e{ni}"] = ctok[f"ed{ni}"]
                tew = k.dma("sync",
                            c.expb[(nt4 * 4 + ni) * 128:(nt4 * 4 + ni + 1) * 128,
                                   tc * 512:(tc + 1) * 512],
                            c.hA[:, ni * 512:(ni + 1) * 512], c.sW)
            k.wait("sync", c.sW, tew)
            for ni in range(4):
                ctok[f"ew{ni}"] = tew
            ctok["sen"] = c.sMM.n
            c.pump(1)
        tdn = c.sMM.n
        k.wait("scalar", c.sMM, tdn)
        tce = act_op(k, c.G32[0:1, tc * 512:(tc + 1) * 512], c.ps[4][0:1, :],
                     ACT.Copy, sem=c.sSC)
        k.wait("sync", c.sSC, tce)
        k.wait("vector", c.sSC, tce)
        ctok["crw"] = tdn
    c.tok_expb = c.sW.n
    k.wait("vector", c.sSC, c.tok_l2s_sc)
    dtok = {}
    for d2t in range(8):
        for nt in range(32):
            bs = nt % 2
            k.wait("sync", c.sMM, dtok.get(f"p{bs}", 0))
            k.dma("sync", c.bfT[:, bs * 128:(bs + 1) * 128],
                  c.procb[nt * 128:(nt + 1) * 128, d2t * 128:(d2t + 1) * 128], c.sW)
            tpe = k.dma("sync", c.hA[:, bs * 2048:(bs + 1) * 2048],
                        c.expb[nt * 128:(nt + 1) * 128, :], c.sW)
            k.wait("sync", c.sW, tpe)
            k.wait("tensor", c.sW, tpe)
            if nt == 0:
                for tc in range(4):
                    k.wait("tensor", c.sVE, dtok.get(f"fin{tc}", 0))
            for tc in range(4):
                mm(k, c.ps[tc][:, :], c.bfT[:, bs * 128:(bs + 1) * 128],
                   c.hA[:, bs * 2048 + tc * 512: bs * 2048 + (tc + 1) * 512],
                   start=(nt == 0), stop=(nt == 31))
            dtok[f"p{bs}"] = k.inc("tensor", c.sMM)
        tnm = c.sMM.n
        for tc in range(4):
            k.wait("vector", c.sMM, dtok.get("rcm", 0))
            k.raw("vector", lambda e, o=c.crow5[0:1, 0:512],
                  i=c.G32[0:1, tc * 512:(tc + 1) * 512]: e.reciprocal(o, i))
            trc = k.inc("vector", c.sVE)
            k.wait("tensor", c.sVE, trc)
            k.wait("tensor", c.sSC, dtok.get("rev", 0))
            mm(k, c.ps[5][:, :], c.cONESROW[0:1, 0:128], c.crow5[0:1, 0:512],
               start=True, stop=True)
            trp = k.inc("tensor", c.sMM)
            dtok["rcm"] = trp
            k.wait("scalar", c.sMM, trp)
            k.wait("scalar", c.sMM, tnm)
            k.wait("scalar", c.sVE, dtok.get("fv", 0))
            trv = copy_op(k, "scalar", c.F32[:, 512:1024], c.ps[5][:, :], sem=c.sSC)
            dtok["rev"] = trv
            k.wait("vector", c.sSC, trv)
            tmu = tt_op(k, "vector", c.F32[:, 1024:1536], c.ps[tc][:, :],
                        c.F32[:, 512:1024], ALU.mult, sem=c.sVE)
            dtok[f"fin{tc}"] = tmu
            k.wait("scalar", c.sVE, tmu)
            tsi = act_op(k, c.hA[:, 4096 + (tc % 2) * 512: 4096 + (tc % 2 + 1) * 512],
                         c.F32[:, 1024:1536], ACT.Silu, sem=c.sSC)
            k.wait("vector", c.sSC, tsi)
            sl = slice(d2t * T + tc * 512, d2t * T + (tc + 1) * 512)
            tt_op(k, "vector", c.F32[:, 1536:2048], c.sres[:, sl],
                  c.hA[:, 4096 + (tc % 2) * 512: 4096 + (tc % 2 + 1) * 512], ALU.mult)
            k.drain("vector")
            tcx = tt_op(k, "vector", c.xres[:, sl], c.xres[:, sl],
                        c.F32[:, 1536:2048], ALU.add, sem=c.sVE)
            dtok["fv"] = tcx
        c.pump(1)
    c.tok_l2 = k.inc("vector", c.sVE)



def l3_cast_gen(k, c):
    t = None
    for dst, wsrc in ((c.l3wi, c.L3WinT), (c.l3wo, c.L3WoutT), (c.l3wx, c.L3WxT),
                      (c.l3wdt, c.L3WdtT), (c.l3gw, c.L3gwT),
                      (c.l3sg, c.L3sgT), (c.l3su, c.L3suT), (c.l3sd, c.L3sdT)):
        t = k.dma("gpsimd", dst, wsrc, c.sD2)
        yield
    k.wait("gpsimd", c.sD2, t)
    c.tok_l3pre = c.sD2.n
    yield
    for e in range(32):
        k.dma("gpsimd", c.l3eg[e], c.L3egT[e], c.sD2)
        k.dma("gpsimd", c.l3eu[e], c.L3euT[e], c.sD2)
        t = k.dma("gpsimd", c.l3ed[e], c.L3edT[e], c.sD2)
        k.wait("gpsimd", c.sD2, t)
        setattr(c, f"tok_l3e{e}", c.sD2.n)
        yield
    while True:
        yield


def l3_pre(k, c):
    pump_until(c, "tok_l3pre")
    k.wait("sync", c.sD2, c.tok_l3pre)
    k.wait("sync", c.sVE, c.tok_ln)
    k.wait("tensor", c.sVE, c.tok_ln)
    tcl = k.dma("sync", c.colsC[:, 0:64], c.L3cw, c.sW)
    k.dma("sync", c.colsC[:, 64:80], c.L3D, c.sW)
    k.dma("sync", c.colsB[:, 0:16], c.L3bdt, c.sW)
    tcb = k.dma("sync", c.colsB[:, 16:32], c.L3cb, c.sW)
    k.wait("sync", c.sW, tcb)
    k.wait("scalar", c.sW, tcb)
    k.wait("vector", c.sW, tcb)
    wtok = {}
    tsc_prev = 0
    twq_prev = 0
    evac = lambda i: c.aQ[:, i * 512:(i + 1) * 512]
    wiv = c.l3wi.rearrange("d (a b) -> d a b", b=128)
    for tc in range(4):
        for ow in range(8):
            for dt in range(8):
                bsel = dt % 2
                k.wait("sync", c.sMM, wtok.get(bsel, 0))
                tw = k.dma("sync", c.wbf[bsel][:, 0:512],
                           c.l3wi[dt * 128:(dt + 1) * 128, ow * 512:(ow + 1) * 512],
                           c.sW)
                k.wait("sync", c.sW, tw)
                k.wait("tensor", c.sW, tw)
                if dt == 0:
                    k.wait("tensor", c.sSC, tsc_prev)
                for oi in range(4):
                    mm(k, c.ps[oi][:, :], c.wbf[bsel][:, oi * 128:(oi + 1) * 128],
                       c.xnbf[:, dt * T + tc * 512: dt * T + (tc + 1) * 512],
                       start=(dt == 0), stop=(dt == 7))
                wtok[bsel] = k.inc("tensor", c.sMM)
            tmm = c.sMM.n
            k.wait("scalar", c.sMM, tmm)
            k.wait("scalar", c.sW, twq_prev)
            for oi in range(4):
                copy_op(k, "scalar", evac(oi), c.ps[oi][:, :], sem=c.sSC)
            tsc_prev = c.sSC.n
            k.wait("sync", c.sSC, tsc_prev)
            for oi in range(4):
                k.dma("sync", c.uzb[(ow * 4 + oi) * 128:(ow * 4 + oi + 1) * 128,
                                    tc * 512:(tc + 1) * 512], evac(oi), c.sW)
            twq_prev = c.sW.n
            k.wait("sync", c.sW, twq_prev)
            c.pump(1)
    k.wait("tensor", c.sSC, tsc_prev)
    k.wait("gpsimd", c.sW, twq_prev)
    ctok = {}
    for cht in range(16):
        k.wait("sync", c.sMM, ctok.get("ucmm", 0))
        twx = k.dma("sync", c.mwb[:, 0:96], c.l3wx[cht * 128:(cht + 1) * 128, :],
                    c.sW)
        k.wait("sync", c.sW, twx)
        k.wait("tensor", c.sW, twx)
        for tc in range(4):
            k.wait("gpsimd", c.sVE, ctok.get("pad", 0))
            if tc == 0:
                k.raw("vector", lambda e: e.memset(c.G32[:, 0:3], 0.0))
                tmz = k.inc("vector", c.sVE)
                k.wait("gpsimd", c.sVE, tmz)
                tpd = k.dma("gpsimd", c.G32[:, 3:515],
                            c.uzb[cht * 128:(cht + 1) * 128, 0:512], c.sD3)
            else:
                tpd = k.dma("gpsimd", c.G32[:, 0:515],
                            c.uzb[cht * 128:(cht + 1) * 128, tc * 512 - 3:(tc + 1) * 512],
                            c.sD3)
            k.wait("gpsimd", c.sD3, tpd)
            tpg = k.inc("gpsimd", c.sGP)
            k.wait("vector", c.sGP, tpg)
            k.wait("vector", c.sSC, ctok.get("silu", 0))
            ts_op(k, "vector", c.F32[:, 0:512], c.G32[:, 0:512],
                  c.colsC[:, cht * 4:cht * 4 + 1], ALU.mult)
            k.drain("vector")
            for kk in range(1, 4):
                stt_op(k, "vector", c.F32[:, 0:512], c.G32[:, kk:kk + 512],
                       c.colsC[:, cht * 4 + kk:cht * 4 + kk + 1], ALU.mult,
                       c.F32[:, 0:512], ALU.add)
                k.drain("vector")
            tcv = k.inc("vector", c.sVE)
            ctok["pad"] = tcv
            k.wait("scalar", c.sVE, tcv)
            k.wait("scalar", c.sMM, ctok.get("ucmm", 0))
            k.wait("scalar", c.sW, ctok.get("ucw", 0))
            tsl = act_op(k, c.wbf[0][:, 0:512], c.F32[:, 0:512], ACT.Silu,
                         bias=c.colsB[:, 16 + cht:17 + cht], sem=c.sSC)
            ctok["silu"] = tsl
            k.wait("sync", c.sSC, tsl)
            k.wait("tensor", c.sSC, tsl)
            ctok["ucw"] = k.dma("sync", c.ucfb[cht * 128:(cht + 1) * 128,
                                               tc * 512:(tc + 1) * 512],
                                c.wbf[0][:, 0:512], c.sW)
            k.wait("sync", c.sW, ctok["ucw"])
            mm(k, c.ps[tc][0:96, :], c.mwb[:, 0:96], c.wbf[0][:, 0:512],
               start=(cht == 0), stop=(cht == 15))
            ctok["ucmm"] = k.inc("tensor", c.sMM)
    tdbl = c.sMM.n
    k.wait("scalar", c.sMM, tdbl)
    k.wait("sync", c.sW, ctok["ucw"])
    for tc in range(4):
        copy_op(k, "scalar", c.aK[0:64, tc * 512:(tc + 1) * 512],
                c.ps[tc][0:64, :])
        tbe = copy_op(k, "scalar", c.F32[64:96, tc * 512:(tc + 1) * 512],
                      c.ps[tc][64:96, :], sem=c.sSC)
    k.wait("sync", c.sSC, tbe)
    tbc = k.dma("sync", c.bcd, c.F32[64:96, 0:2048], c.sW)
    k.wait("sync", c.sW, tbc)
    k.wait("tensor", c.sSC, tbe)
    c.tok_l3bc = tbc
    c.tok_l3dtr = tbe
    c.tok_l3w = c.sW.n


def l3_mamba_gen(k, c):
    g = {}
    k.wait("gpsimd", c.sSC, c.tok_l3dtr)
    k.wait("gpsimd", c.sW, c.tok_l3w)
    for cht in range(16):
        k.wait("gpsimd", c.sVE, g.get("dtu", 0))
        k.wait("gpsimd", c.sVE, g.get("y3v", 0))
        k.wait("gpsimd", c.sMM, g.get("rep", 0))
        k.wait("gpsimd", c.sSC, g.get("lastda", 0))
        tuc = k.dma("gpsimd", c.aVtm[:, 0:2048],
                    c.ucfb[cht * 128:(cht + 1) * 128, :], c.sD3)
        k.dma("gpsimd", c.cA[:, 0:16], c.L3AL[:, cht * 16:(cht + 1) * 16], c.sD3)
        tal = k.dma("gpsimd", c.mwb[0:64, 96:224],
                    c.l3wdt[0:64, cht * 128:(cht + 1) * 128], c.sD3)
        k.wait("gpsimd", c.sD3, tal)
        tgb = k.inc("gpsimd", c.sGP)
        k.wait("scalar", c.sGP, tgb)
        k.wait("scalar", c.sVE, g.get("negA", 0))
        tae = act_op(k, c.cA[:, 16:32], c.cA[:, 0:16], ACT.Exp, sem=c.sSC)
        k.wait("vector", c.sSC, tae)
        tan = ts_op(k, "vector", c.cA[:, 16:32], c.cA[:, 16:32], -1.0, ALU.mult,
                    sem=c.sVE)
        g["negA"] = tan
        k.wait("scalar", c.sVE, tan)
        k.wait("tensor", c.sGP, tgb)
        yield
        for tc in range(4):
            k.wait("tensor", c.sVE, g.get("dtmm", 0))
            k.wait("tensor", c.sVE, g.get("wtt", 0))
            k.wait("tensor", c.sVE, g.get("scan", 0))
            k.wait("tensor", c.sSC, g.get("dtsc", 0))
            mm(k, c.ps[5][:, :], c.mwb[0:64, 96:224],
               c.aK[0:64, tc * 512:(tc + 1) * 512], start=True, stop=True)
            tdm = k.inc("tensor", c.sMM)
            k.wait("scalar", c.sMM, tdm)
            k.wait("scalar", c.sVE, g.get("t1", 0))
            fsl = c.F32[:, tc * 512:(tc + 1) * 512]
            act_op(k, fsl, c.ps[5][:, :], ACT.Relu, bias=c.colsB[:, cht:cht + 1])
            k.drain("scalar")
            act_op(k, c.M32[:, 0:512], c.ps[5][:, :], ACT.Abs,
                   bias=c.colsB[:, cht:cht + 1])
            k.drain("scalar")
            act_op(k, c.M32[:, 512:1024], c.M32[:, 0:512], ACT.Exp, scale=-1.0)
            k.drain("scalar")
            tsp = act_op(k, c.M32[:, 0:512], c.M32[:, 512:1024], ACT.Ln,
                         bias=1.0, sem=c.sSC)
            g["dtsc"] = tsp
            k.wait("vector", c.sSC, tsp)
            k.wait("vector", c.sMM, tdm)
            tt_op(k, "vector", fsl, fsl, c.M32[:, 0:512], ALU.add)
            k.drain("vector")
            tdt = tt_op(k, "vector", c.aQ[:, tc * 512:(tc + 1) * 512], fsl,
                        c.aVtm[:, tc * 512:(tc + 1) * 512], ALU.mult, sem=c.sVE)
            g["dtmm"] = tdt
            g["dtu"] = tdt
            k.wait("scalar", c.sVE, tdt)
        k.wait("gpsimd", c.sVE, g.get("dtu", 0))
        if c.debug:
            k.dma("gpsimd", c.dbg_dt[cht * 128:(cht + 1) * 128, :],
                  c.F32[:, 0:2048], c.sD3)
        yield
        for s in range(16):
            k.wait("gpsimd", c.sMM, g.get("rep", 0))
            tbh = k.dma("gpsimd", c.brow[0:1, 0:1024], c.bcd[s, 0:1024], c.sD3)
            tch_ = k.dma("gpsimd", c.crow[0:1, 0:1024], c.bcd[16 + s, 0:1024], c.sD3)
            k.wait("gpsimd", c.sD3, tch_)
            tgb = k.inc("gpsimd", c.sGP)
            k.wait("tensor", c.sGP, tgb)
            for tc in range(4):
                if tc == 2:
                    k.wait("gpsimd", c.sMM, g.get("rep", 0))
                    tbh = k.dma("gpsimd", c.brow[0:1, 0:1024], c.bcd[s, 1024:2048],
                                c.sD3)
                    tch_ = k.dma("gpsimd", c.crow[0:1, 0:1024],
                                 c.bcd[16 + s, 1024:2048], c.sD3)
                    k.wait("gpsimd", c.sD3, tch_)
                    tgb = k.inc("gpsimd", c.sGP)
                    k.wait("tensor", c.sGP, tgb)
                k.wait("tensor", c.sVE, g.get("wtt", 0))
                if s == 0 and tc == 0:
                    k.wait("tensor", c.sSC, g.get("dtsc", 0))
                mm(k, c.ps[5][:, :], c.cONESROW[0:1, 0:128],
                   c.brow[0:1, (tc % 2) * 512:(tc % 2) * 512 + 512],
                   start=True, stop=True)
                k.wait("tensor", c.sVE, g.get("t1", 0))
                mm(k, c.ps[6][:, :], c.cONESROW[0:1, 0:128],
                   c.crow[0:1, (tc % 2) * 512:(tc % 2) * 512 + 512],
                   start=True, stop=True)
                trep = k.inc("tensor", c.sMM)
                g["rep"] = trep
                k.wait("scalar", c.sVE, g.get("scan", 0))
                k.wait("scalar", c.sVE, g.get("t1", 0))
                tda = act_op(k, c.Tst[:, 0:512], c.F32[:, tc * 512:(tc + 1) * 512],
                             ACT.Exp, scale=c.cA[:, 16 + s:17 + s], sem=c.sSC)
                g["lastda"] = tda
                if s == 0:
                    k.wait("gpsimd", c.sSC, tda)
                    if c.debug:
                        tdbg = k.dma("gpsimd", c.dbg_da[cht * 128:(cht + 1) * 128,
                                                        tc * 512:(tc + 1) * 512],
                                     c.Tst[:, 0:512], c.sD3)
                        k.wait("gpsimd", c.sD3, tdbg)
                    tgd = k.inc("gpsimd", c.sGP)
                    k.wait("vector", c.sGP, tgd)
                k.wait("vector", c.sMM, trep)
                twt = tt_op(k, "vector", c.M32[:, 0:512],
                            c.aQ[:, tc * 512:(tc + 1) * 512], c.ps[5][:, :],
                            ALU.mult, sem=c.sVE)
                g["wtt"] = twt
                k.wait("vector", c.sSC, tda)
                init = 0.0 if tc == 0 else c.M32[:, 1024:1025]
                k.raw("vector", lambda e, o=c.M32[:, 512:1024], d0=c.Tst[:, 0:512],
                      d1=c.M32[:, 0:512], init=init:
                      e.tensor_tensor_scan(o, d0, d1, init, ALU.mult, ALU.add))
                k.drain("vector")
                copy_op(k, "vector", c.M32[:, 1024:1025], c.M32[:, 1023:1024])
                k.drain("vector")
                tsc2 = k.inc("vector", c.sVE)
                g["scan"] = tsc2
                tt_op(k, "vector", c.Tst[:, 0:512], c.M32[:, 512:1024],
                      c.ps[6][:, :], ALU.mult)
                k.drain("vector")
                gy = c.G32[:, tc * 512:(tc + 1) * 512]
                if s == 0:
                    t1 = copy_op(k, "vector", gy, c.Tst[:, 0:512], sem=c.sVE)
                else:
                    t1 = tt_op(k, "vector", gy, gy, c.Tst[:, 0:512], ALU.add,
                               sem=c.sVE)
                g["t1"] = t1
                g["t1v"] = t1
            yield
        k.wait("gpsimd", c.sVE, g.get("t1", 0))
        if c.debug:
            k.dma("gpsimd", c.dbg_ys[cht * 128:(cht + 1) * 128, :],
                  c.G32[:, 0:2048], c.sD3)
        for tc in range(4):
            k.wait("gpsimd", c.sSC, g.get("zsl", 0))
            tzg = k.dma("gpsimd", c.M32[:, 0:512],
                        c.uzb[2048 + cht * 128: 2048 + (cht + 1) * 128,
                              tc * 512:(tc + 1) * 512], c.sD3)
            k.wait("gpsimd", c.sD3, tzg)
            tgz = k.inc("gpsimd", c.sGP)
            k.wait("scalar", c.sGP, tgz)
            k.wait("scalar", c.sVE, g.get("t1", 0))
            k.wait("scalar", c.sVE, g.get("y3v", 0))
            tzs = act_op(k, c.M32[:, 512:1024], c.M32[:, 0:512], ACT.Silu,
                         sem=c.sSC)
            g["zsl"] = tzs
            k.wait("vector", c.sSC, tzs)
            gy = c.G32[:, tc * 512:(tc + 1) * 512]
            stt_op(k, "vector", gy, c.aVtm[:, tc * 512:(tc + 1) * 512],
                   c.colsC[:, 64 + cht:65 + cht], ALU.mult, gy, ALU.add)
            k.drain("vector")
            k.wait("vector", c.sW, g.get("y3w", 0))
            ty3 = tt_op(k, "vector", c.aKtm[:, 0:512], gy, c.M32[:, 512:1024],
                        ALU.mult, sem=c.sVE)
            g["y3v"] = ty3
            k.wait("sync", c.sVE, ty3)
            g["y3w"] = k.dma("sync", c.y3b[cht * 128:(cht + 1) * 128,
                                           tc * 512:(tc + 1) * 512],
                             c.aKtm[:, 0:512], c.sW)
            k.wait("sync", c.sW, g["y3w"])
        yield
    c.tok_mamba = (c.sVE.n, c.sW.n, c.sGP.n)
    while True:
        yield


def l3_router(k, c):
    tgw = k.dma("sync", c.gwb[:, 0:256].rearrange("p (a b) -> p a b", b=32),
                c.l3gw.rearrange("(a p) e -> p a e", p=128), c.sW)
    k.wait("sync", c.sW, tgw)
    k.wait("tensor", c.sW, tgw)
    tdma_prev = 0
    ttr_prev = 0
    for tt in range(16):
        for dt in range(8):
            mm(k, c.ps[4][:, 0:32],
               c.xnbf[:, dt * T + tt * 128:dt * T + (tt + 1) * 128],
               c.gwb[:, dt * 32:(dt + 1) * 32], start=(dt == 0), stop=(dt == 7))
        tlg = k.inc("tensor", c.sMM)
        k.wait("scalar", c.sMM, tlg)
        k.wait("scalar", c.sVE, ttr_prev)
        tsg = act_op(k, c.F32[:, 0:32], c.ps[4][:, 0:32], ACT.Sigmoid, sem=c.sSC)
        k.wait("vector", c.sSC, tsg)
        k.raw("vector", lambda e, o=c.F32[:, 32:40].rearrange("p (g o) -> p g o", o=1),
              i=c.F32[:, 0:32].rearrange("p (g e) -> p g e", e=4):
              e.reduce_sum(o, i, axis=AX.X))
        k.drain("vector")
        copy_op(k, "vector", c.F32[:, 40:48], c.F32[:, 32:40])
        k.drain("vector")
        for i in range(3):
            k.raw("vector", lambda e, o=c.F32[:, 48:49], i2=c.F32[:, 40:48]:
                  e.reduce_max(o, i2, axis=AX.X))
            k.drain("vector")
            stt_op(k, "vector", c.F32[:, 40:48], c.F32[:, 40:48], c.F32[:, 48:49],
                   ALU.is_lt, c.F32[:, 40:48], ALU.mult)
            k.drain("vector")
        k.raw("vector", lambda e, o=c.F32[:, 48:49], i2=c.F32[:, 40:48]:
              e.reduce_max(o, i2, axis=AX.X))
        k.drain("vector")
        ts_op(k, "vector", c.F32[:, 40:48], c.F32[:, 32:40], c.F32[:, 48:49],
              ALU.is_ge)
        k.drain("vector")
        tt_op(k, "vector", c.F32[:, 64:96].rearrange("p (g e) -> p g e", e=4),
              c.F32[:, 0:32].rearrange("p (g e) -> p g e", e=4),
              c.F32[:, 40:48].rearrange("p (g o) -> p g o", o=1).broadcast_to((128, 8, 4)),
              ALU.mult)
        k.drain("vector")
        copy_op(k, "vector", c.F32[:, 96:128], c.F32[:, 64:96])
        k.drain("vector")
        for i in range(7):
            k.raw("vector", lambda e, o=c.F32[:, 48:49], i2=c.F32[:, 96:128]:
                  e.reduce_max(o, i2, axis=AX.X))
            k.drain("vector")
            stt_op(k, "vector", c.F32[:, 96:128], c.F32[:, 96:128], c.F32[:, 48:49],
                   ALU.is_lt, c.F32[:, 96:128], ALU.mult)
            k.drain("vector")
        k.raw("vector", lambda e, o=c.F32[:, 48:49], i2=c.F32[:, 96:128]:
              e.reduce_max(o, i2, axis=AX.X))
        k.drain("vector")
        ts_op(k, "vector", c.F32[:, 96:128], c.F32[:, 64:96], c.F32[:, 48:49],
              ALU.is_ge)
        k.drain("vector")
        tt_op(k, "vector", c.F32[:, 96:128], c.F32[:, 96:128], c.F32[:, 64:96],
              ALU.mult)
        k.drain("vector")
        k.raw("vector", lambda e, o=c.F32[:, 48:49].rearrange("p (g o) -> p g o", o=1),
              i=c.F32[:, 96:128].rearrange("p (g e) -> p g e", e=32):
              e.reduce_sum(o, i, axis=AX.X))
        k.drain("vector")
        ts_op(k, "vector", c.F32[:, 48:49], c.F32[:, 48:49], 1e-12, ALU.max)
        k.drain("vector")
        k.raw("vector", lambda e, o=c.F32[:, 49:50], i2=c.F32[:, 48:49]:
              e.reciprocal(o, i2))
        k.drain("vector")
        tcg = ts_op(k, "vector", c.bfS[:, 0:32], c.F32[:, 96:128], c.F32[:, 49:50],
                    ALU.mult, sem=c.sVE)
        ttr_prev = tcg
        k.wait("tensor", c.sVE, tcg)
        tr2 = tr_op(k, c.ps7b[0:32, 0:128], c.bfS[:, 0:32], c.cIDTB[:], sem=c.sMM)
        k.wait("scalar", c.sMM, tr2)
        k.wait("scalar", c.sW, tdma_prev)
        tev = copy_op(k, "scalar", c.F32[0:32, 128:256], c.ps7b[0:32, 0:128],
                      sem=c.sSC)
        k.wait("sync", c.sSC, tev)
        tdma_prev = k.dma("sync", c.ctr32[tt], c.F32[0:32, 128:256], c.sW)
        k.wait("sync", c.sW, tdma_prev)
    c.tok_router3 = tdma_prev


def l3_moe(k, c):
    egv = c.l3eg.rearrange("e (a p) f -> e p a f", p=128)
    euv = c.l3eu.rearrange("e (a p) f -> e p a f", p=128)
    edv = c.l3ed.rearrange("e (a p) o -> e p a o", p=128)
    sgv = c.l3sg.rearrange("(a p) f -> p a f", p=128)
    suv = c.l3su.rearrange("(a p) f -> p a f", p=128)
    sdv = c.l3sd.rearrange("(a p) o -> p a o", p=128)
    ytok = {}
    wtok = {}
    gtok = {}
    stok = 0
    for e in range(33):
        shared = (e == 32)
        if not shared:
            pump_until(c, f"tok_l3e{e}")
            k.wait("sync", c.sD2, getattr(c, f"tok_l3e{e}"))
        for tf in range(4):
            tfs = slice(tf * 512, (tf + 1) * 512)
            if not shared:
                k.wait("sync", c.sMM, gtok.get("crow", 0))
                tcr = k.dma("sync",
                            c.crow5[0:1, 0:512].rearrange("p (a b) -> p a b", b=128),
                            c.ctr32[tf * 4:(tf + 1) * 4, e:e + 1, :], c.sW)
                k.wait("sync", c.sW, tcr)
                k.wait("tensor", c.sW, tcr)
                k.wait("tensor", c.sSC, stok)
                mm(k, c.ps[4][:, :], c.cONESROW[0:1, 0:128], c.crow5[0:1, 0:512],
                   start=True, stop=True)
                tcm = k.inc("tensor", c.sMM)
                gtok["crow"] = tcm
                k.wait("scalar", c.sMM, tcm)
                k.wait("scalar", c.sGP, gtok.get("h2", 0))
                stok = copy_op(k, "scalar", c.MF[:, 512:1024], c.ps[4][:, :],
                               sem=c.sSC)
                k.wait("gpsimd", c.sSC, stok)
            for ft in range(4):
                bsel = ft % 2
                k.wait("sync", c.sMM, wtok.get(f"w{bsel}", 0))
                if shared:
                    tw1 = k.dma("sync", c.wbf[bsel][:, 0:1024].rearrange(
                        "p (a b) -> p a b", b=128),
                        sgv[:, :, ft * 128:(ft + 1) * 128], c.sW)
                    tw2 = k.dma("sync", c.wbg[bsel][:, 0:1024].rearrange(
                        "p (a b) -> p a b", b=128),
                        suv[:, :, ft * 128:(ft + 1) * 128], c.sW)
                else:
                    tw1 = k.dma("sync", c.wbf[bsel][:, 0:1024].rearrange(
                        "p (a b) -> p a b", b=128),
                        egv[e][:, :, ft * 128:(ft + 1) * 128], c.sW)
                    tw2 = k.dma("sync", c.wbg[bsel][:, 0:1024].rearrange(
                        "p (a b) -> p a b", b=128),
                        euv[e][:, :, ft * 128:(ft + 1) * 128], c.sW)
                k.wait("sync", c.sW, tw2)
                k.wait("tensor", c.sW, tw2)
                pb1 = c.ps[ft % 2]
                pb2 = c.ps[2 + ft % 2]
                ent = gtok.get(ft % 2)
                if ent is not None:
                    k.wait("tensor", c.sVE, ent[0])
                    k.wait("tensor", c.sSC, ent[1])
                entp = ytok.get(f"pb{ft % 2}")
                if entp is not None:
                    k.wait("tensor", entp[0], entp[1])
                k.wait("tensor", c.sGP, gtok.get(f"hg{ft % 2}", 0))
                k.wait("tensor", c.sMM, ytok.get(f"hs{ft}", 0))
                for kt in range(8):
                    ksl = slice(kt * T + tf * 512, kt * T + (tf + 1) * 512)
                    mm(k, pb1[:, :], c.wbf[bsel][:, kt * 128:(kt + 1) * 128],
                       c.xnbf[:, ksl], start=(kt == 0), stop=(kt == 7))
                    mm(k, pb2[:, :], c.wbg[bsel][:, kt * 128:(kt + 1) * 128],
                       c.xnbf[:, ksl], start=(kt == 0), stop=(kt == 7))
                th = k.inc("tensor", c.sMM)
                wtok[f"w{bsel}"] = th
                k.wait("scalar", c.sMM, th)
                k.wait("scalar", c.sGP, gtok.get(f"hg{ft % 2}", 0))
                tg1 = act_op(k, c.hC[:, ft * 512:(ft + 1) * 512], pb1[:, :],
                             ACT.Silu, sem=c.sSC)
                k.wait("vector", c.sSC, tg1)
                k.wait("vector", c.sMM, th)
                k.wait("vector", c.sGP, gtok.get("h2", 0))
                thh = tt_op(k, "vector", c.MF[:, 0:512],
                            c.hC[:, ft * 512:(ft + 1) * 512], pb2[:, :],
                            ALU.mult, sem=c.sVE)
                gtok[ft % 2] = (thh, tg1)
                k.wait("gpsimd", c.sVE, thh)
                if shared:
                    th2 = copy_op(k, "gpsimd", c.hC[:, ft * 512:(ft + 1) * 512],
                                  c.MF[:, 0:512], sem=c.sGP)
                else:
                    th2 = tt_op(k, "gpsimd", c.hC[:, ft * 512:(ft + 1) * 512],
                                c.MF[:, 0:512], c.MF[:, 512:1024], ALU.mult,
                                sem=c.sGP)
                gtok[f"hg{ft % 2}"] = th2
                gtok["h2"] = th2
                c.pump(1)
                c.pump2(1)
            thp = k.inc("gpsimd", c.sGP)
            k.wait("tensor", c.sGP, thp)
            for ot in range(8):
                rk = f"y{ot}_{tf}"
                if e > 0:
                    k.wait("sync", c.sW, ytok.get(rk + "w", 0))
                    k.wait("sync", c.sVE, ytok.get("yin", 0))
                    tyi = k.dma("sync", c.MF[:, 1024:1536], c.yacc[ot][:, tfs],
                                c.sW)
                    k.wait("sync", c.sW, tyi)
                    k.wait("vector", c.sW, tyi)
                pb = c.ps[ot % 2]
                entp = ytok.get(f"pb{ot % 2}")
                if entp is not None:
                    k.wait("tensor", entp[0], entp[1])
                k.wait("tensor", c.sSC, gtok.get("lastg", 0))
                bsel = ot % 2
                k.wait("sync", c.sMM, wtok.get(f"w{bsel}", 0))
                if shared:
                    tw = k.dma("sync", c.wbf[bsel][:, 0:512].rearrange(
                        "p (a b) -> p a b", b=128),
                        sdv[:, :, ot * 128:(ot + 1) * 128], c.sW)
                else:
                    tw = k.dma("sync", c.wbf[bsel][:, 0:512].rearrange(
                        "p (a b) -> p a b", b=128),
                        edv[e][:, :, ot * 128:(ot + 1) * 128], c.sW)
                k.wait("sync", c.sW, tw)
                k.wait("tensor", c.sW, tw)
                for ft in range(4):
                    mm(k, pb[:, :], c.wbf[bsel][:, ft * 128:(ft + 1) * 128],
                       c.hC[:, ft * 512:(ft + 1) * 512],
                       start=(ft == 0), stop=(ft == 3))
                tym = k.inc("tensor", c.sMM)
                wtok[f"w{bsel}"] = tym
                for ft in range(4):
                    ytok[f"hs{ft}"] = tym
                if e == 0:
                    k.wait("scalar", c.sMM, tym)
                    k.wait("scalar", c.sW, ytok.get(f"evw{ot % 2}", 0))
                    tev = copy_op(k, "scalar", c.MF[:, 1024:1536], pb[:, :],
                                  sem=c.sSC)
                    ytok[f"pb{ot % 2}"] = (c.sSC, tev)
                    k.wait("sync", c.sSC, tev)
                else:
                    k.wait("vector", c.sMM, tym)
                    tev = tt_op(k, "vector", c.MF[:, 1024:1536],
                                c.MF[:, 1024:1536], pb[:, :], ALU.add, sem=c.sVE)
                    ytok["yin"] = tev
                    ytok[f"pb{ot % 2}"] = (c.sVE, tev)
                    k.wait("sync", c.sVE, tev)
                tyw = k.dma("sync", c.yacc[ot][:, tfs], c.MF[:, 1024:1536], c.sW)
                k.wait("sync", c.sW, tyw)
                ytok[rk + "w"] = tyw
                ytok[f"evw{ot % 2}"] = tyw
                c.pump(1)
                c.pump2(1)
            gtok["lastg"] = stok
    c.tok_moey = ytok


def l3_wout_combine(k, c):
    while True:
        if hasattr(c, "tok_mamba"):
            break
        c.pump2(1)
    tve, tsw, tgp = c.tok_mamba
    k.wait("sync", c.sW, tsw)
    k.wait("sync", c.sVE, tve)
    wtok = {}
    tsc_prev = 0
    tve_prev = 0
    for tc in range(4):
        for og in range(2):
            for cht in range(16):
                bsel = cht % 2
                k.wait("sync", c.sMM, wtok.get(f"s{bsel}", 0))
                tw = k.dma("sync", c.wbf[bsel][:, 0:512],
                           c.l3wo[cht * 128:(cht + 1) * 128, og * 512:(og + 1) * 512],
                           c.sW)
                k.wait("sync", c.sMM, wtok.get(f"m{bsel}", 0))
                tmv = k.dma("sync", c.wbg[bsel][:, 0:512],
                            c.y3b[cht * 128:(cht + 1) * 128, tc * 512:(tc + 1) * 512],
                            c.sW)
                k.wait("sync", c.sW, tmv)
                k.wait("tensor", c.sW, tmv)
                if cht == 0:
                    k.wait("tensor", c.sSC, tsc_prev)
                    k.wait("tensor", c.sVE, tve_prev)
                for oo in range(4):
                    mm(k, c.ps[oo][:, :], c.wbf[bsel][:, oo * 128:(oo + 1) * 128],
                       c.wbg[bsel][:, 0:512], start=(cht == 0), stop=(cht == 15))
                tmm = k.inc("tensor", c.sMM)
                wtok[f"s{bsel}"] = tmm
                wtok[f"m{bsel}"] = tmm
            tmm = c.sMM.n
            k.wait("scalar", c.sMM, tmm)
            for oo in range(4):
                copy_op(k, "scalar",
                        c.sres[:, (og * 4 + oo) * T + tc * 512:
                               (og * 4 + oo) * T + (tc + 1) * 512],
                        c.ps[oo][:, :])
            tsc_prev = k.inc("scalar", c.sSC)
            k.wait("tensor", c.sSC, tsc_prev)
    tsf = tsc_prev
    k.wait("vector", c.sSC, tsf)
    ytok = c.tok_moey
    ct = {}
    for tf in range(4):
        tfs = slice(tf * 512, (tf + 1) * 512)
        for ot in range(8):
            k.wait("sync", c.sW, ytok.get(f"y{ot}_{tf}w", 0))
            k.wait("sync", c.sSC, ct.get("cmb", 0))
            tyi = k.dma("sync", c.F32[:, 1536:2048], c.yacc[ot][:, tfs], c.sW)
            k.wait("sync", c.sW, tyi)
            k.wait("scalar", c.sW, tyi)
            k.wait("scalar", c.sVE, ct.get("cmbv", 0))
            tsi = act_op(k, c.hC[:, 0:512], c.F32[:, 1536:2048], ACT.Silu,
                         sem=c.sSC)
            ct["cmb"] = tsi
            k.wait("vector", c.sSC, tsi)
            sl = slice(ot * T + tf * 512, ot * T + (tf + 1) * 512)
            tt_op(k, "vector", c.F32[:, 512:1024], c.sres[:, sl], c.hC[:, 0:512],
                  ALU.mult)
            k.drain("vector")
            tcx = tt_op(k, "vector", c.xres[:, sl], c.xres[:, sl],
                        c.F32[:, 512:1024], ALU.add, sem=c.sVE)
            ct["cmbv"] = tcx
    c.tok_l3 = k.inc("vector", c.sVE)

def stage_inputs(inputs):
    g = {}
    f32 = np.float32
    t = lambda a: np.ascontiguousarray(np.asarray(a, f32).T)
    import ml_dtypes
    g["c_ident"] = np.eye(16, dtype=f32)
    g["c_identb"] = np.eye(128).astype(ml_dtypes.bfloat16)
    onr = np.zeros((65, 128), dtype=f32); onr[0] = 1.0; onr[64] = 1.0
    g["c_onesrow"] = onr
    g["c_onescolb"] = np.ones((128, 1)).astype(ml_dtypes.bfloat16)
    jj, ii = np.meshgrid(np.arange(128), np.arange(128), indexing="ij")
    g["c_mbig"] = np.where(ii >= jj, 0.0, -1e30).astype(f32)
    repl = np.zeros((16, 8 * 128), dtype=f32)
    for dt in range(8):
        for m in range(128):
            repl[2 * dt + m // 64, dt * 128 + m] = 1.0
    g["c_repl2"] = repl
    import math as _m
    ac = np.zeros((128, 20), dtype=f32)
    for i, v in enumerate(ACT_CONSTS):
        ac[:, i] = v
    g["c_actc"] = ac
    g["L0WqT"] = t(inputs["L0_attn_Wq"]); g["L0WkT"] = t(inputs["L0_attn_Wk"])
    g["L0WvT"] = t(inputs["L0_attn_Wv"]); g["L0WoT"] = t(inputs["L0_attn_Wo"])
    g["L0WwT"] = t(inputs["L0_attn_Ww"]); g["L0WeT"] = t(inputs["L0_attn_We"])
    g["L0bw"] = np.asarray(inputs["L0_attn_bw"], f32).reshape(16, 1)
    g["L0be"] = np.asarray(inputs["L0_attn_be"], f32).reshape(16, 1)
    g["L0W1T"] = t(inputs["L0_ffn_W1"]); g["L0W2T"] = t(inputs["L0_ffn_W2"])
    g["L0b1"] = np.ascontiguousarray(np.asarray(inputs["L0_ffn_b1"], f32).reshape(32, 128).T)
    g["L0b2"] = np.ascontiguousarray(np.asarray(inputs["L0_ffn_b2"], f32).reshape(8, 128).T)
    g["L1Wq"] = np.ascontiguousarray(np.asarray(inputs["L1_attn_Wq"], f32))
    g["L1Wk"] = np.ascontiguousarray(np.asarray(inputs["L1_attn_Wk"], f32))
    g["L1Wv"] = np.ascontiguousarray(np.asarray(inputs["L1_attn_Wv"], f32))
    omgt = np.asarray(inputs["L1_attn_omega"], f32).T
    omg2 = np.zeros((128, 256), dtype=f32)
    omg2[0:64] = omgt; omg2[64:128] = omgt
    g["c_omgt2"] = omg2.astype(ml_dtypes.bfloat16)
    g["c_negh"] = np.full((128, 1), -0.5, dtype=f32)
    g["L1rw"] = np.ascontiguousarray(np.asarray(inputs["L1_ffn_router_w"], f32))
    g["L1w1"] = np.ascontiguousarray(np.asarray(inputs["L1_ffn_w1"], f32))
    g["L1v1"] = np.ascontiguousarray(np.asarray(inputs["L1_ffn_v1"], f32))
    g["L1w2"] = np.ascontiguousarray(np.asarray(inputs["L1_ffn_w2"], f32))
    g["L1bias"] = np.ascontiguousarray(np.asarray(inputs["L1_ffn_bias"], f32).reshape(8, 128).T)
    g["L2WqT"] = t(inputs["L2_attn_Wq"]); g["L2WkT"] = t(inputs["L2_attn_Wk"])
    g["L2WvT"] = t(inputs["L2_attn_Wv"]); g["L2WuT"] = t(inputs["L2_attn_Wu"])
    g["L2se"] = np.ascontiguousarray(np.asarray(inputs["L2_ffn_slot_embeds"], f32).reshape(4096, D))
    g["L2w1"] = np.ascontiguousarray(np.asarray(inputs["L2_ffn_w1"], f32))
    g["L2w2"] = np.ascontiguousarray(np.asarray(inputs["L2_ffn_w2"], f32))
    g["L2gx"] = np.ascontiguousarray(np.asarray(inputs["L2_ffn_gamma_x"], f32).reshape(8, 128).T)
    g["L2bu"] = np.ascontiguousarray(np.asarray(inputs["L2_attn_bu"], f32).reshape(8, 128).T)
    g["L2gs"] = np.ascontiguousarray(np.asarray(inputs["L2_ffn_gamma_slot"], f32).reshape(1, D))
    g["c_identf"] = np.eye(128, dtype=f32)
    g["L3WinT"] = t(inputs["L3_attn_W_in"])
    g["L3WoutT"] = t(inputs["L3_attn_W_out"])
    g["L3WxT"] = t(inputs["L3_attn_W_x"])
    g["L3WdtT"] = t(inputs["L3_attn_W_dt"])
    cw = np.asarray(inputs["L3_attn_conv_w"], f32)
    g["L3cw"] = np.ascontiguousarray(
        cw.reshape(16, 128, 4).transpose(1, 0, 2).reshape(128, 64))
    g["L3cb"] = np.ascontiguousarray(np.asarray(inputs["L3_attn_conv_b"], f32).reshape(16, 128).T)
    g["L3D"] = np.ascontiguousarray(np.asarray(inputs["L3_attn_D"], f32).reshape(16, 128).T)
    g["L3AL"] = np.ascontiguousarray(
        np.asarray(inputs["L3_attn_A_log"], f32).reshape(16, 128, 16).transpose(1, 0, 2).reshape(128, 256))
    g["L3bdt"] = np.ascontiguousarray(np.asarray(inputs["L3_attn_b_dt"], f32).reshape(16, 128).T)
    g["L3gwT"] = t(inputs["L3_ffn_gate_w"])
    g["L3egT"] = np.ascontiguousarray(np.transpose(np.asarray(inputs["L3_ffn_expert_gate"], f32), (0, 2, 1)))
    g["L3euT"] = np.ascontiguousarray(np.transpose(np.asarray(inputs["L3_ffn_expert_up"], f32), (0, 2, 1)))
    g["L3edT"] = np.ascontiguousarray(np.transpose(np.asarray(inputs["L3_ffn_expert_down"], f32), (0, 2, 1)))
    g["L3sgT"] = t(inputs["L3_ffn_shared_gate_w"])
    g["L3suT"] = t(inputs["L3_ffn_shared_up_w"])
    g["L3sdT"] = t(inputs["L3_ffn_shared_down_w"])
    return g


def _np_threefry2x32(k1, k2, x0, x1):
    k1 = np.uint32(k1); k2 = np.uint32(k2)
    x0 = x0.astype(np.uint32).copy(); x1 = x1.astype(np.uint32).copy()
    ks = [k1, k2, np.uint32(k1 ^ k2 ^ np.uint32(0x1BD11BDA))]
    rot = [(13, 15, 26, 6), (17, 29, 16, 24)]
    x0 += ks[0]; x1 += ks[1]
    for i in range(5):
        for r in rot[i % 2]:
            x0 += x1
            x1 = ((x1 << np.uint32(r)) | (x1 >> np.uint32(32 - r))).astype(np.uint32)
            x1 ^= x0
        x0 += ks[(i + 1) % 3]
        x1 += ks[(i + 2) % 3] + np.uint32(i + 1)
    return x0, x1


def _dropout_masks():
    import ml_dtypes
    n = B * T * D
    idx = np.arange(n, dtype=np.uint64)
    c1 = (idx >> np.uint64(32)).astype(np.uint32)
    c2 = (idx & np.uint64(0xFFFFFFFF)).astype(np.uint32)
    b1, b2 = _np_threefry2x32(0, 42, c1, c2)
    if os.environ.get("BASS_SIM_FP64") == "1":
        bits = (b1.astype(np.uint64) << np.uint64(32)) | b2.astype(np.uint64)
        u = ((bits >> np.uint64(12)) | np.uint64(0x3FF0000000000000)).view(np.float64) - 1.0
        keep = (u < 0.8).reshape(B, T, D)
    else:
        bits = b1 ^ b2
        u = (((bits >> np.uint32(9)) | np.uint32(0x3F800000)).view(np.float32)
             - np.float32(1.0))
        keep = (u < np.float32(0.8)).reshape(B, T, D)
    scale = np.where(keep, 1.25, 0.0).astype(ml_dtypes.bfloat16)
    return [np.ascontiguousarray(scale[b].T) for b in range(B)]


def kernel(**inputs):
    x = np.asarray(inputs["L0_attn_x"], np.float32)
    shared = stage_inputs(inputs)
    drops = _dropout_masks()
    nc = build_program()
    in_maps = []
    for b in range(B):
        m = dict(shared)
        m["x_fm"] = np.ascontiguousarray(x[b].T)
        m["dropscale"] = drops[b]
        in_maps.append(m)
    res = run_bass_kernel_spmd(nc, in_maps, core_ids=list(range(B)))
    out = np.empty((B, T, D), dtype=np.float32)
    for b in range(B):
        out[b] = res.results[b]["out_fm"].T
    return out


# version 19
# speedup vs baseline: 1.0175x; 1.0175x over previous
import os
import math
import contextlib
import numpy as np
import concourse.bass as bass
import concourse.mybir as mybir
from concourse.bass_utils import run_bass_kernel_spmd

F32 = mybir.dt.float32
BF16 = mybir.dt.bfloat16
FP8 = mybir.dt.float8e5
ALU = mybir.AluOpType
ACT = mybir.ActivationFunctionType
AX = mybir.AxisListType

B, T, D = 8, 2048, 1024
NDT = 8
CK = 128
LN2 = math.log(2.0)

ENGS = ("sync", "tensor", "vector", "scalar", "gpsimd")
ACT_CONSTS = [LN2, -LN2, 2 * LN2, -2 * LN2, 7 * LN2, -7 * LN2, 1e-8, -math.log(16.0), LN2 / 2.0, 7 * LN2 / 2.0,
              -1.0, 0.5, 2.0, 1e-6, 1e-5, 1e-12, 0.03125, 32.0, 0.0, 1.0]


class Sem:
    __slots__ = ("name", "h", "n")

    def __init__(self, name):
        self.name, self.h, self.n = name, None, 0


class KB:
    def __init__(self):
        self.q = {e: [] for e in ENGS}
        self.sems = []

    def sem(self, name):
        s = Sem(name)
        self.sems.append(s)
        return s

    def raw(self, eng, fn):
        self.q[eng].append(fn)

    def op(self, eng, fn, sem=None, d=1):
        if sem is None:
            self.q[eng].append(fn)
            return None
        sem.n += d
        def run(e, fn=fn, sem=sem, d=d):
            fn(e).then_inc(sem.h, d)
        self.q[eng].append(run)
        return sem.n

    def dma(self, eng, out, in_, sem, accum=None):
        kw = {}
        if accum is not None:
            kw["accum_op"] = accum
        return self.op(eng, lambda e, o=out, i=in_, kw=kw: e.dma_start(out=o, in_=i, **kw),
                       sem=sem, d=16)

    def wait(self, eng, sem, val):
        self.maxwait = getattr(self, "maxwait", {})
        self.maxwait[sem.name] = max(self.maxwait.get(sem.name, 0), val)
        self.q[eng].append(lambda e, s=sem, v=val: e.wait_ge(s.h, v))

    def drain(self, eng):
        self.q[eng].append(lambda e: e.drain())

    def inc(self, eng, sem, d=1):
        sem.n += d
        def run(e, s=sem, d=d):
            e.drain()
            e.sem_inc(s.h, d)
        self.q[eng].append(run)
        return sem.n


class NS:
    pass


def ts_op(k, eng, out, in_, s1, op0, s2=None, op1=None, sem=None, accum=None):
    def fn(e, out=out, in_=in_, s1=s1, s2=s2, op0=op0, op1=op1, accum=accum):
        kw = {}
        if accum is not None:
            kw["accum_out"] = accum
        if op1 is not None:
            return e.tensor_scalar(out, in_, s1, s2, op0, op1, **kw)
        return e.tensor_scalar(out, in_, s1, None, op0, **kw)
    return k.op(eng, fn, sem=sem)


def tt_op(k, eng, out, a, b, op, sem=None):
    return k.op(eng, lambda e, out=out, a=a, b=b, op=op: e.tensor_tensor(out, a, b, op),
                sem=sem)


def stt_op(k, eng, out, in0, scalar, op0, in1, op1, sem=None):
    return k.op(eng, lambda e, out=out, in0=in0, s=scalar, op0=op0, in1=in1, op1=op1:
                e.scalar_tensor_tensor(out, in0, s, in1, op0, op1), sem=sem)


def act_op(k, out, in_, func, bias=None, scale=None, accum=None, sem=None):
    def fn(e, out=out, in_=in_, func=func, bias=bias, scale=scale, accum=accum):
        kw = {}
        if bias is not None:
            kw["bias"] = bias
        if scale is not None:
            kw["scale"] = scale
        if accum is not None:
            kw["accum_out"] = accum
        return e.activation(out, in_, func, **kw)
    return k.op("scalar", fn, sem=sem)


def copy_op(k, eng, out, in_, sem=None):
    if eng == "scalar":
        return k.op(eng, lambda e, o=out, i=in_: e.copy(o, i), sem=sem)
    return k.op(eng, lambda e, o=out, i=in_: e.tensor_copy(o, i), sem=sem)


def mm(k, out, lhsT, rhs, start, stop, sem=None):
    return k.op("tensor",
                lambda e, o=out, l=lhsT, r=rhs, st=start, sp=stop:
                e.matmul(o, l, r, start=st, stop=sp), sem=sem)


def tr_op(k, out, in_, ident, sem=None):
    return k.op("tensor", lambda e, o=out, i=in_, d=ident: e.transpose(o, i, d), sem=sem)



def emit_magic(k, c):
    C = c.cols[:, 68:69]
    Tm = c.cols[:, 69:70]
    k.raw("vector", lambda e: e.memset(C, 1.0))
    k.drain("vector")
    for _ in range(64):
        ts_op(k, "vector", Tm, C, 1.0, ALU.add)
        k.drain("vector")
        tt_op(k, "vector", Tm, Tm, C, ALU.subtract)
        k.drain("vector")
        ts_op(k, "vector", Tm, Tm, 0.5, ALU.is_lt, -1.0, ALU.mult)
        k.drain("vector")
        ts_op(k, "vector", Tm, Tm, 2.0, ALU.add)
        k.drain("vector")
        tt_op(k, "vector", C, C, Tm, ALU.mult)
        k.drain("vector")
    ts_op(k, "vector", C, C, 0.75, ALU.mult)
    k.drain("vector")
    c.tok_magic = k.inc("vector", c.sVE)


def q_chain(k, c, eng, out, in_, M, emax_neg, W, ties_even=None, relu_bias=None,
            bias_col=None, nonneg=False, sem=None, first_sem=None,
            first_on_scalar=False):
    semE = c.sVE if eng == "vector" else c.sGP
    MAG = c.cols[:, 68:69]
    k.drain(eng)
    if first_on_scalar:
        if relu_bias is not None:
            c.first_tok = act_op(k, W(0), in_, ACT.Relu, bias=relu_bias,
                                 sem=first_sem)
            nonneg = True
        elif bias_col is not None:
            c.first_tok = act_op(k, W(0), in_, ACT.Copy, bias=bias_col,
                                 sem=first_sem)
        else:
            c.first_tok = copy_op(k, "scalar", W(0), in_, sem=first_sem)
        tfs = k.inc("scalar", c.sSC)
        k.wait(eng, c.sSC, tfs)
    elif relu_bias is not None:
        c.first_tok = ts_op(k, eng, W(0), in_, relu_bias, ALU.add, 0.0, ALU.max,
                            sem=first_sem)
        nonneg = True
    elif bias_col is not None:
        c.first_tok = ts_op(k, eng, W(0), in_, bias_col, ALU.add, sem=first_sem)
    else:
        c.first_tok = copy_op(k, eng, W(0), in_, sem=first_sem)
    k.drain(eng)
    signed = not nonneg
    if signed:
        t0e = k.inc(eng, semE)
        k.wait("scalar", semE, t0e)
        act_op(k, W(5), W(0), ACT.Sign)
        ta = act_op(k, W(1), W(0), ACT.Abs, sem=c.sSC)
        k.wait(eng, c.sSC, ta)
        ts_op(k, eng, W(1), W(1), 1e-38, ALU.max)
    else:
        ts_op(k, eng, W(1), W(0), 1e-38, ALU.max)
    t1 = k.inc(eng, semE)
    k.wait("scalar", semE, t1)
    t2 = act_op(k, W(2), W(1), ACT.Ln, sem=c.sSC)
    k.wait(eng, c.sSC, t2)
    ts_op(k, eng, W(2), W(2), -1.0 / LN2, ALU.mult, 0.5, ALU.add)
    k.drain(eng)
    ts_op(k, eng, W(2), W(2), MAG, ALU.add)
    k.drain(eng)
    ts_op(k, eng, W(2), W(2), MAG, ALU.subtract)
    k.drain(eng)
    ts_op(k, eng, W(2), W(2), float(emax_neg), ALU.min)
    t3 = k.inc(eng, semE)
    k.wait("scalar", semE, t3)
    split = emax_neg > 100
    if split:
        act_op(k, W(3), W(2), ACT.Exp, scale=LN2 / 2.0, bias=float(M) * LN2 / 2.0)
    else:
        act_op(k, W(3), W(2), ACT.Exp, scale=LN2, bias=float(M) * LN2)
    t4 = act_op(k, W(4), W(2), ACT.Exp, scale=-LN2, bias=-float(M) * LN2,
                sem=c.sSC)
    k.wait(eng, c.sSC, t4)
    tt_op(k, eng, W(2), W(1), W(3), ALU.mult)
    k.drain(eng)
    if split:
        tt_op(k, eng, W(2), W(2), W(3), ALU.mult)
        k.drain(eng)
    ts_op(k, eng, W(2), W(2), MAG, ALU.add)
    k.drain(eng)
    ts_op(k, eng, W(2), W(2), MAG, ALU.subtract)
    k.drain(eng)
    tt_op(k, eng, W(2), W(2), W(4), ALU.mult)
    k.drain(eng)
    if signed:
        tok = tt_op(k, eng, out, W(2), W(5), ALU.mult, sem=sem)
    else:
        tok = copy_op(k, eng, out, W(2), sem=sem)
    k.drain(eng)
    return tok



def build_program(debug=False, probe=False):
    nc = bass.Bass("TRN2", target_bir_lowering=False)
    k = KB()
    c = NS()
    c.nc = nc
    c.debug = debug

    def din(name, shape, dt=F32):
        return nc.dram_tensor(name, list(shape), dt, kind="ExternalInput").ap()

    c.x_fm = din("x_fm", (D, T))
    c.out_fm = nc.dram_tensor("out_fm", [D, T], F32, kind="ExternalOutput").ap()
    if debug:
        c.dbg = {}
        for nm in ("dbg_x1", "dbg_s0", "dbg_xn0", "dbg_spre",
                   "dbg_xn1", "dbg_s1", "dbg_x2",
                   "dbg_xn2", "dbg_s2", "dbg_x3",
                   "dbg_xn3", "dbg_s3", "dbg_x4"):
            c.dbg[nm] = nc.dram_tensor(nm, [D, T], F32, kind="ExternalOutput").ap()

    c.IDT = din("c_ident", (16, 16))
    c.IDTB = din("c_identb", (128, 128), BF16)
    c.ONESROW = din("c_onesrow", (65, 128))
    c.ONESCOLB = din("c_onescolb", (128, 1), BF16)
    c.MBIG = din("c_mbig", (128, 128))
    c.REPL2 = din("c_repl2", (16, 8 * 128))
    c.ACTC = din("c_actc", (128, 20))
    c.DROP = din("dropscale", (D, T), BF16)

    c.L0WqT = din("L0WqT", (D, D)); c.L0WkT = din("L0WkT", (D, D))
    c.L0WvT = din("L0WvT", (D, D)); c.L0WoT = din("L0WoT", (D, D))
    c.L0WwT = din("L0WwT", (D, 16)); c.L0WeT = din("L0WeT", (D, 16))
    c.L0bw = din("L0bw", (16, 1)); c.L0be = din("L0be", (16, 1))
    c.L0W1T = din("L0W1T", (D, 4096)); c.L0W2T = din("L0W2T", (4096, D))
    c.L0b1 = din("L0b1", (128, 32)); c.L0b2 = din("L0b2", (128, 8))
    c.L1Wq = din("L1Wq", (D, D)); c.L1Wk = din("L1Wk", (D, D)); c.L1Wv = din("L1Wv", (D, D))
    c.OMGT2 = din("c_omgt2", (128, 256), BF16)
    c.NEGH = din("c_negh", (128, 1))
    c.L1rw = din("L1rw", (D, 8))
    c.L1w1 = din("L1w1", (8, D, 2816)); c.L1v1 = din("L1v1", (8, D, 2816))
    c.L1w2 = din("L1w2", (8, 2816, D)); c.L1bias = din("L1bias", (128, 8))
    c.L2WqT = din("L2WqT", (D, D)); c.L2WkT = din("L2WkT", (D, D))
    c.L2WvT = din("L2WvT", (D, D)); c.L2WuT = din("L2WuT", (D, D))
    c.L2se = din("L2se", (4096, D))
    c.L2w1 = din("L2w1", (D, 4096)); c.L2w2 = din("L2w2", (4096, D))
    c.L2gx = din("L2gx", (128, 8)); c.L2bu = din("L2bu", (128, 8))
    c.L2gs = din("L2gs", (1, D))
    c.IDTF = din("c_identf", (128, 128))
    c.L3WinT = din("L3WinT", (D, 4096)); c.L3WoutT = din("L3WoutT", (2048, D))
    c.L3WxT = din("L3WxT", (2048, 96)); c.L3WdtT = din("L3WdtT", (64, 2048))
    c.L3cw = din("L3cw", (128, 64)); c.L3cb = din("L3cb", (128, 16))
    c.L3D = din("L3D", (128, 16)); c.L3AL = din("L3AL", (128, 256))
    c.L3bdt = din("L3bdt", (128, 16))
    c.L3gwT = din("L3gwT", (D, 32))
    c.L3egT = din("L3egT", (32, D, 512)); c.L3euT = din("L3euT", (32, D, 512))
    c.L3edT = din("L3edT", (32, 512, D))
    c.L3sgT = din("L3sgT", (D, 512)); c.L3suT = din("L3suT", (D, 512))
    c.L3sdT = din("L3sdT", (512, D))

    c.qW1s = nc.dram_tensor("qW1s", [32, 128, 1024], FP8, kind="Internal").ap()
    c.qW2s = nc.dram_tensor("qW2s", [8, 128, 4096], FP8, kind="Internal").ap()
    c.WbfA = nc.dram_tensor("WbfA", [4, D, D], BF16, kind="Internal").ap()
    c.qxs = nc.dram_tensor("qxs", [NDT, 128, T], FP8, kind="Internal").ap()
    c.L1wb = nc.dram_tensor("L1wb", [8, 2, D, 2816], BF16, kind="Internal").ap()
    c.L1w2b = nc.dram_tensor("L1w2b", [8, 2816, D], BF16, kind="Internal").ap()
    c.ctrd = nc.dram_tensor("ctrd", [16, 8, 128], F32, kind="Internal").ap()
    c.qfms = nc.dram_tensor("qfms", [D, T], BF16, kind="Internal").ap()
    c.kfms = nc.dram_tensor("kfms", [D, T], BF16, kind="Internal").ap()
    c.vtms = nc.dram_tensor("vtms", [T, D], BF16, kind="Internal").ap()
    c.yacc = nc.dram_tensor("yacc", [8, 128, T], F32, kind="Internal").ap()
    c.l2w1b = nc.dram_tensor("l2w1b", [D, 4096], BF16, kind="Internal").ap()
    c.l2w2b = nc.dram_tensor("l2w2b", [4096, D], BF16, kind="Internal").ap()
    c.senTb = nc.dram_tensor("senTb", [D, 4096], BF16, kind="Internal").ap()
    c.slotsTb = nc.dram_tensor("slotsTb", [D, 4096], BF16, kind="Internal").ap()
    c.procb = nc.dram_tensor("procb", [4096, D], BF16, kind="Internal").ap()
    c.expb = nc.dram_tensor("expb", [4096, T], BF16, kind="Internal").ap()
    dk3 = "ExternalOutput" if (debug or probe) else "Internal"
    c.dbg_dt = nc.dram_tensor("dbg_dt", [2048, T], F32, kind=dk3).ap()
    c.dbg_da = nc.dram_tensor("dbg_da", [2048, T], F32, kind=dk3).ap()
    c.dbg_ys = nc.dram_tensor("dbg_ys", [2048, T], F32, kind=dk3).ap()
    c.uzb = nc.dram_tensor("uzb", [4096, T], BF16, kind=dk3).ap()
    c.ucfb = nc.dram_tensor("ucfb", [2048, T], BF16, kind=dk3).ap()
    c.y3b = nc.dram_tensor("y3b", [2048, T], BF16, kind=dk3).ap()
    c.bcd = nc.dram_tensor("bcd", [32, T], F32, kind=dk3).ap()
    c.l3wi = nc.dram_tensor("l3wi", [D, 4096], BF16, kind="Internal").ap()
    c.l3wo = nc.dram_tensor("l3wo", [2048, D], BF16, kind="Internal").ap()
    c.l3wx = nc.dram_tensor("l3wx", [2048, 96], BF16, kind="Internal").ap()
    c.l3wdt = nc.dram_tensor("l3wdt", [64, 2048], BF16, kind="Internal").ap()
    c.l3eg = nc.dram_tensor("l3eg", [32, D, 512], BF16, kind="Internal").ap()
    c.l3eu = nc.dram_tensor("l3eu", [32, D, 512], BF16, kind="Internal").ap()
    c.l3ed = nc.dram_tensor("l3ed", [32, 512, D], BF16, kind="Internal").ap()
    c.l3sg = nc.dram_tensor("l3sg", [D, 512], BF16, kind="Internal").ap()
    c.l3su = nc.dram_tensor("l3su", [D, 512], BF16, kind="Internal").ap()
    c.l3sd = nc.dram_tensor("l3sd", [512, D], BF16, kind="Internal").ap()
    c.l3gw = nc.dram_tensor("l3gw", [D, 32], BF16, kind="Internal").ap()
    c.ctr32 = nc.dram_tensor("ctr32", [16, 32, 128], F32, kind="Internal").ap()

    with contextlib.ExitStack() as st:
        def sb(name, shape, dt=F32):
            return st.enter_context(nc.sbuf_tensor(name, list(shape), dt))

        c.sb = sb
        c.xres = sb("xres", (128, NDT * T))
        c.xnbf = sb("xnbf", (128, NDT * T), BF16)
        c.sres = sb("sres", (128, NDT * T), BF16)
        c.aQ = sb("aQ", (128, NDT * 256), BF16)
        c.aK = sb("aK", (128, NDT * 256), BF16)
        c.aKtm = sb("aKtm", (128, 1024), BF16)
        c.aVtm = sb("aVtm", (128, 2 * 16 * 65), BF16)
        c.F32 = sb("f32ws", (128, 2304))
        c.G32 = sb("g32ws", (128, 2048))
        c.wbf = [sb(f"wbf{i}", (128, 1024), BF16) for i in range(2)]
        c.rows = sb("rows", (1, 1024))
        c.Tst = sb("Tst", (128, 8 * 65))
        c.cols = sb("cols", (128, 70))
        c.cIDT = sb("cIDT", (16, 16))
        c.cIDTB = sb("cIDTB", (128, 128), BF16)
        c.cONESROW = sb("cONESROW", (65, 128))
        c.cONESCOLB = sb("cONESCOLB", (128, 1), BF16)
        c.cACTC = sb("cACTC", (128, 20))
        c.bfS = sb("bfS", (128, 384), BF16)
        c.bfT = sb("bfT", (128, 256), BF16)
        c.st0 = contextlib.ExitStack()
        def sb0(name, shape, dt=F32):
            return c.st0.enter_context(nc.sbuf_tensor(name, list(shape), dt))
        c.qxb = sb0("qxb", (128, 2048), FP8)
        c.hq = sb0("hq", (128, 32 * 256), FP8)
        c.w8 = [sb0(f"w8{i}", (128, 1024), FP8) for i in range(2)]
        c.gates = sb0("gates", (16, T))
        c.wgbf = sb0("wgbf", (16, T), BF16)
        c.crows = sb0("crows", (1, 512))
        c.sm16 = sb0("sm16", (16, 416))
        c.efr = sb0("efr", (128, 16))
        c.kdT = sb0("kdT", (128, 32))
        c.cMBIG = sb0("cMBIG", (128, 128))
        c.cREPL2 = sb0("cREPL2", (16, 8 * 128))

        c.ps = [st.enter_context(nc.psum_tensor(f"ps{i}", [128, 512], F32))
                for i in range(7)]
        c.ps7b = st.enter_context(nc.psum_tensor("ps7b", [128, 512], BF16))

        for i, v in enumerate(ACT_CONSTS):
            nc.const_aps.register(F32, v, c.cACTC[:, i:i + 1])
        for nm in ("sX", "sW", "sMM", "sVE", "sSC", "sGP", "sD2", "sD3", "sOUT"):
            setattr(c, nm, k.sem(nm))
        for s in k.sems:
            s.h = st.enter_context(nc.semaphore(s.name))

        plan(k, c)
        for s in k.sems:
            mw = getattr(k, "maxwait", {}).get(s.name, 0)
            assert mw <= s.n, f"sem {s.name}: waits up to {mw} but only {s.n} incs"

        with nc.Block() as blk:
            for eng in ENGS:
                getattr(blk, eng)(lambda e, q=list(k.q[eng]): [f(e) for f in q])

    return nc


def wait_free(k, c, eng, key):
    ent = c.bank_free.get(key)
    if ent:
        sem, tok = ent
        k.wait(eng, getattr(c, sem), tok)


def plan(k, c):
    c.bank_free = {}
    for dt in range(NDT):
        k.dma("sync", c.xres[:, dt * T:(dt + 1) * T], c.x_fm[dt * 128:(dt + 1) * 128, :], c.sX)
    for dst, src in ((c.cIDT, c.IDT), (c.cIDTB, c.IDTB), (c.cONESROW, c.ONESROW),
                     (c.cONESCOLB, c.ONESCOLB), (c.cMBIG, c.MBIG), (c.cREPL2, c.REPL2),
                     (c.cACTC, c.ACTC)):
        k.dma("sync", dst[:], src, c.sX)
    tX = c.sX.n
    k.wait("scalar", c.sX, tX)

    k.raw("vector", lambda e: e.memset(
        c.aVtm[:].rearrange("p (j v) -> p j v", v=65)[:, :, 64:65], 1.0))
    k.raw("vector", lambda e: e.memset(c.cols[:, 67:68], 1.0))
    k.drain("vector")
    emit_magic(k, c)
    k.wait("gpsimd", c.sVE, c.tok_magic)
    k.wait("scalar", c.sVE, c.tok_magic)
    gen = l0_weight_prep_gen(k, c)
    c.pump = lambda n=1: [next(gen, None) for _ in range(n)]
    c.pump(2)
    layernorm(k, c, c.sX, tX, make_qx=True)
    if c.debug:
        dump_dbg(k, c, "dbg_xn0", src="xnbf")
    l0_attn(k, c)
    if c.debug:
        dump_dbg(k, c, "dbg_s0", src="sres")
    c.pump(1000)
    l0_ffn_combine(k, c)
    if c.debug:
        dump_dbg(k, c, "dbg_x1", src="xres")

    c.st0.close()
    c.wbg = [c.sb(f"wbg{i}", (128, 1024), BF16) for i in range(2)]
    c.colsB = c.sb("colsB", (128, 32))
    c.crow5 = c.sb("crow5", (1, 512))
    c.cIDF = c.sb("cIDF", (128, 128))
    c.st1 = contextlib.ExitStack()
    def sb1(name, shape, dt=F32):
        return c.st1.enter_context(c.nc.sbuf_tensor(name, list(shape), dt))
    c.hA = sb1("hA", (128, 6144), BF16)
    c.cOMGT2 = sb1("cOMGT2", (128, 256), BF16)
    c.cNEGH = sb1("cNEGH", (128, 1))
    c.rwb = sb1("rwb", (128, 64), BF16)
    gen1 = l1_cast_gen(k, c)
    c.pump = lambda n=1: [next(gen1, None) for _ in range(n)]
    layernorm(k, c, c.sVE, c.tok_l0)
    if c.debug:
        dump_dbg(k, c, "dbg_xn1", src="xnbf")
    proj_qkv(k, c, "tok_l1qkv", l1_consts=True)
    l1_attn_heads(k, c)
    if c.debug:
        dump_dbg(k, c, "dbg_s1", src="sres")
    l1_router(k, c)
    l1_ffn_combine(k, c)
    if c.debug:
        dump_dbg(k, c, "dbg_x2", src="xres")

    gen2 = l2_cast_gen(k, c)
    gen3c = l3_cast_gen(k, c)
    def pump_l2l3(n=1):
        for _ in range(n):
            if not hasattr(c, "tok_l2w"):
                next(gen2, None)
            else:
                next(gen3c, None)
    c.pump = pump_l2l3
    layernorm(k, c, c.sVE, c.tok_l1)
    if c.debug:
        dump_dbg(k, c, "dbg_xn2", src="xnbf")
    proj_qkv(k, c, "tok_l2qkv")
    gen2s = l2_sen_gen(k, c)
    c.pump2 = lambda n=1: [next(gen2s, None) for _ in range(n)]
    c.pump2(1)
    l2_attn_heads(k, c)
    if c.debug:
        dump_dbg(k, c, "dbg_s2", src="sres")
    l2_xnr(k, c)
    l2_slot_ffn(k, c)
    if c.debug:
        dump_dbg(k, c, "dbg_x3", src="xres")

    c.st1.close()
    c.M32 = c.sb("M32", (128, 1025))
    c.MF = c.sb("MF", (128, 1536))
    c.hC = c.sb("hC", (128, 2048), BF16)
    c.brow = c.sb("brow", (1, 1024))
    c.crow = c.sb("crow", (1, 1024))
    c.cA = c.sb("cA", (128, 32))
    c.colsC = c.sb("colsC", (128, 96))
    c.mwb = c.sb("mwb", (128, 224), BF16)
    c.gwb = c.sb("gwb", (128, 256), BF16)
    layernorm(k, c, c.sVE, c.tok_l2)
    if c.debug:
        dump_dbg(k, c, "dbg_xn3", src="xnbf")
    l3_pre(k, c)
    l3_router(k, c)
    gen3m = l3_mamba_gen(k, c)
    c.pump2 = lambda n=1: [next(gen3m, None) for _ in range(n)]
    l3_moe(k, c)
    l3_wout_combine(k, c)
    if c.debug:
        dump_dbg(k, c, "dbg_s3", src="sres")
        dump_dbg(k, c, "dbg_x4", src="xres")

    k.wait("gpsimd", c.sVE, c.sVE.n)
    k.wait("gpsimd", c.sSC, c.sSC.n)
    k.wait("gpsimd", c.sMM, c.sMM.n)
    for dt in range(NDT):
        k.dma("gpsimd", c.out_fm[dt * 128:(dt + 1) * 128, :],
              c.xres[:, dt * T:(dt + 1) * T], c.sOUT)
    k.wait("gpsimd", c.sOUT, c.sOUT.n)


def dump_dbg(k, c, name, src):
    ten = getattr(c, src)
    k.wait("gpsimd", c.sVE, c.sVE.n)
    k.wait("gpsimd", c.sSC, c.sSC.n)
    k.wait("gpsimd", c.sMM, c.sMM.n)
    k.wait("gpsimd", c.sD3, c.sD3.n)
    k.wait("sync", c.sW, c.sW.n)
    k.wait("gpsimd", c.sW, c.sW.n)
    if ten.dtype != F32:
        for dt in range(NDT):
            k.drain("gpsimd")
            t = copy_op(k, "gpsimd", c.F32[:, 0:2048], ten[:, dt * T:(dt + 1) * T], sem=c.sGP)
            k.wait("gpsimd", c.sGP, t)
            td = k.dma("gpsimd", c.dbg[name][dt * 128:(dt + 1) * 128, :], c.F32[:, 0:2048], c.sD3)
            k.wait("gpsimd", c.sD3, td)
    else:
        for dt in range(NDT):
            k.dma("gpsimd", c.dbg[name][dt * 128:(dt + 1) * 128, :],
                  ten[:, dt * T:(dt + 1) * T], c.sD3)
        k.wait("gpsimd", c.sD3, c.sD3.n)
    tokd = k.inc("gpsimd", c.sGP)
    for eng in ("vector", "scalar", "tensor", "sync"):
        k.wait(eng, c.sGP, tokd)



def l0_weight_prep_gen(k, c):
    WG = lambda i: c.G32[:, i * 256:(i + 1) * 256]
    WS = lambda i: c.G32[:, i * 40:(i + 1) * 40]
    t = k.dma("gpsimd", c.G32[:, 1792:1824], c.L0b1, c.sD2)
    t = k.dma("gpsimd", c.G32[:, 1824:1832], c.L0b2, c.sD2)
    k.wait("gpsimd", c.sD2, t)
    q_chain(k, c, "gpsimd", c.cols[:, 0:32], c.G32[:, 1792:1824], 2, 14,
            lambda i: c.G32[:, i * 32:(i + 1) * 32], ties_even=False, sem=c.sGP)
    q_chain(k, c, "gpsimd", c.cols[:, 32:40], c.G32[:, 1824:1832], 2, 14,
            lambda i: c.G32[:, i * 8:(i + 1) * 8], ties_even=False, sem=c.sGP)
    c.tok_bq = c.sGP.n
    yield
    for wi, wsrc in enumerate((c.L0WqT, c.L0WkT, c.L0WvT, c.L0WoT)):
        td = k.dma("gpsimd", c.WbfA[wi], wsrc, c.sD2)
        yield
    k.wait("gpsimd", c.sD2, td)
    c.tok_wcast = c.sD2.n
    yield
    WGS = lambda i: c.G32[:, i * 256:(i + 1) * 256]
    for kt in range(8):
        for fc4 in range(4):
            for sub in range(4):
                fc = fc4 * 4 + sub
                t = k.dma("gpsimd", c.G32[:, 1792:2048],
                          c.L0W1T[kt * 128:(kt + 1) * 128, fc * 256:(fc + 1) * 256], c.sD2)
                k.wait("gpsimd", c.sD2, t)
                q_chain(k, c, "gpsimd", c.w8[0][:, 0:256], c.G32[:, 1792:2048], 2, 14,
                        WGS, ties_even=False, sem=c.sGP)
                k.dma("gpsimd", c.qW1s[2 * fc, :, kt * 128:(kt + 1) * 128],
                      c.w8[0][:, 0:128], c.sD2)
                td = k.dma("gpsimd", c.qW1s[2 * fc + 1, :, kt * 128:(kt + 1) * 128],
                           c.w8[0][:, 128:256], c.sD2)
                k.wait("gpsimd", c.sD2, td)
            yield
    for ft in range(32):
        for oc in range(4):
            t = k.dma("gpsimd", c.G32[:, 1792:2048],
                      c.L0W2T[ft * 128:(ft + 1) * 128, oc * 256:(oc + 1) * 256], c.sD2)
            k.wait("gpsimd", c.sD2, t)
            q_chain(k, c, "gpsimd", c.w8[0][:, 0:256], c.G32[:, 1792:2048], 2, 14,
                    WGS, ties_even=False, sem=c.sGP)
            k.dma("gpsimd", c.qW2s[2 * oc, :, ft * 128:(ft + 1) * 128],
                  c.w8[0][:, 0:128], c.sD2)
            td = k.dma("gpsimd", c.qW2s[2 * oc + 1, :, ft * 128:(ft + 1) * 128],
                       c.w8[0][:, 128:256], c.sD2)
            k.wait("gpsimd", c.sD2, td)
        yield
    c.tok_wprep = c.sD2.n
    while True:
        yield



def layernorm(k, c, sem, tX, make_qx=False):
    k.wait("vector", sem, tX)
    pump = getattr(c, "pump", lambda n=1: None)
    k.raw("vector", lambda e: e.memset(c.cols[:, 66:67], 1.0 / D))
    tones = k.inc("vector", c.sVE)
    k.wait("tensor", c.sVE, tones)
    k.wait("tensor", sem, tX)
    k.wait("scalar", sem, tX)
    W = lambda i: c.F32[:, i * 256:(i + 1) * 256]
    for tc in range(4):
        for dt in range(NDT):
            mm(k, c.ps[0][0:1, :], c.cols[:, 66:67],
               c.xres[:, dt * T + tc * 512: dt * T + (tc + 1) * 512],
               start=(dt == 0), stop=(dt == NDT - 1))
        k.inc("tensor", c.sMM)
        buf = c.F32[:, 1792:2304]
        for dt in range(NDT):
            tsq = act_op(k, buf, c.xres[:, dt * T + tc * 512: dt * T + (tc + 1) * 512],
                         ACT.Square, sem=c.sSC)
            k.wait("tensor", c.sSC, tsq)
            mm(k, c.ps[1][0:1, :], c.cols[:, 66:67], buf,
               start=(dt == 0), stop=(dt == NDT - 1))
            tm2 = k.inc("tensor", c.sMM)
            if dt < NDT - 1:
                k.wait("scalar", c.sMM, tm2)
        tmm2 = c.sMM.n
        k.wait("vector", c.sMM, tmm2)
        ts_op(k, "vector", c.rows[0:1, 0:512], c.ps[0][0:1, :], 1.0, ALU.mult)
        k.drain("vector")
        tt_op(k, "vector", c.rows[0:1, 512:1024], c.rows[0:1, 0:512],
              c.rows[0:1, 0:512], ALU.mult)
        k.drain("vector")
        stt_op(k, "vector", c.rows[0:1, 512:1024], c.ps[1][0:1, :], 1.0, ALU.mult,
               c.rows[0:1, 512:1024], ALU.subtract)
        k.drain("vector")
        tmu = ts_op(k, "vector", c.rows[0:1, 512:1024], c.rows[0:1, 512:1024],
                    1e-5, ALU.add, sem=c.sVE)
        k.wait("tensor", c.sVE, tmu)
        mm(k, c.ps[2][:, :], c.cONESROW[0:1, :], c.rows[0:1, 0:512], start=True, stop=True)
        tmr = k.inc("tensor", c.sMM)
        k.wait("vector", c.sMM, tmr)
        k.raw("vector", lambda e: e.reciprocal(c.rows[0:1, 0:512], c.rows[0:1, 512:1024]))
        trec = k.inc("vector", c.sVE)
        k.wait("scalar", c.sVE, trec)
        tsq2 = act_op(k, c.rows[0:1, 512:1024], c.rows[0:1, 0:512], ACT.Sqrt, sem=c.sSC)
        k.wait("tensor", c.sSC, tsq2)
        mm(k, c.ps[3][:, :], c.cONESROW[0:1, :], c.rows[0:1, 512:1024], start=True, stop=True)
        trep = k.inc("tensor", c.sMM)
        k.wait("vector", c.sMM, trep)
        for dt in range(NDT):
            sl = slice(dt * T + tc * 512, dt * T + (tc + 1) * 512)
            xn = c.F32[:, 1792:2304]
            tt_op(k, "vector", xn, c.xres[:, sl], c.ps[2][:, :], ALU.subtract)
            k.drain("vector")
            tv = tt_op(k, "vector", xn, xn, c.ps[3][:, :], ALU.mult, sem=c.sVE)
            k.wait("scalar", c.sVE, tv)
            tb = copy_op(k, "scalar", c.xnbf[:, sl], xn, sem=c.sSC)
            if not make_qx:
                k.drain("scalar")
            if make_qx:
                for half in range(2):
                    hs = slice(1792 + half * 256, 1792 + (half + 1) * 256)
                    tq = q_chain(k, c, "vector", c.qxb[:, half * 256:(half + 1) * 256],
                                 c.F32[:, hs], 2, 14, W, ties_even=False, sem=c.sVE)
                k.wait("sync", c.sVE, tq)
                k.wait("sync", c.sSC, tb)
                tqd = k.dma("sync", c.qxs[dt, :, tc * 512:(tc + 1) * 512],
                            c.qxb[:, 0:512], c.sW)
                k.wait("sync", c.sW, tqd)
                k.wait("vector", c.sW, tqd)
            k.wait("vector", c.sSC, tb)
        tdone = k.inc("vector", c.sVE)
        k.wait("tensor", c.sVE, tdone)
        pump(3)
    c.tok_ln = k.inc("vector", c.sVE)
    c.tok_lnsc = k.inc("scalar", c.sSC)



def l0_attn(k, c):
    k.wait("sync", c.sVE, c.tok_ln)
    for dt in range(NDT):
        k.dma("sync", c.F32[:, 1024 + dt * 32:1024 + dt * 32 + 16],
              c.L0WwT[dt * 128:(dt + 1) * 128, :], c.sW)
        k.dma("sync", c.F32[:, 1024 + dt * 32 + 16:1024 + dt * 32 + 32],
              c.L0WeT[dt * 128:(dt + 1) * 128, :], c.sW)
    k.dma("sync", c.cols[0:16, 64:65], c.L0bw, c.sW)
    tW = k.dma("sync", c.cols[0:16, 65:66], c.L0be, c.sW)
    k.wait("sync", c.sW, tW)
    k.wait("scalar", c.sW, tW)
    k.wait("scalar", c.sSC, c.tok_lnsc)
    tgw = copy_op(k, "scalar", c.bfS[:, 0:256], c.F32[:, 1024:1280], sem=c.sSC)
    k.wait("tensor", c.sSC, tgw)
    k.wait("tensor", c.sVE, c.tok_ln)
    for tc in range(4):
        for dt in range(NDT):
            sl = slice(dt * T + tc * 512, dt * T + (tc + 1) * 512)
            mm(k, c.ps[4][0:16, :], c.bfS[:, dt * 32:dt * 32 + 16], c.xnbf[:, sl],
               start=(dt == 0), stop=(dt == NDT - 1))
            mm(k, c.ps[5][0:16, :], c.bfS[:, dt * 32 + 16:dt * 32 + 32], c.xnbf[:, sl],
               start=(dt == 0), stop=(dt == NDT - 1))
        tg = k.inc("tensor", c.sMM)
        k.wait("scalar", c.sMM, tg)
        tsl = slice(tc * 512, (tc + 1) * 512)
        act_op(k, c.wgbf[:, tsl], c.ps[4][0:16, :], ACT.Sigmoid,
               bias=c.cols[0:16, 64:65])
        te = act_op(k, c.F32[0:16, 512:1024], c.ps[5][0:16, :], ACT.Sigmoid,
                    bias=c.cols[0:16, 65:66], sem=c.sSC)
        k.wait("vector", c.sSC, te)
        tv = ts_op(k, "vector", c.gates[0:16, tsl], c.F32[0:16, 512:1024],
                   -0.95, ALU.mult, 0.95, ALU.add, sem=c.sVE)
        k.wait("scalar", c.sVE, tv)
        tl = act_op(k, c.gates[0:16, tsl], c.gates[0:16, tsl], ACT.Ln, bias=1e-8,
                    sem=c.sSC)
        k.wait("tensor", c.sSC, tl)
        k.wait("vector", c.sSC, tl)
    k.raw("vector", lambda e: e.memset(c.F32[0:16, 0:512], 1.0))
    k.drain("vector")
    for tc in range(4):
        init = 0.0 if tc == 0 else c.gates[0:16, tc * 512 - 1:tc * 512]
        k.raw("vector", lambda e, tc=tc, init=init: e.tensor_tensor_scan(
            c.gates[0:16, tc * 512:(tc + 1) * 512], c.F32[0:16, 0:512],
            c.gates[0:16, tc * 512:(tc + 1) * 512], init, ALU.mult, ALU.add))
        k.drain("vector")
    k.raw("vector", lambda e: e.memset(c.Tst[:], 0.0))
    tzero = k.inc("vector", c.sVE)
    tlwc = tzero
    k.wait("sync", c.sVE, tlwc)
    k.wait("tensor", c.sVE, tzero)
    k.wait("tensor", c.sD2, c.tok_wcast)
    k.wait("sync", c.sD2, c.tok_wcast)
    c.crow_mm_tok = 0

    for tg in range(8):
        g0 = tg * 256
        for q4 in range(4):
            for kt in range(8):
                bsel = kt % 2
                k.dma("sync", c.wbf[bsel][:, 0:256],
                      c.WbfA[0, kt * 128:(kt + 1) * 128, q4 * 256:(q4 + 1) * 256], c.sW)
                tq2 = k.dma("sync", c.wbf[bsel][:, 256:512],
                            c.WbfA[1, kt * 128:(kt + 1) * 128, q4 * 256:(q4 + 1) * 256], c.sW)
                k.wait("sync", c.sW, tq2)
                k.wait("tensor", c.sW, tq2)
                for dq in range(2):
                    mm(k, c.ps[dq][:, 0:256], c.wbf[bsel][:, dq * 128:(dq + 1) * 128],
                       c.xnbf[:, kt * T + g0: kt * T + g0 + 256],
                       start=(kt == 0), stop=(kt == 7))
                    mm(k, c.ps[2 + dq][:, 0:256],
                       c.wbf[bsel][:, 256 + dq * 128:256 + (dq + 1) * 128],
                       c.xnbf[:, kt * T + g0: kt * T + g0 + 256],
                       start=(kt == 0), stop=(kt == 7))
                tmm = k.inc("tensor", c.sMM)
                k.wait("sync", c.sMM, tmm)
            tmm = c.sMM.n
            k.wait("vector", c.sMM, tmm)
            k.wait("scalar", c.sMM, tmm)
            for dq in range(2):
                dtq = q4 * 2 + dq
                for pbank, arena in ((dq, c.aQ), (2 + dq, c.aK)):
                    ps = c.ps[pbank]
                    ts_op(k, "vector", c.F32[:, 0:256], ps[:, 0:256], 0.0, ALU.min)
                    tmn = k.inc("vector", c.sVE)
                    k.wait("scalar", c.sVE, tmn)
                    texp = act_op(k, c.F32[:, 256:512], c.F32[:, 0:256], ACT.Exp, sem=c.sSC)
                    k.wait("vector", c.sSC, texp)
                    stt_op(k, "vector", arena[:, dtq * 256:(dtq + 1) * 256],
                           ps[:, 0:256], 0.0, ALU.max, c.F32[:, 256:512], ALU.add)
                    k.drain("vector")
            tphi = k.inc("vector", c.sVE)
            k.wait("tensor", c.sVE, tphi)
        for kt in range(8):
            bsel = kt % 2
            tvw = k.dma("sync", c.wbf[bsel][:, 0:1024],
                        c.WbfA[2, kt * 128:(kt + 1) * 128, :], c.sW)
            k.wait("sync", c.sW, tvw)
            k.wait("tensor", c.sW, tvw)
            for sub in range(2):
                xsl = c.xnbf[:, kt * T + g0 + sub * 128: kt * T + g0 + (sub + 1) * 128]
                mm(k, c.ps[sub * 2][:, :], xsl, c.wbf[bsel][:, 0:512],
                   start=(kt == 0), stop=(kt == 7))
                mm(k, c.ps[sub * 2 + 1][:, :], xsl, c.wbf[bsel][:, 512:1024],
                   start=(kt == 0), stop=(kt == 7))
            tmm = k.inc("tensor", c.sMM)
            k.wait("sync", c.sMM, tmm)
        tmm = c.sMM.n
        k.wait("scalar", c.sMM, tmm)
        for sub in range(2):
            for hb in range(2):
                vbase = sub * 1040 + hb * 8 * 65
                dst = c.aVtm[:, vbase:vbase + 520].rearrange(
                    "p (j v) -> p j v", v=65)[:, :, 0:64]
                copy_op(k, "scalar", dst, c.ps[sub * 2 + hb][:, :], sem=c.sSC)
        tvm = c.sSC.n
        k.wait("tensor", c.sSC, tvm)
        for ci in range(2):
            l0_attn_chunk(k, c, tg, ci)
            c.pump(1)
    if c.debug:
        dump_dbg(k, c, "dbg_spre", src="sres")
    for tc in range(4):
        for og in range(2):
            for dv in range(8):
                bsel = dv % 2
                tw = k.dma("sync", c.wbf[bsel][:, 0:512],
                           c.WbfA[3, dv * 128:(dv + 1) * 128,
                                  og * 512:(og + 1) * 512], c.sW)
                k.wait("sync", c.sW, tw)
                k.wait("tensor", c.sW, tw)
                for oo in range(4):
                    mm(k, c.ps[oo][:, :], c.wbf[bsel][:, oo * 128:(oo + 1) * 128],
                       c.sres[:, dv * T + tc * 512: dv * T + (tc + 1) * 512],
                       start=(dv == 0), stop=(dv == 7))
                tmm = k.inc("tensor", c.sMM)
                k.wait("sync", c.sMM, tmm)
            tmm = c.sMM.n
            if og == 0:
                k.wait("scalar", c.sMM, tmm)
                for oo in range(4):
                    copy_op(k, "scalar", c.F32[:, oo * 512:(oo + 1) * 512],
                            c.ps[oo][:, :], sem=c.sSC)
                tstg = c.sSC.n
                k.wait("tensor", c.sSC, tstg)
        k.wait("scalar", c.sMM, tmm)
        for oo in range(4):
            copy_op(k, "scalar",
                    c.sres[:, (4 + oo) * T + tc * 512: (4 + oo) * T + (tc + 1) * 512],
                    c.ps[oo][:, :], sem=c.sSC)
        tev2 = c.sSC.n
        k.wait("vector", c.sSC, tev2)
        for oo in range(4):
            tt_op(k, "vector",
                  c.sres[:, oo * T + tc * 512: oo * T + (tc + 1) * 512],
                  c.F32[:, oo * 512:(oo + 1) * 512],
                  c.F32[:, oo * 512:(oo + 1) * 512], ALU.min, sem=c.sVE)
        tev3 = c.sVE.n
        k.drain("vector")
        k.wait("tensor", c.sVE, tev3)
        k.wait("tensor", c.sSC, tev2)
    c.tok_attn = k.inc("vector", c.sVE)
    k.wait("scalar", c.sVE, c.tok_attn)
    c.tok_attn_sc = k.inc("scalar", c.sSC)


def l0_attn_chunk(k, c, tg, ci):
    ck = tg * 2 + ci
    t0 = ck * CK
    for dtq in range(8):
        mmt = tr_op(k, c.ps7b[:, (dtq % 4) * 128:(dtq % 4 + 1) * 128],
                    c.aK[:, dtq * 256 + ci * 128: dtq * 256 + (ci + 1) * 128],
                    c.cIDTB[:], sem=c.sMM)
        k.wait("scalar", c.sMM, mmt)
        tkc = copy_op(k, "scalar",
                      c.aKtm[:, dtq * 128:(dtq + 1) * 128],
                      c.ps7b[:, (dtq % 4) * 128:(dtq % 4 + 1) * 128], sem=c.sSC)
        k.wait("tensor", c.sSC, tkc)
    tlnw = act_op(k, c.sm16[:, 288:416], c.wgbf[0:16, t0:t0 + CK], ACT.Ln, sem=c.sSC)
    k.wait("vector", c.sSC, tlnw)
    tt_op(k, "vector", c.sm16[:, 288:416], c.sm16[:, 288:416],
          c.gates[0:16, t0:t0 + CK], ALU.subtract)
    k.drain("vector")
    cendcol = c.gates[0:16, t0 + CK - 1:t0 + CK]
    if ck > 0:
        cstcol = c.gates[0:16, t0 - 1:t0]
        ts_op(k, "vector", c.sm16[:, 0:128], c.sm16[:, 288:416], cendcol, ALU.add)
        k.drain("vector")
        ts_op(k, "vector", c.sm16[:, 128:256], c.gates[0:16, t0:t0 + CK],
              cstcol, ALU.subtract)
        k.drain("vector")
        ts_op(k, "vector", c.sm16[:, 256:257], cendcol, cstcol, ALU.subtract)
    else:
        ts_op(k, "vector", c.sm16[:, 0:128], c.sm16[:, 288:416], cendcol, ALU.add)
        k.drain("vector")
        copy_op(k, "vector", c.sm16[:, 128:256], c.gates[0:16, t0:t0 + CK])
        k.drain("vector")
        copy_op(k, "vector", c.sm16[:, 256:257], cendcol)
    tprep = k.inc("vector", c.sVE)
    k.wait("scalar", c.sVE, tprep)
    taexp = act_op(k, c.sm16[:, 128:256], c.sm16[:, 128:256], ACT.Exp, sem=c.sSC)
    k.wait("tensor", c.sVE, tprep)
    k.wait("tensor", c.sVE, c.bank_free.get("p6v", 0))
    k.wait("tensor", c.sSC, c.bank_free.get("p6s", 0))
    tr_op(k, c.ps[6][:, 0:16], c.sm16[0:16, 288:416], c.cIDT[:])
    tr_op(k, c.ps[6][:, 16:32], c.sm16[0:16, 0:128], c.cIDT[:])
    ttr3 = tr_op(k, c.ps[6][0:1, 32:48], c.sm16[0:16, 256:257], c.cIDT[:],
                 sem=c.sMM)
    k.wait("vector", c.sMM, ttr3)
    tqT = copy_op(k, "vector", c.kdT[:, 16:32], c.ps[6][:, 0:16], sem=c.sVE)
    k.drain("vector")
    k.wait("scalar", c.sVE, tqT)
    tkdT = act_op(k, c.kdT[:, 0:16], c.ps[6][:, 16:32], ACT.Exp, sem=c.sSC)
    tef0 = copy_op(k, "scalar", c.F32[0:1, 1345:1361], c.ps[6][0:1, 32:48], sem=c.sSC)
    c.bank_free["p6v"] = tqT
    c.bank_free["p6s"] = tef0
    k.wait("tensor", c.sSC, tef0)
    mm(k, c.ps[6][0:128, 48:64], c.cONESROW[0:1, 0:128], c.F32[0:1, 1345:1361],
       start=True, stop=True)
    tefm = k.inc("tensor", c.sMM)
    k.wait("scalar", c.sMM, tefm)
    tefr = act_op(k, c.efr[:, :], c.ps[6][0:128, 48:64], ACT.Exp, sem=c.sSC)
    k.wait("tensor", c.sSC, taexp)
    for dt in range(NDT):
        mm(k, c.ps[6][:, 128:256], c.cREPL2[:, dt * 128:(dt + 1) * 128],
           c.sm16[0:16, 128:256], start=True, stop=True)
        tar = k.inc("tensor", c.sMM)
        k.wait("vector", c.sMM, tar)
        tqa = tt_op(k, "vector", c.F32[:, dt * 128:(dt + 1) * 128],
                    c.aQ[:, dt * 256 + ci * 128: dt * 256 + (ci + 1) * 128],
                    c.ps[6][:, 128:256], ALU.mult, sem=c.sVE)
        k.drain("vector")
        k.wait("tensor", c.sVE, tqa)
    c.bank_free["p6s"] = tefr
    c.bank_free["p6v"] = c.sVE.n
    k.wait("vector", c.sSC, tefr)
    k.wait("vector", c.sSC, tkdT)

    for h in range(16):
        if h % 4 == 0:
            k.wait("sync", c.sMM, c.crow_mm_tok)
            for hh in range(h, h + 4):
                tcrow = k.dma("sync", c.crows[0:1, (hh % 4) * 128:(hh % 4 + 1) * 128],
                              c.gates[hh:hh + 1, t0:t0 + CK], c.sW)
            k.wait("sync", c.sW, tcrow)
            k.wait("tensor", c.sW, tcrow)
        dth = h // 2
        base = 64 * (h % 2)
        vsl = c.aVtm[:, ci * 1040 + h * 65: ci * 1040 + h * 65 + 65]
        ksl = c.aKtm[:, h * 64:(h + 1) * 64]
        k.wait("tensor", c.sVE, c.bank_free.get("p0", 0))
        k.wait("tensor", c.sVE, c.bank_free.get("p1", 0))
        mm(k, c.ps[0][:, 0:128],
           c.aK[base:base + 64, dth * 256 + ci * 128: dth * 256 + (ci + 1) * 128],
           c.aQ[base:base + 64, dth * 256 + ci * 128: dth * 256 + (ci + 1) * 128],
           start=True, stop=True)
        mm(k, c.ps[1][:, 0:128], c.cONESROW[0:1, :],
           c.crows[0:1, (h % 4) * 128:(h % 4 + 1) * 128], start=True, stop=True)
        tsm = k.inc("tensor", c.sMM)
        c.crow_mm_tok = tsm
        tkd = ts_op(k, "vector", c.bfS[:, 128:192], ksl, c.kdT[:, h:h + 1], ALU.mult,
                    sem=c.sVE)
        k.wait("vector", c.sMM, tsm)
        stt_op(k, "vector", c.F32[:, 1024:1152], c.ps[1][:, 0:128],
               c.kdT[:, 16 + h:17 + h], ALU.add, c.cMBIG[:], ALU.add)
        tmc = k.inc("vector", c.sVE)
        c.bank_free["p1"] = tmc
        k.wait("scalar", c.sVE, tmc)
        tE = act_op(k, c.F32[:, 1152:1280], c.F32[:, 1024:1152], ACT.Exp, sem=c.sSC)
        k.wait("vector", c.sSC, tE)
        teff = tt_op(k, "vector", c.bfS[:, 0:128], c.ps[0][:, 0:128],
                     c.F32[:, 1152:1280], ALU.mult, sem=c.sVE)
        k.drain("vector")
        c.bank_free["p0"] = teff
        k.wait("tensor", c.sVE, teff)
        k.wait("tensor", c.sVE, c.bank_free.get("p2", 0))
        mm(k, c.ps[2][0:65, 0:128], vsl, c.bfS[:, 0:128], start=True, stop=False)
        mm(k, c.ps[2][0:65, 0:128], c.Tst[base:base + 64, dth * 65:(dth + 1) * 65],
           c.F32[base:base + 64, dth * 128:(dth + 1) * 128],
           start=False, stop=True)
        tO = k.inc("tensor", c.sMM)
        k.wait("tensor", c.sVE, tkd)
        k.wait("tensor", c.sVE, c.bank_free.get("p4", 0))
        mm(k, c.ps[4][base:base + 64, 0:65], c.bfS[:, 128:192], vsl,
           start=True, stop=True)
        tWs = k.inc("tensor", c.sMM)
        k.wait("vector", c.sMM, tO)
        ts_op(k, "vector", c.F32[64:65, 1024:1152], c.ps[2][64:65, 0:128], 1e-6, ALU.add)
        k.drain("vector")
        k.raw("vector", lambda e: e.reciprocal(c.F32[64:65, 1152:1280], c.F32[64:65, 1024:1152]))
        trc = k.inc("vector", c.sVE)
        k.wait("tensor", c.sVE, trc)
        k.wait("tensor", c.sVE, c.bank_free.get("p3", 0))
        mm(k, c.ps[3][0:64, 0:128], c.cONESROW[64:65, 0:64], c.F32[64:65, 1152:1280],
           start=True, stop=True)
        trp = k.inc("tensor", c.sMM)
        k.wait("scalar", c.sMM, trp)
        trcp = copy_op(k, "scalar", c.F32[0:64, 1024:1152], c.ps[3][0:64, 0:128],
                       sem=c.sSC)
        k.wait("vector", c.sSC, trcp)
        slot = (h % 2) * 128
        k.wait("vector", c.sW, c.bank_free.get(f"soutdma{h % 2}", 0))
        tso = tt_op(k, "vector", c.bfT[0:64, slot:slot + 128],
                    c.ps[2][0:64, 0:128], c.F32[0:64, 1024:1152], ALU.mult, sem=c.sVE)
        k.drain("vector")
        c.bank_free["p2"] = tso
        c.bank_free["p3"] = tso
        k.wait("sync", c.sVE, tso)
        tsd = k.dma("sync", c.sres[base:base + 64, dth * T + t0: dth * T + t0 + CK],
                    c.bfT[0:64, slot:slot + 128], c.sW)
        k.wait("sync", c.sW, tsd)
        c.bank_free[f"soutdma{h % 2}"] = tsd
        k.wait("vector", c.sMM, tWs)
        stt_op(k, "vector", c.F32[base:base + 64, 1280:1345],
               c.Tst[base:base + 64, dth * 65:(dth + 1) * 65],
               c.efr[base:base + 64, h:h + 1], ALU.mult,
               c.ps[4][base:base + 64, 0:65], ALU.add)
        k.drain("vector")
        tTu = copy_op(k, "vector", c.Tst[base:base + 64, dth * 65:(dth + 1) * 65],
                      c.F32[base:base + 64, 1280:1345], sem=c.sVE)
        k.drain("vector")
        c.bank_free["p4"] = tTu
        k.wait("tensor", c.sVE, tTu)
    k.wait("sync", c.sW, c.sW.n)


def l0_ffn_combine(k, c):
    k.wait("sync", c.sD2, c.tok_wprep)
    WV = lambda i: c.F32[:, i * 256:(i + 1) * 256]
    WG = lambda i: c.G32[:, i * 256:(i + 1) * 256]
    hfree = {}
    yfree = {}
    qx_mm_tok = 0
    for tf in range(8):
        k.wait("sync", c.sMM, qx_mm_tok)
        for kt in range(8):
            tqx = k.dma("sync", c.qxb[:, kt * 256:(kt + 1) * 256],
                        c.qxs[kt, :, tf * 256:(tf + 1) * 256], c.sW)
        k.wait("sync", c.sW, tqx)
        k.wait("tensor", c.sW, tqx)
        for ft in range(32):
            bsel = ft % 2
            k.wait("sync", c.sMM, c.bank_free.get(f"w8{bsel}", 0))
            tw = k.dma("sync", c.w8[bsel][:, 0:1024], c.qW1s[ft][:, 0:1024], c.sW)
            k.wait("sync", c.sW, tw)
            k.wait("tensor", c.sW, tw)
            ent = hfree.get(ft % 2)
            if ent:
                k.wait("tensor", getattr(c, ent[0]), ent[1])
            pb = c.ps[ft % 2]
            for kt in range(8):
                mm(k, pb[:, 0:256], c.w8[bsel][:, kt * 128:(kt + 1) * 128],
                   c.qxb[:, kt * 256:(kt + 1) * 256],
                   start=(kt == 0), stop=(kt == 7))
            th = k.inc("tensor", c.sMM)
            c.bank_free[f"w8{bsel}"] = th
            eng = "vector" if ft % 3 != 2 else "gpsimd"
            semE = "sVE" if eng == "vector" else "sGP"
            Wc = WV if eng == "vector" else WG
            hrm = (c.G32[:, 1792:2048] if eng == "vector" else c.F32[:, 2048:2304])
            k.wait(eng, c.sMM, th)
            if eng == "gpsimd":
                k.wait("scalar", c.sMM, th)
            q_chain(k, c, eng, hrm, pb[:, 0:256], 7, 126, Wc, ties_even=False,
                    relu_bias=c.cols[:, ft:ft + 1],
                    first_sem=c.sVE if eng == "vector" else c.sSC,
                    first_on_scalar=(eng == "gpsimd"))
            hfree[ft % 2] = (("sVE" if eng == "vector" else "sSC"), c.first_tok)
            q_chain(k, c, eng, c.hq[:, ft * 256:(ft + 1) * 256], hrm, 2, 14, Wc,
                    ties_even=True, nonneg=True,
                    sem=c.sVE if eng == "vector" else c.sGP)
        qx_mm_tok = c.sMM.n
        c.bank_free["yw80"] = c.bank_free.get("w80", 0)
        c.bank_free["yw81"] = c.bank_free.get("w81", 0)
        tvq = k.inc("vector", c.sVE)
        tgq = k.inc("gpsimd", c.sGP)
        k.wait("tensor", c.sVE, tvq)
        k.wait("tensor", c.sGP, tgq)
        for ot in range(8):
            k.wait("sync", c.sVE, c.bank_free.get("drop", 0))
            td = k.dma("sync", c.bfS[:, 128:384],
                       c.DROP[ot * 128:(ot + 1) * 128, tf * 256:(tf + 1) * 256], c.sW)
            k.wait("sync", c.sW, td)
            ent = yfree.get(ot % 2)
            if ent:
                k.wait("tensor", getattr(c, ent[0]), ent[1])
            pb = c.ps[2 + ot % 2]
            for qtr in range(4):
                qb = qtr % 2
                k.wait("sync", c.sMM, c.bank_free.get(f"yw8{qb}", 0))
                tw = k.dma("sync", c.w8[qb][:, 0:1024],
                           c.qW2s[ot][:, qtr * 1024:(qtr + 1) * 1024], c.sW)
                k.wait("sync", c.sW, tw)
                k.wait("tensor", c.sW, tw)
                if qtr == 0:
                    k.wait("tensor", c.sW, td)
                for fi in range(8):
                    ft = qtr * 8 + fi
                    mm(k, pb[:, 0:256], c.w8[qb][:, fi * 128:(fi + 1) * 128],
                       c.hq[:, ft * 256:(ft + 1) * 256],
                       start=(ft == 0), stop=(ft == 31))
                tqm = k.inc("tensor", c.sMM)
                c.bank_free[f"yw8{qb}"] = tqm
            ty = c.sMM.n
            k.wait("vector", c.sMM, ty)
            k.wait("vector", c.sW, td)
            q_chain(k, c, "vector", c.F32[:, 1792:2048], pb[:, 0:256], 7, 126, WV,
                    ties_even=False, bias_col=c.cols[:, 32 + ot:33 + ot],
                    first_sem=c.sVE)
            yfree[ot % 2] = ("sVE", c.first_tok)
            tt_op(k, "vector", c.F32[:, 1792:2048], c.F32[:, 1792:2048],
                  c.bfS[:, 128:384], ALU.mult)
            tdm = k.inc("vector", c.sVE)
            c.bank_free["drop"] = tdm
            k.wait("scalar", c.sVE, tdm)
            tsi = act_op(k, c.bfT[:, 0:256], c.F32[:, 1792:2048], ACT.Silu, sem=c.sSC)
            k.wait("vector", c.sSC, tsi)
            sl = slice(ot * T + tf * 256, ot * T + (tf + 1) * 256)
            tt_op(k, "vector", c.F32[:, 1792:2048], c.sres[:, sl], c.bfT[:, 0:256],
                  ALU.mult)
            k.drain("vector")
            tt_op(k, "vector", c.xres[:, sl], c.xres[:, sl], c.F32[:, 1792:2048],
                  ALU.add, sem=c.sVE)
            k.drain("vector")
        c.bank_free["w80"] = c.sMM.n
        c.bank_free["w81"] = c.sMM.n
    c.tok_l0 = k.inc("vector", c.sVE)





def pump_until(c, attr):
    while not hasattr(c, attr):
        c.pump(1)


def l1_cast_gen(k, c):
    for wi, wsrc in enumerate((c.L1Wq, c.L1Wk, c.L1Wv)):
        t = k.dma("gpsimd", c.WbfA[wi], wsrc, c.sD2)
        yield
    k.wait("gpsimd", c.sD2, t)
    c.tok_l1qkv = c.sD2.n
    yield
    for e in range(8):
        for ph, wsrc in ((0, c.L1w1), (1, c.L1v1)):
            k.dma("gpsimd", c.L1wb[e, ph], wsrc[e], c.sD2)
            yield
        t2 = k.dma("gpsimd", c.L1w2b[e], c.L1w2[e], c.sD2)
        k.wait("gpsimd", c.sD2, t2)
        setattr(c, f"tok_l1e{e}", c.sD2.n)
        yield
    while True:
        yield


def proj_qkv(k, c, tok_attr, l1_consts=False):
    pump_until(c, tok_attr)
    k.wait("sync", c.sD2, getattr(c, tok_attr))
    k.wait("sync", c.sVE, c.tok_ln)
    k.wait("tensor", c.sVE, c.tok_ln)
    if l1_consts:
        k.dma("sync", c.cOMGT2[:], c.OMGT2, c.sW)
        tcn = k.dma("sync", c.cNEGH[:], c.NEGH, c.sW)
        k.wait("sync", c.sW, tcn)
        k.wait("tensor", c.sW, tcn)
        k.wait("scalar", c.sW, tcn)
        k.wait("vector", c.sW, tcn)
    evac = lambda i: (c.aQ[:, i * 512:(i + 1) * 512] if i < 4
                      else c.aK[:, (i - 4) * 512:(i - 3) * 512])
    wtok = {}
    tsc_prev = 0
    twq_prev = 0
    for pi, dstm in ((0, c.qfms), (1, c.kfms)):
        for tc in range(4):
            for og in range(2):
                for dt in range(8):
                    bsel = dt % 2
                    k.wait("sync", c.sMM, wtok.get(bsel, 0))
                    tw = k.dma("sync", c.wbf[bsel][:, 0:1024],
                               c.WbfA[pi, dt * 128:(dt + 1) * 128, :], c.sW)
                    k.wait("sync", c.sW, tw)
                    k.wait("tensor", c.sW, tw)
                    if dt == 0:
                        k.wait("tensor", c.sSC, tsc_prev)
                    for oi in range(4):
                        ot = og * 4 + oi
                        mm(k, c.ps[oi][:, :],
                           c.wbf[bsel][:, ot * 128:(ot + 1) * 128],
                           c.xnbf[:, dt * T + tc * 512: dt * T + (tc + 1) * 512],
                           start=(dt == 0), stop=(dt == 7))
                    wtok[bsel] = k.inc("tensor", c.sMM)
                tmm = c.sMM.n
                k.wait("scalar", c.sMM, tmm)
                k.wait("scalar", c.sW, twq_prev)
                for oi in range(4):
                    copy_op(k, "scalar", evac(og * 4 + oi), c.ps[oi][:, :], sem=c.sSC)
                tsc_prev = c.sSC.n
                k.wait("sync", c.sSC, tsc_prev)
                for oi in range(4):
                    ot = og * 4 + oi
                    k.dma("sync",
                          dstm[ot * 128:(ot + 1) * 128, tc * 512:(tc + 1) * 512],
                          evac(ot), c.sW)
                twq_prev = c.sW.n
                k.wait("sync", c.sW, twq_prev)
                c.pump(1)
    for tch in range(4):
        for oh in range(2):
            for dt in range(8):
                bsel = dt % 2
                k.wait("sync", c.sMM, wtok.get(bsel, 0))
                tw = k.dma("sync", c.wbf[bsel][:, 0:1024],
                           c.WbfA[2, dt * 128:(dt + 1) * 128, :], c.sW)
                k.wait("sync", c.sW, tw)
                k.wait("tensor", c.sW, tw)
                if dt == 0:
                    k.wait("tensor", c.sSC, tsc_prev)
                for tt2 in range(4):
                    mm(k, c.ps[tt2][:, :],
                       c.xnbf[:, dt * T + tch * 512 + tt2 * 128:
                              dt * T + tch * 512 + (tt2 + 1) * 128],
                       c.wbf[bsel][:, oh * 512:(oh + 1) * 512],
                       start=(dt == 0), stop=(dt == 7))
                wtok[bsel] = k.inc("tensor", c.sMM)
            tmm = c.sMM.n
            k.wait("scalar", c.sMM, tmm)
            k.wait("scalar", c.sW, twq_prev)
            for tt2 in range(4):
                copy_op(k, "scalar", evac(tt2), c.ps[tt2][:, :], sem=c.sSC)
            tsc_prev = c.sSC.n
            k.wait("sync", c.sSC, tsc_prev)
            for tt2 in range(4):
                k.dma("sync", c.vtms[tch * 512 + tt2 * 128: tch * 512 + (tt2 + 1) * 128,
                                     oh * 512:(oh + 1) * 512],
                      evac(tt2), c.sW)
            twq_prev = c.sW.n
            k.wait("sync", c.sW, twq_prev)
            c.pump(1)
    c.tok_l1proj = twq_prev
    k.wait("tensor", c.sSC, tsc_prev)


def l1_attn_heads(k, c):
    NL16 = -math.log(16.0)
    tok_mm_prev = c.sMM.n
    tok_sc_prev = c.sSC.n
    tok_ve_prev = c.sVE.n
    tok_s_prev = 0
    for h in range(16):
        base = 64 * (h % 2)
        bsl = slice(base, base + 64)
        k.wait("sync", c.sMM, tok_mm_prev)
        k.wait("sync", c.sSC, tok_sc_prev)
        k.wait("sync", c.sVE, tok_ve_prev)
        k.wait("sync", c.sW, tok_s_prev)
        k.wait("vector", c.sMM, tok_mm_prev)
        k.wait("vector", c.sSC, tok_sc_prev)
        k.dma("sync", c.aK[bsl, 0:2048], c.kfms[64 * h:64 * h + 64, :], c.sW)
        k.dma("sync", c.aQ[bsl, 0:2048], c.qfms[64 * h:64 * h + 64, :], c.sW)
        vdst = c.aVtm[:, 0:1040].rearrange("p (j v) -> p j v", v=65)[:, :, 0:64]
        vsrc = c.vtms.rearrange("(n p) o -> p n o", p=128)[:, :, 64 * h:64 * h + 64]
        tv = k.dma("sync", vdst, vsrc, c.sW)
        k.wait("sync", c.sW, tv)
        for eng in ("tensor", "vector", "scalar"):
            k.wait(eng, c.sW, tv)
        pkslot = lambda nt: c.hA[:, (nt % 2) * 256:(nt % 2) * 256 + 256]
        pktok = {}
        tpk_prev = 0
        for nt in range(16):
            nsl = slice(nt * 128, (nt + 1) * 128)
            tq2 = tt_op(k, "vector", c.F32[bsl, 0:128], c.aK[bsl, nsl],
                        c.aK[bsl, nsl], ALU.mult, sem=c.sVE)
            k.wait("tensor", c.sVE, tq2)
            mm(k, c.ps[0][:, 0:256], c.aK[bsl, nsl], c.cOMGT2[bsl, :],
               start=True, stop=True)
            mm(k, c.ps[1][:, 0:1], c.F32[bsl, 0:128], c.cNEGH[bsl, :],
               start=True, stop=True)
            tpm = k.inc("tensor", c.sMM)
            k.wait("vector", c.sMM, tpm)
            k.wait("vector", c.sSC, tpk_prev)
            tnk = ts_op(k, "vector", c.F32[:, 2048:2049], c.ps[1][:, 0:1],
                        NL16, ALU.add, sem=c.sVE)
            k.wait("scalar", c.sVE, tnk)
            k.wait("scalar", c.sMM, tpm)
            ent = pktok.get(nt % 2)
            if ent:
                k.wait("scalar", c.sMM, ent)
            tpk = act_op(k, pkslot(nt), c.ps[0][:, 0:256], ACT.Exp,
                         bias=c.F32[:, 2048:2049], sem=c.sSC)
            tpk_prev = tpk
            k.wait("tensor", c.sSC, tpk)
            for mt in range(2):
                mm(k, c.ps[2 + mt][:, 0:65],
                   pkslot(nt)[:, mt * 128:(mt + 1) * 128],
                   c.aVtm[:, nt * 65:nt * 65 + 65],
                   start=(nt == 0), stop=(nt == 15))
            pktok[nt % 2] = k.inc("tensor", c.sMM)
        tkv = c.sMM.n
        k.wait("scalar", c.sMM, tkv)
        copy_op(k, "scalar", c.Tst[:, 0:65], c.ps[2][:, 0:65])
        tkve = copy_op(k, "scalar", c.Tst[:, 65:130], c.ps[3][:, 0:65], sem=c.sSC)
        k.wait("tensor", c.sSC, tkve)
        k.wait("vector", c.sMM, tkv)
        tep_prev = 0
        for tch in range(4):
            tsl = slice(tch * 512, (tch + 1) * 512)
            tq2 = tt_op(k, "vector", c.F32[bsl, 0:512], c.aQ[bsl, tsl],
                        c.aQ[bsl, tsl], ALU.mult, sem=c.sVE)
            k.wait("tensor", c.sVE, tq2)
            k.wait("tensor", c.sSC, tep_prev)
            for mt in range(2):
                mm(k, c.ps[mt][:, :], c.cOMGT2[bsl, mt * 128:(mt + 1) * 128],
                   c.aQ[bsl, tsl], start=True, stop=True)
            mm(k, c.ps[4][0:1, :], c.cNEGH[bsl, :], c.F32[bsl, 0:512],
               start=True, stop=True)
            tpm = k.inc("tensor", c.sMM)
            k.wait("scalar", c.sMM, tpm)
            ten = act_op(k, c.F32[0:1, 1536:2048], c.ps[4][0:1, :], ACT.Exp,
                         bias=c.cACTC[0:1, 7:8], sem=c.sSC)
            act_op(k, c.F32[:, 512:1024], c.ps[0][:, :], ACT.Exp)
            tep = act_op(k, c.F32[:, 1024:1536], c.ps[1][:, :], ACT.Exp, sem=c.sSC)
            tep_prev = tep
            k.wait("tensor", c.sSC, ten)
            mm(k, c.ps[5][:, :], c.cONESROW[0:1, 0:128], c.F32[0:1, 1536:2048],
               start=True, stop=True)
            trep = k.inc("tensor", c.sMM)
            k.wait("vector", c.sMM, trep)
            k.wait("vector", c.sSC, tep)
            tt_op(k, "vector", c.F32[:, 512:1024], c.F32[:, 512:1024],
                  c.ps[5][:, :], ALU.mult)
            k.drain("vector")
            tpq = tt_op(k, "vector", c.F32[:, 1024:1536], c.F32[:, 1024:1536],
                        c.ps[5][:, :], ALU.mult, sem=c.sVE)
            k.wait("tensor", c.sVE, tpq)
            mm(k, c.ps[6][0:65, :], c.Tst[:, 0:65], c.F32[:, 512:1024],
               start=True, stop=False)
            mm(k, c.ps[6][0:65, :], c.Tst[:, 65:130], c.F32[:, 1024:1536],
               start=False, stop=True)
            tout = k.inc("tensor", c.sMM)
            k.wait("scalar", c.sMM, tout)
            tden = act_op(k, c.F32[64:65, 1536:2048], c.ps[6][64:65, :], ACT.Copy,
                          bias=1e-6, sem=c.sSC)
            k.wait("vector", c.sSC, tden)
            k.raw("vector", lambda e, o=c.F32[64:65, 1024:1536],
                  i=c.F32[64:65, 1536:2048]: e.reciprocal(o, i))
            trc = k.inc("vector", c.sVE)
            k.wait("tensor", c.sVE, trc)
            mm(k, c.ps[5][:, :], c.cONESROW[64:65, 0:128], c.F32[64:65, 1024:1536],
               start=True, stop=True)
            trp = k.inc("tensor", c.sMM)
            k.wait("scalar", c.sMM, trp)
            trv = copy_op(k, "scalar", c.F32[0:64, 1536:2048], c.ps[5][0:64, :],
                          sem=c.sSC)
            k.wait("vector", c.sSC, trv)
            tso = tt_op(k, "vector", c.aK[0:64, tsl], c.ps[6][0:64, :],
                        c.F32[0:64, 1536:2048], ALU.mult, sem=c.sVE)
            k.wait("sync", c.sVE, tso)
            tsd = k.dma("sync", c.sres[bsl, (h // 2) * T + tch * 512:
                                       (h // 2) * T + (tch + 1) * 512],
                        c.aK[0:64, tsl], c.sW)
            k.wait("sync", c.sW, tsd)
            c.pump(2)
        tok_mm_prev = c.sMM.n
        tok_sc_prev = c.sSC.n
        tok_ve_prev = c.sVE.n
        tok_s_prev = c.sW.n
    c.tok_l1s = tok_s_prev
    c.tok_l1attn = (tok_mm_prev, tok_sc_prev, tok_ve_prev)


def l1_router(k, c):
    tm0, ts0, tv0 = c.tok_l1attn
    k.wait("sync", c.sMM, tm0)
    t1 = k.dma("sync", c.F32[:, 0:64].rearrange("p (a b) -> p a b", b=8),
               c.L1rw.rearrange("(a p) b -> p a b", p=128), c.sW)
    tb = k.dma("sync", c.colsB[:, 0:8], c.L1bias, c.sW)
    k.wait("sync", c.sW, tb)
    k.wait("vector", c.sW, tb)
    k.wait("vector", c.sMM, tm0)
    k.wait("vector", c.sSC, ts0)
    tr = copy_op(k, "vector", c.rwb[:, 0:64], c.F32[:, 0:64], sem=c.sVE)
    k.wait("tensor", c.sVE, tr)
    k.wait("tensor", c.sSC, ts0)
    k.wait("scalar", c.sMM, tm0)
    k.wait("scalar", c.sVE, tv0)
    tdma_prev = 0
    for tt in range(16):
        for dt in range(8):
            mm(k, c.ps[0][:, 0:8], c.xnbf[:, dt * T + tt * 128:dt * T + (tt + 1) * 128],
               c.rwb[:, dt * 8:(dt + 1) * 8], start=(dt == 0), stop=(dt == 7))
        tlg = k.inc("tensor", c.sMM)
        k.wait("vector", c.sMM, tlg)
        k.raw("vector", lambda e, o=c.F32[:, 64:65], i=c.ps[0][:, 0:8]:
              e.reduce_max(o, i, axis=AX.X))
        k.drain("vector")
        tneg = ts_op(k, "vector", c.F32[:, 64:65], c.F32[:, 64:65], -1.0,
                     ALU.mult, sem=c.sVE)
        k.wait("scalar", c.sVE, tneg)
        texp = act_op(k, c.F32[:, 72:80], c.ps[0][:, 0:8], ACT.Exp,
                      bias=c.F32[:, 64:65], accum=c.F32[:, 65:66], sem=c.sSC)
        k.wait("vector", c.sSC, texp)
        k.raw("vector", lambda e, o=c.F32[:, 66:67], i=c.F32[:, 72:80]:
              e.reduce_max(o, i, axis=AX.X))
        k.drain("vector")
        stt_op(k, "vector", c.F32[:, 80:88], c.F32[:, 72:80], c.F32[:, 66:67],
               ALU.is_lt, c.F32[:, 72:80], ALU.mult)
        k.drain("vector")
        k.raw("vector", lambda e, o=c.F32[:, 67:68], i=c.F32[:, 80:88]:
              e.reduce_max(o, i, axis=AX.X))
        k.drain("vector")
        ts_op(k, "vector", c.F32[:, 80:88], c.F32[:, 72:80], c.F32[:, 67:68],
              ALU.is_ge)
        k.drain("vector")
        tt_op(k, "vector", c.F32[:, 80:88], c.F32[:, 80:88], c.F32[:, 72:80],
              ALU.mult)
        k.drain("vector")
        k.raw("vector", lambda e, o=c.F32[:, 68:69], i=c.F32[:, 65:66]:
              e.reciprocal(o, i))
        k.drain("vector")
        tcg = ts_op(k, "vector", c.bfS[:, 0:8], c.F32[:, 80:88], c.F32[:, 68:69],
                    ALU.mult, sem=c.sVE)
        k.wait("tensor", c.sVE, tcg)
        tr2 = tr_op(k, c.ps7b[0:8, 0:128], c.bfS[:, 0:8], c.cIDTB[:], sem=c.sMM)
        k.wait("scalar", c.sMM, tr2)
        k.wait("scalar", c.sW, tdma_prev)
        tev = copy_op(k, "scalar", c.F32[0:8, 96:224], c.ps7b[0:8, 0:128],
                      sem=c.sSC)
        k.wait("sync", c.sSC, tev)
        tdma_prev = k.dma("sync", c.ctrd[tt], c.F32[0:8, 96:224], c.sW)
        k.wait("sync", c.sW, tdma_prev)
    c.tok_router = tdma_prev


def hslot(c, ft):
    if ft < 12:
        return c.hA[:, ft * 512:(ft + 1) * 512]
    if ft < 16:
        return c.aQ[:, (ft - 12) * 512:(ft - 11) * 512]
    if ft < 20:
        return c.aK[:, (ft - 16) * 512:(ft - 15) * 512]
    return c.aKtm[:, (ft - 20) * 512:(ft - 19) * 512]


def l1_ffn_combine(k, c):
    ytok = {}
    wtok = {}
    gtok = {}
    stok = 0
    for e in range(8):
        pump_until(c, f"tok_l1e{e}")
        k.wait("sync", c.sD2, getattr(c, f"tok_l1e{e}"))
        for tf in range(4):
            tfs = slice(tf * 512, (tf + 1) * 512)
            k.wait("sync", c.sMM, gtok.get("crow", 0))
            tcr = k.dma("sync", c.crow5[0:1, 0:512].rearrange("p (a b) -> p a b", b=128),
                        c.ctrd[tf * 4:(tf + 1) * 4, e:e + 1, :], c.sW)
            k.wait("sync", c.sW, tcr)
            k.wait("tensor", c.sW, tcr)
            k.wait("tensor", c.sSC, stok)
            mm(k, c.ps[6][:, :], c.cONESROW[0:1, 0:128], c.crow5[0:1, 0:512],
               start=True, stop=True)
            tcm = k.inc("tensor", c.sMM)
            gtok["crow"] = tcm
            k.wait("scalar", c.sMM, tcm)
            k.wait("scalar", c.sVE, gtok.get("thh_last", 0))
            stok = copy_op(k, "scalar", c.F32[:, 1024:1536], c.ps[6][:, :], sem=c.sSC)
            k.wait("vector", c.sSC, stok)
            for ft in range(22):
                bsel = ft % 2
                k.wait("sync", c.sMM, wtok.get(f"w{bsel}", 0))
                tw1 = k.dma("sync", c.wbf[bsel][:, 0:1024].rearrange(
                    "p (a b) -> p a b", b=128),
                    c.L1wb[e, 0].rearrange("(a p) f -> p a f", p=128)[:, :, ft * 128:(ft + 1) * 128],
                    c.sW)
                tw2 = k.dma("sync", c.wbg[bsel][:, 0:1024].rearrange(
                    "p (a b) -> p a b", b=128),
                    c.L1wb[e, 1].rearrange("(a p) f -> p a f", p=128)[:, :, ft * 128:(ft + 1) * 128],
                    c.sW)
                k.wait("sync", c.sW, tw2)
                k.wait("tensor", c.sW, tw2)
                pb1 = c.ps[(ft % 2) * 2]
                pb2 = c.ps[(ft % 2) * 2 + 1]
                ent = gtok.get(ft % 2)
                if ent:
                    k.wait("tensor", c.sVE, ent[0])
                    k.wait("tensor", c.sSC, ent[1])
                k.wait("tensor", c.sMM, ytok.get(f"hs{ft}", 0))
                for kt in range(8):
                    ksl = slice(kt * T + tf * 512, kt * T + (tf + 1) * 512)
                    mm(k, pb1[:, :], c.wbf[bsel][:, kt * 128:(kt + 1) * 128],
                       c.xnbf[:, ksl], start=(kt == 0), stop=(kt == 7))
                    mm(k, pb2[:, :], c.wbg[bsel][:, kt * 128:(kt + 1) * 128],
                       c.xnbf[:, ksl], start=(kt == 0), stop=(kt == 7))
                th = k.inc("tensor", c.sMM)
                wtok[f"w{bsel}"] = th
                k.wait("scalar", c.sMM, th)
                k.wait("scalar", c.sVE, gtok.get("thh_last", 0))
                tg1 = act_op(k, c.F32[:, 0:512], pb1[:, :], ACT.Gelu, sem=c.sSC)
                k.wait("vector", c.sSC, tg1)
                k.wait("vector", c.sMM, th)
                tt_op(k, "vector", c.F32[:, 512:1024], c.F32[:, 0:512], pb2[:, :],
                      ALU.mult)
                k.drain("vector")
                thh = tt_op(k, "vector", hslot(c, ft), c.F32[:, 512:1024],
                            c.F32[:, 1024:1536], ALU.mult, sem=c.sVE)
                gtok[ft % 2] = (thh, tg1)
                gtok["thh_last"] = thh
                c.pump(1)
            thp = k.inc("vector", c.sVE)
            k.wait("tensor", c.sVE, thp)
            for ot in range(8):
                rk = f"y{ot}_{tf}"
                if e > 0:
                    k.wait("sync", c.sW, ytok.get(rk + "w", 0))
                    k.wait("sync", c.sVE, ytok.get("yin", 0))
                    tyi = k.dma("sync", c.F32[:, 1536:2048],
                                c.yacc[ot][:, tfs], c.sW)
                    k.wait("sync", c.sW, tyi)
                    k.wait("vector", c.sW, tyi)
                pb = c.ps[4 + ot % 2]
                entp = ytok.get(f"pb{ot % 2}")
                if entp is not None:
                    k.wait("tensor", entp[0], entp[1])
                for cc in range(3):
                    nft = 8 if cc < 2 else 6
                    qb = cc % 2
                    k.wait("sync", c.sMM, wtok.get(f"w{qb}", 0))
                    tw = k.dma("sync", c.wbf[qb][:, 0:nft * 128].rearrange(
                        "p (a b) -> p a b", b=128),
                        c.L1w2b[e].rearrange("(a p) f -> p a f", p=128)[
                            :, cc * 8:cc * 8 + nft, ot * 128:(ot + 1) * 128],
                        c.sW)
                    k.wait("sync", c.sW, tw)
                    k.wait("tensor", c.sW, tw)
                    for fi in range(nft):
                        ft = cc * 8 + fi
                        mm(k, pb[:, :], c.wbf[qb][:, fi * 128:(fi + 1) * 128],
                           hslot(c, ft), start=(ft == 0), stop=(ft == 21))
                    wtok[f"w{qb}"] = k.inc("tensor", c.sMM)
                ty = c.sMM.n
                for ft in range(22):
                    ytok[f"hs{ft}"] = ty
                ev = c.G32[:, (ot % 2) * 512:(ot % 2) * 512 + 512]
                if e == 0:
                    k.wait("scalar", c.sMM, ty)
                    k.wait("scalar", c.sW, ytok.get(f"evw{ot % 2}", 0))
                    tev = copy_op(k, "scalar", ev, pb[:, :], sem=c.sSC)
                    ytok[f"pb{ot % 2}"] = (c.sSC, tev)
                    k.wait("sync", c.sSC, tev)
                else:
                    k.wait("vector", c.sMM, ty)
                    k.wait("vector", c.sW, ytok.get(f"evw{ot % 2}", 0))
                    tev = tt_op(k, "vector", ev, c.F32[:, 1536:2048], pb[:, :],
                                ALU.add, sem=c.sVE)
                    ytok["yin"] = tev
                    ytok[f"pb{ot % 2}"] = (c.sVE, tev)
                    k.wait("sync", c.sVE, tev)
                tyw = k.dma("sync", c.yacc[ot][:, tfs], ev, c.sW)
                k.wait("sync", c.sW, tyw)
                ytok[rk + "w"] = tyw
                ytok[f"evw{ot % 2}"] = tyw
                c.pump(1)
    k.wait("vector", c.sW, c.tok_l1s)
    k.wait("scalar", c.sW, c.tok_l1s)
    for tf in range(4):
        tfs = slice(tf * 512, (tf + 1) * 512)
        for ot in range(8):
            k.wait("sync", c.sW, ytok.get(f"y{ot}_{tf}w", 0))
            k.wait("sync", c.sSC, ytok.get("cmb", 0))
            tyi = k.dma("sync", c.F32[:, 1536:2048], c.yacc[ot][:, tfs], c.sW)
            k.wait("sync", c.sW, tyi)
            k.wait("scalar", c.sW, tyi)
            k.wait("scalar", c.sVE, ytok.get("cmbv", 0))
            tsi = act_op(k, c.hA[:, 0:512], c.F32[:, 1536:2048], ACT.Silu,
                         bias=c.colsB[:, ot:ot + 1], sem=c.sSC)
            ytok["cmb"] = tsi
            k.wait("vector", c.sSC, tsi)
            sl = slice(ot * T + tf * 512, ot * T + (tf + 1) * 512)
            tt_op(k, "vector", c.F32[:, 512:1024], c.sres[:, sl], c.hA[:, 0:512],
                  ALU.mult)
            k.drain("vector")
            tcx = tt_op(k, "vector", c.xres[:, sl], c.xres[:, sl],
                        c.F32[:, 512:1024], ALU.add, sem=c.sVE)
            ytok["cmbv"] = tcx
    c.tok_l1 = k.inc("vector", c.sVE)




def l2_cast_gen(k, c):
    k.wait("gpsimd", c.sVE, c.tok_l1)
    t = None
    for wi, wsrc in enumerate((c.L2WqT, c.L2WkT, c.L2WvT, c.L2WuT)):
        t = k.dma("gpsimd", c.WbfA[wi], wsrc, c.sD2)
        yield
    k.wait("gpsimd", c.sD2, t)
    c.tok_l2qkv = c.sD2.n
    yield
    k.dma("gpsimd", c.l2w1b, c.L2w1, c.sD2)
    yield
    t = k.dma("gpsimd", c.l2w2b, c.L2w2, c.sD2)
    k.wait("gpsimd", c.sD2, t)
    c.tok_l2w = c.sD2.n
    while True:
        yield


def l2_sen_gen(k, c):
    tg = k.dma("sync", c.F32[0:1, 1024:2048], c.L2gs, c.sW)
    k.wait("sync", c.sW, tg)
    k.wait("tensor", c.sW, tg)
    mm(k, c.ps[4][:, :], c.cONESROW[0:1, 0:128], c.F32[0:1, 1024:1536],
       start=True, stop=True)
    mm(k, c.ps[5][:, :], c.cONESROW[0:1, 0:128], c.F32[0:1, 1536:2048],
       start=True, stop=True)
    tgr = k.inc("tensor", c.sMM)
    k.wait("scalar", c.sMM, tgr)
    copy_op(k, "scalar", c.F32[:, 1024:1536], c.ps[4][:, :])
    tge = copy_op(k, "scalar", c.F32[:, 1536:2048], c.ps[5][:, :], sem=c.sSC)
    k.wait("sync", c.sSC, tge)
    k.wait("vector", c.sSC, tge)
    yield
    tdma_prev = 0
    ttr_prev = 0
    tsc_prev = 0
    tsq_prev = 0
    for nt in range(32):
        k.wait("sync", c.sVE, tsc_prev)
        k.wait("sync", c.sSC, tsq_prev)
        tld = k.dma("sync", c.G32[:, 0:1024], c.L2se[nt * 128:(nt + 1) * 128, :], c.sW)
        k.wait("sync", c.sW, tld)
        k.wait("scalar", c.sW, tld)
        tsq = act_op(k, c.G32[:, 1024:2048], c.G32[:, 0:1024], ACT.Square,
                     accum=c.F32[:, 2048:2049], sem=c.sSC)
        tsq_prev = tsq
        k.wait("scalar", c.sSC, tsq)
        k.drain("scalar")
        tsr = act_op(k, c.F32[:, 2049:2050], c.F32[:, 2048:2049], ACT.Sqrt, sem=c.sSC)
        k.wait("vector", c.sSC, tsr)
        ts_op(k, "vector", c.F32[:, 2050:2051], c.F32[:, 2049:2050], 1e-12, ALU.max)
        k.drain("vector")
        k.raw("vector", lambda e, o=c.F32[:, 2051:2052], i=c.F32[:, 2050:2051]:
              e.reciprocal(o, i))
        k.drain("vector")
        ts_op(k, "vector", c.F32[:, 2051:2052], c.F32[:, 2051:2052], 32.0, ALU.mult)
        k.drain("vector")
        k.wait("vector", c.sMM, ttr_prev)
        tsc = stt_op(k, "vector", c.wbf[1][:, 0:1024], c.G32[:, 0:1024],
                     c.F32[:, 2051:2052], ALU.mult, c.F32[:, 1024:2048], ALU.mult,
                     sem=c.sVE)
        tsc_prev = tsc
        k.wait("tensor", c.sVE, tsc)
        for wv in range(2):
            for dq in range(4):
                dt = wv * 4 + dq
                tr_op(k, c.ps7b[:, dq * 128:(dq + 1) * 128],
                      c.wbf[1][:, dt * 128:(dt + 1) * 128], c.cIDTB[:])
            ttr = k.inc("tensor", c.sMM)
            k.wait("vector", c.sMM, ttr)
            k.wait("vector", c.sW, tdma_prev)
            for dq in range(4):
                dt = wv * 4 + dq
                copy_op(k, "vector", c.wbf[0][:, dt * 128:(dt + 1) * 128],
                        c.ps7b[:, dq * 128:(dq + 1) * 128])
            tev = k.inc("vector", c.sVE)
            k.wait("tensor", c.sVE, tev)
        ttr_prev = c.sMM.n
        k.wait("sync", c.sVE, c.sVE.n)
        tdma_prev = k.dma("sync",
                          c.senTb.rearrange("(a p) n -> p a n", p=128)[:, :, nt * 128:(nt + 1) * 128],
                          c.wbf[0][:, 0:1024].rearrange("p (a b) -> p a b", b=128),
                          c.sW)
        k.wait("sync", c.sW, tdma_prev)
        yield
    c.tok_sen = tdma_prev
    c.tok_sen_ve = c.sVE.n
    while True:
        yield


def l2_attn_heads(k, c):
    pump2 = c.pump2
    tok_mm_prev = c.sMM.n
    tok_sc_prev = c.sSC.n
    tok_ve_prev = c.sVE.n
    tok_s_prev = 0
    for h in range(16):
        base = 64 * (h % 2)
        bsl = slice(base, base + 64)
        k.wait("sync", c.sMM, tok_mm_prev)
        k.wait("sync", c.sSC, tok_sc_prev)
        k.wait("sync", c.sVE, tok_ve_prev)
        k.wait("sync", c.sW, tok_s_prev)
        k.wait("vector", c.sMM, tok_mm_prev)
        k.dma("sync", c.aK[bsl, 0:2048], c.kfms[64 * h:64 * h + 64, :], c.sW)
        k.dma("sync", c.aQ[bsl, 0:2048], c.qfms[64 * h:64 * h + 64, :], c.sW)
        vdst = c.aVtm[:, 0:1040].rearrange("p (j v) -> p j v", v=65)[:, :, 0:64]
        vsrc = c.vtms.rearrange("(n p) o -> p n o", p=128)[:, :, 64 * h:64 * h + 64]
        tv = k.dma("sync", vdst, vsrc, c.sW)
        k.wait("sync", c.sW, tv)
        for eng in ("tensor", "vector", "scalar"):
            k.wait(eng, c.sW, tv)
        tso_prev = 0
        for tch in range(4):
            tsl = slice(tch * 512, (tch + 1) * 512)
            ptok = {}
            tpk_prev = 0
            k.wait("tensor", c.sVE, tso_prev)
            for jt in range(16):
                ent = ptok.get(jt % 2)
                if ent:
                    k.wait("tensor", c.sSC, ent)
                mm(k, c.ps[jt % 2][:, :], c.aK[bsl, jt * 128:(jt + 1) * 128],
                   c.aQ[bsl, tsl], start=True, stop=True)
                tpm = k.inc("tensor", c.sMM)
                k.wait("scalar", c.sMM, tpm)
                k.wait("scalar", c.sMM, ptok.get(f"pv{jt % 2}", 0))
                tpk = act_op(k, c.hA[:, (jt % 2) * 512:(jt % 2) * 512 + 512],
                             c.ps[jt % 2][:, :], ACT.Exp, scale=0.03125, sem=c.sSC)
                ptok[jt % 2] = tpk
                k.wait("tensor", c.sSC, tpk)
                mm(k, c.ps[2][0:65, :], c.aVtm[:, jt * 65:jt * 65 + 65],
                   c.hA[:, (jt % 2) * 512:(jt % 2) * 512 + 512],
                   start=(jt == 0), stop=(jt == 15))
                ptok[f"pv{jt % 2}"] = k.inc("tensor", c.sMM)
            tout = c.sMM.n
            k.wait("scalar", c.sMM, tout)
            tden = act_op(k, c.F32[64:65, 0:512], c.ps[2][64:65, :], ACT.Copy,
                          sem=c.sSC)
            k.wait("vector", c.sSC, tden)
            k.raw("vector", lambda e, o=c.F32[64:65, 512:1024],
                  i=c.F32[64:65, 0:512]: e.reciprocal(o, i))
            trc = k.inc("vector", c.sVE)
            k.wait("tensor", c.sVE, trc)
            mm(k, c.ps[3][:, :], c.cONESROW[64:65, 0:128], c.F32[64:65, 512:1024],
               start=True, stop=True)
            trp = k.inc("tensor", c.sMM)
            k.wait("scalar", c.sMM, trp)
            trv = copy_op(k, "scalar", c.F32[0:64, 0:512], c.ps[3][0:64, :],
                          sem=c.sSC)
            k.wait("vector", c.sSC, trv)
            k.wait("vector", c.sW, c.bank_free.get("l2so", 0))
            tso = tt_op(k, "vector", c.hA[0:64, 1024:1536], c.ps[2][0:64, :],
                        c.F32[0:64, 0:512], ALU.mult, sem=c.sVE)
            tso_prev = tso
            k.wait("sync", c.sVE, tso)
            tsd = k.dma("sync", c.sres[bsl, (h // 2) * T + tch * 512:
                                       (h // 2) * T + (tch + 1) * 512],
                        c.hA[0:64, 1024:1536], c.sW)
            c.bank_free["l2so"] = tsd
            k.wait("sync", c.sW, tsd)
            pump2(1)
        tok_mm_prev = c.sMM.n
        tok_sc_prev = c.sSC.n
        tok_ve_prev = c.sVE.n
        tok_s_prev = c.sW.n
    c.tok_l2pre = tok_s_prev
    while not hasattr(c, "tok_sen"):
        c.pump2(1)
    tbu = k.dma("sync", c.colsB[:, 24:32], c.L2bu, c.sW)
    k.wait("sync", c.sW, tbu)
    k.wait("scalar", c.sW, tbu)
    k.wait("sync", c.sVE, c.tok_sen_ve)
    k.wait("sync", c.sMM, tok_mm_prev)
    k.wait("tensor", c.sVE, tok_ve_prev)
    k.wait("tensor", c.sSC, tok_sc_prev)
    for tc in range(4):
        for og in range(2):
            for dv in range(8):
                bsel = dv % 2
                k.wait("sync", c.sMM, c.bank_free.get(f"l2wu{bsel}", 0))
                tw = k.dma("sync", c.wbf[bsel][:, 0:512],
                           c.WbfA[3, dv * 128:(dv + 1) * 128,
                                  og * 512:(og + 1) * 512], c.sW)
                k.wait("sync", c.sW, tw)
                k.wait("tensor", c.sW, tw)
                if dv == 0 and og == 0:
                    k.wait("tensor", c.sVE, c.bank_free.get("l2wu_ve", 0))
                    k.wait("tensor", c.sSC, c.bank_free.get("l2wu_sc", 0))
                for oo in range(4):
                    mm(k, c.ps[oo][:, :], c.wbf[bsel][:, oo * 128:(oo + 1) * 128],
                       c.sres[:, dv * T + tc * 512: dv * T + (tc + 1) * 512],
                       start=(dv == 0), stop=(dv == 7))
                tmm = k.inc("tensor", c.sMM)
                c.bank_free[f"l2wu{bsel}"] = tmm
            tmm = c.sMM.n
            if og == 0:
                k.wait("vector", c.sMM, tmm)
                for oo in range(4):
                    ts_op(k, "vector", c.F32[:, oo * 512:(oo + 1) * 512],
                          c.ps[oo][:, :], c.colsB[:, 24 + oo:25 + oo], ALU.add)
                    k.drain("vector")
                tstg = k.inc("vector", c.sVE)
                k.wait("tensor", c.sVE, tstg)
        k.wait("vector", c.sMM, tmm)
        for oo in range(4):
            ts_op(k, "vector",
                  c.sres[:, (4 + oo) * T + tc * 512: (4 + oo) * T + (tc + 1) * 512],
                  c.ps[oo][:, :], c.colsB[:, 28 + oo:29 + oo], ALU.add)
            k.drain("vector")
        for oo in range(4):
            tt_op(k, "vector",
                  c.sres[:, oo * T + tc * 512: oo * T + (tc + 1) * 512],
                  c.F32[:, oo * 512:(oo + 1) * 512],
                  c.F32[:, oo * 512:(oo + 1) * 512], ALU.min)
            k.drain("vector")
        tev3 = k.inc("vector", c.sVE)
        k.wait("tensor", c.sVE, tev3)
        c.bank_free["l2wu_ve"] = tev3
        c.bank_free["l2wu_sc"] = 0
    c.tok_l2s = k.inc("vector", c.sVE)
    k.wait("scalar", c.sVE, c.tok_l2s)
    c.tok_l2s_sc = k.inc("scalar", c.sSC)


def l2_xnr(k, c):
    k.wait("sync", c.sVE, c.tok_l2s)
    tgx = k.dma("sync", c.colsB[:, 8:16], c.L2gx, c.sW)
    tcd = k.dma("sync", c.cIDF[:], c.IDTF, c.sW)
    k.wait("sync", c.sW, tcd)
    k.wait("vector", c.sW, tcd)
    k.wait("scalar", c.sVE, c.tok_l2s)
    k.wait("tensor", c.sVE, c.tok_l2s)
    k.wait("scalar", c.sSC, c.tok_l2s_sc)
    evac = lambda i: (c.aQ[:, i * 512:(i + 1) * 512] if i < 4
                      else c.aK[:, (i - 4) * 512:(i - 3) * 512])
    tve_prev = 0
    tw_prev = 0
    trr_prev = 0
    for tc in range(4):
        buf = c.F32[:, 1024:1536]
        for dt in range(8):
            sl = slice(dt * T + tc * 512, dt * T + (tc + 1) * 512)
            if dt == 0:
                k.wait("scalar", c.sVE, trr_prev)
            tsq = act_op(k, buf, c.xnbf[:, sl], ACT.Square, sem=c.sSC)
            k.wait("tensor", c.sSC, tsq)
            mm(k, c.ps[0][0:1, :], c.cols[:, 67:68], buf,
               start=(dt == 0), stop=(dt == 7))
            tm2 = k.inc("tensor", c.sMM)
            if dt < 7:
                k.wait("scalar", c.sMM, tm2)
        tmm = c.sMM.n
        k.wait("scalar", c.sMM, tmm)
        trw = act_op(k, c.F32[0:1, 1536:2048], c.ps[0][0:1, :], ACT.Sqrt, sem=c.sSC)
        k.wait("vector", c.sSC, trw)
        ts_op(k, "vector", c.F32[0:1, 1536:2048], c.F32[0:1, 1536:2048], 1e-12,
              ALU.max)
        k.drain("vector")
        k.raw("vector", lambda e, o=c.F32[0:1, 1024:1536],
              i=c.F32[0:1, 1536:2048]: e.reciprocal(o, i))
        k.drain("vector")
        trr = ts_op(k, "vector", c.F32[0:1, 1536:2048], c.F32[0:1, 1024:1536],
                    32.0, ALU.mult, sem=c.sVE)
        trr_prev = trr
        k.wait("tensor", c.sVE, trr)
        mm(k, c.ps[1][:, :], c.cONESROW[0:1, 0:128], c.F32[0:1, 1536:2048],
           start=True, stop=True)
        trp = k.inc("tensor", c.sMM)
        k.wait("vector", c.sMM, trp)
        for dt in range(8):
            sl = slice(dt * T + tc * 512, dt * T + (tc + 1) * 512)
            tt_op(k, "vector", c.F32[:, 0:512], c.xnbf[:, sl], c.ps[1][:, :],
                  ALU.mult)
            k.drain("vector")
            tvx = ts_op(k, "vector", evac(dt), c.F32[:, 0:512],
                        c.colsB[:, 8 + dt:9 + dt], ALU.mult, sem=c.sVE)
            k.drain("vector")
        tve = c.sVE.n
        k.wait("sync", c.sVE, tve)
        k.wait("tensor", c.sVE, tve)
        for dt in range(8):
            k.dma("sync", c.qfms[dt * 128:(dt + 1) * 128, tc * 512:(tc + 1) * 512],
                  evac(dt), c.sW)
        for tt4 in range(4):
            for wv in range(2):
                for dq in range(4):
                    dt = wv * 4 + dq
                    tr_op(k, c.ps7b[:, dq * 128:(dq + 1) * 128],
                          evac(dt)[:, tt4 * 128:(tt4 + 1) * 128], c.cIDTB[:])
                ttr = k.inc("tensor", c.sMM)
                k.wait("vector", c.sMM, ttr)
                k.wait("vector", c.sW, tw_prev)
                for dq in range(4):
                    dt = wv * 4 + dq
                    copy_op(k, "vector", c.wbf[0][:, dt * 128:(dt + 1) * 128],
                            c.ps7b[:, dq * 128:(dq + 1) * 128])
                tev = k.inc("vector", c.sVE)
                k.wait("tensor", c.sVE, tev)
            k.wait("sync", c.sVE, c.sVE.n)
            tw_prev = k.dma("sync",
                            c.vtms[tc * 512 + tt4 * 128: tc * 512 + (tt4 + 1) * 128, :],
                            c.wbf[0][:, 0:1024], c.sW)
            k.wait("sync", c.sW, tw_prev)
    c.tok_xnr = c.sW.n


def l2_slot_ffn(k, c):
    sslot = lambda i: (c.aK[:, i * 512:(i + 1) * 512] if i < 4 else
                       (c.aKtm[:, (i - 4) * 512:(i - 3) * 512] if i < 6 else
                        c.wbf[0][:, (i - 6) * 512:(i - 5) * 512]))
    hslot2 = lambda i: (c.hA[:, i * 512:(i + 1) * 512] if i < 12
                        else c.aQ[:, (i - 12) * 512:(i - 11) * 512])
    qfv = c.qfms.rearrange("(a p) t -> p a t", p=128)
    snv = c.senTb.rearrange("(a p) n -> p a n", p=128)
    w1v = c.l2w1b.rearrange("(a p) f -> p a f", p=128)
    w2v = c.l2w2b.rearrange("(a p) o -> p a o", p=128)
    stok = {}
    for ng in range(8):
        k.wait("sync", c.sMM, stok.get("sen", 0))
        for dt in range(8):
            tsn = k.dma("sync", sslot(dt),
                        c.senTb[dt * 128:(dt + 1) * 128, ng * 512:(ng + 1) * 512],
                        c.sW)
        k.wait("sync", c.sW, tsn)
        k.wait("tensor", c.sW, tsn)
        for tc in range(4):
            for dt in range(8):
                bs = dt % 2
                k.wait("sync", c.sMM, stok.get(f"x{bs}", 0))
                tx = k.dma("sync", c.wbg[bs][:, 0:512],
                           c.qfms[dt * 128:(dt + 1) * 128, tc * 512:(tc + 1) * 512],
                           c.sW)
                k.wait("sync", c.sW, tx)
                k.wait("tensor", c.sW, tx)
                if dt == 0:
                    k.wait("tensor", c.sVE, stok.get("rm", 0))
                for ni in range(4):
                    mm(k, c.ps[ni][:, :], sslot(dt)[:, ni * 128:(ni + 1) * 128],
                       c.wbg[bs][:, 0:512], start=(dt == 0), stop=(dt == 7))
                stok[f"x{bs}"] = k.inc("tensor", c.sMM)
            tmm = c.sMM.n
            k.wait("vector", c.sMM, tmm)
            for ni in range(4):
                dcol = c.F32[:, 2048 + ng * 4 + ni:2049 + ng * 4 + ni]
                if tc == 0:
                    k.raw("vector", lambda e, o=dcol, i=c.ps[ni][:, :]:
                          e.reduce_max(o, i, axis=AX.X))
                else:
                    k.raw("vector", lambda e, o=c.F32[:, 2080:2081],
                          i=c.ps[ni][:, :]: e.reduce_max(o, i, axis=AX.X))
                    k.drain("vector")
                    tt_op(k, "vector", dcol, dcol, c.F32[:, 2080:2081], ALU.max)
                k.drain("vector")
            stok["rm"] = k.inc("vector", c.sVE)
        stok["sen"] = c.sMM.n
        c.pump(1)
    tng = ts_op(k, "vector", c.F32[:, 2080:2112], c.F32[:, 2048:2080], -1.0,
                ALU.mult, sem=c.sVE)
    k.wait("tensor", c.sVE, tng)
    tdt = tr_op(k, c.ps[0][0:32, 0:128], c.F32[:, 2080:2112], c.cIDF[:], sem=c.sMM)
    k.wait("scalar", c.sMM, tdt)
    tde = copy_op(k, "scalar", c.F32[0:32, 2048:2176], c.ps[0][0:32, 0:128],
                  sem=c.sSC)
    k.wait("sync", c.sSC, tde)
    tdd = k.dma("sync", c.G32[32:64, 0:128], c.F32[0:32, 2048:2176], c.sW)
    k.wait("sync", c.sW, tdd)
    eslot = lambda i: (c.hA[:, i * 512:(i + 1) * 512] if i < 12
                       else c.aQ[:, (i - 12) * 512:(i - 11) * 512])
    vslot = lambda i: (c.aK[:, i * 512:(i + 1) * 512] if i < 4 else
                       (c.aKtm[:, (i - 4) * 512:(i - 3) * 512] if i < 6 else
                        c.wbf[0][:, (i - 6) * 512:(i - 5) * 512]))
    btok = {}
    for n4 in range(8):
        k.wait("sync", c.sMM, btok.get("crw", 0))
        tcr = k.dma("sync", c.crow5[0:1, 0:512].rearrange("p (a b) -> p a b", b=128),
                    c.G32[32 + 4 * n4:32 + 4 * (n4 + 1), 0:128], c.sW)
        k.wait("sync", c.sW, tcr)
        k.wait("tensor", c.sW, tcr)
        k.wait("tensor", c.sSC, btok.get("nrep", 0))
        mm(k, c.ps[2][:, :], c.cONESROW[0:1, 0:128], c.crow5[0:1, 0:512],
           start=True, stop=True)
        btok["crw"] = k.inc("tensor", c.sMM)
        k.wait("scalar", c.sMM, btok["crw"])
        k.wait("scalar", c.sVE, btok.get("ndrep_ve", 0))
        k.wait("scalar", c.sVE, btok.get("tadl", 0))
        tnr = copy_op(k, "scalar", c.F32[:, 1536:2048], c.ps[2][:, :], sem=c.sSC)
        btok["nrep"] = tnr
        k.wait("vector", c.sSC, tnr)
        k.wait("sync", c.sMM, btok.get("sen", 0))
        for dt in range(8):
            tsn = k.dma("sync", sslot(dt),
                        c.senTb[dt * 128:(dt + 1) * 128, n4 * 512:(n4 + 1) * 512],
                        c.sW)
        k.wait("sync", c.sW, tsn)
        k.wait("tensor", c.sW, tsn)
        for tt in range(16):
            bs = tt % 2
            k.wait("sync", c.sMM, btok.get(f"x{bs}", 0))
            tx = k.dma("sync", c.wbg[bs][:, 0:1024].rearrange("p (a b) -> p a b", b=128),
                       qfv[:, :, tt * 128:(tt + 1) * 128], c.sW)
            k.wait("sync", c.sW, tx)
            k.wait("tensor", c.sW, tx)
            pl = c.ps[tt % 2]
            ent = btok.get(f"pl{tt % 2}")
            if ent is not None:
                k.wait("tensor", c.sVE, ent)
            for dt in range(8):
                mm(k, pl[:, :], c.wbg[bs][:, dt * 128:(dt + 1) * 128], sslot(dt),
                   start=(dt == 0), stop=(dt == 7))
            btok[f"x{bs}"] = k.inc("tensor", c.sMM)
            tlg = btok[f"x{bs}"]
            k.wait("vector", c.sMM, tlg)
            cmx = c.F32[:, 2112 + tt:2113 + tt]
            if n4 == 0:
                k.raw("vector", lambda e, o=cmx, i=pl[:, :]:
                      e.reduce_max(o, i, axis=AX.X))
            else:
                k.raw("vector", lambda e, o=c.F32[:, 2080:2081], i=pl[:, :]:
                      e.reduce_max(o, i, axis=AX.X))
                k.drain("vector")
                tt_op(k, "vector", cmx, cmx, c.F32[:, 2080:2081], ALU.max)
            k.drain("vector")
            k.wait("vector", c.sSC, btok.get("exp", 0))
            tad = tt_op(k, "vector", c.F32[:, 1024:1536], pl[:, :],
                        c.F32[:, 1536:2048], ALU.add, sem=c.sVE)
            btok[f"pl{tt % 2}"] = tad
            btok["tadl"] = tad
            k.wait("scalar", c.sVE, tad)
            k.wait("scalar", c.sMM, btok.get(f"ekv{tt}", 0))
            tex = act_op(k, eslot(tt), c.F32[:, 1024:1536], ACT.Exp, sem=c.sSC)
            btok["exp"] = tex
            k.wait("tensor", c.sSC, tex)
            mm(k, c.ps[4][0:1, :], c.cONESCOLB[:], eslot(tt),
               start=(tt == 0), stop=(tt == 15))
            k.inc("tensor", c.sMM)
        tdn = c.sMM.n
        k.wait("scalar", c.sMM, tdn)
        tdre = act_op(k, c.F32[0:1, 512:1024], c.ps[4][0:1, :], ACT.Copy, sem=c.sSC)
        k.wait("vector", c.sSC, tdre)
        k.raw("vector", lambda e, o=c.F32[0:1, 0:512], i=c.F32[0:1, 512:1024]:
              e.reciprocal(o, i))
        trc = k.inc("vector", c.sVE)
        k.wait("tensor", c.sVE, trc)
        mm(k, c.ps[5][:, :], c.cONESROW[0:1, 0:128], c.F32[0:1, 0:512],
           start=True, stop=True)
        trp = k.inc("tensor", c.sMM)
        k.wait("scalar", c.sMM, trp)
        tdv = copy_op(k, "scalar", c.F32[:, 512:1024], c.ps[5][:, :], sem=c.sSC)
        k.wait("vector", c.sSC, tdv)
        for tt in range(16):
            bs = tt % 2
            k.wait("sync", c.sMM, btok.get(f"x{bs}", 0))
            tx = k.dma("sync", c.wbg[bs][:, 0:1024],
                       c.vtms[tt * 128:(tt + 1) * 128, :], c.sW)
            k.wait("sync", c.sW, tx)
            k.wait("tensor", c.sW, tx)
            if tt == 0:
                k.wait("tensor", c.sSC, tdv)
                k.wait("tensor", c.sVE, btok.get("sev", 0))
            for dt2 in range(7):
                mm(k, c.ps[dt2][:, :], c.wbg[bs][:, dt2 * 128:(dt2 + 1) * 128],
                   eslot(tt), start=(tt == 0), stop=(tt == 15))
            btok[f"x{bs}"] = k.inc("tensor", c.sMM)
        tsl1 = c.sMM.n
        k.wait("vector", c.sMM, tsl1)
        for dt2 in range(7):
            tt_op(k, "vector", vslot(dt2), c.ps[dt2][:, :], c.F32[:, 512:1024],
                  ALU.mult)
            k.drain("vector")
        tse = k.inc("vector", c.sVE)
        k.wait("tensor", c.sVE, tse)
        for tt in range(16):
            bs = tt % 2
            tx = k.dma("sync", c.bfT[:, (bs) * 128:(bs + 1) * 128],
                       c.vtms[tt * 128:(tt + 1) * 128, 896:1024], c.sW)
            k.wait("sync", c.sW, tx)
            k.wait("tensor", c.sW, tx)
            mm(k, c.ps[0][:, :], c.bfT[:, bs * 128:(bs + 1) * 128], eslot(tt),
               start=(tt == 0), stop=(tt == 15))
            tmb = k.inc("tensor", c.sMM)
            if tt < 15:
                k.wait("sync", c.sMM, tmb)
        tsl2 = c.sMM.n
        k.wait("vector", c.sMM, tsl2)
        tt_op(k, "vector", vslot(7), c.ps[0][:, :], c.F32[:, 512:1024], ALU.mult)
        k.drain("vector")
        tse2 = k.inc("vector", c.sVE)
        btok["sev"] = tse2
        btok["ndrep_ve"] = tse2
        btok["pl0"] = tse2
        btok["pl1"] = tse2
        k.wait("sync", c.sVE, tse2)
        for dt2 in range(8):
            tsw = k.dma("sync",
                        c.slotsTb[dt2 * 128:(dt2 + 1) * 128, n4 * 512:(n4 + 1) * 512],
                        vslot(dt2), c.sW)
        k.wait("sync", c.sW, tsw)
        for tt in range(16):
            btok[f"ekv{tt}"] = tsl2
        btok["sen"] = tsl2
        c.pump(1)
    k.wait("vector", c.sMM, c.sMM.n)
    tng = ts_op(k, "vector", c.F32[:, 2080:2096], c.F32[:, 2112:2128], -1.0,
                ALU.mult, sem=c.sVE)
    k.wait("tensor", c.sVE, tng)
    tdt = tr_op(k, c.ps[0][0:16, 0:128], c.F32[:, 2080:2096], c.cIDF[:], sem=c.sMM)
    k.wait("scalar", c.sMM, tdt)
    tde = copy_op(k, "scalar", c.F32[0:16, 2048:2176], c.ps[0][0:16, 0:128],
                  sem=c.sSC)
    k.wait("sync", c.sSC, tde)
    tcd = k.dma("sync", c.Tst[16:32, 0:128], c.F32[0:16, 2048:2176], c.sW)
    k.wait("sync", c.sW, tcd)
    ftok = {}
    pump_until(c, "tok_l2w")
    k.wait("sync", c.sD2, c.tok_l2w)
    for n4 in range(8):
        k.wait("sync", c.sMM, ftok.get("sl", 0))
        for dt in range(8):
            dst = (c.aK[:, dt * 512:(dt + 1) * 512] if dt < 4 else
                   (c.aKtm[:, (dt - 4) * 512:(dt - 3) * 512] if dt < 6 else
                    c.aVtm[:, (dt - 6) * 512:(dt - 5) * 512]))
            tsl = k.dma("sync", dst,
                        c.slotsTb[dt * 128:(dt + 1) * 128, n4 * 512:(n4 + 1) * 512],
                        c.sW)
        k.wait("sync", c.sW, tsl)
        k.wait("tensor", c.sW, tsl)
        slsl = lambda dt: (c.aK[:, dt * 512:(dt + 1) * 512] if dt < 4 else
                           (c.aKtm[:, (dt - 4) * 512:(dt - 3) * 512] if dt < 6 else
                            c.aVtm[:, (dt - 6) * 512:(dt - 5) * 512]))
        for fh in range(2):
            for key in [f"pa{cb}" for cb in range(7)] + ["pa7"]:
                ent = ftok.get(key)
                if ent is not None:
                    k.wait("tensor", ent[0], ent[1])
            for ft in range(16):
                bs = ft % 2
                k.wait("sync", c.sMM, ftok.get(f"w{bs}", 0))
                tw = k.dma("sync", c.wbg[bs][:, 0:1024].rearrange("p (a b) -> p a b", b=128),
                           w1v[:, :, fh * 2048 + ft * 128: fh * 2048 + (ft + 1) * 128],
                           c.sW)
                k.wait("sync", c.sW, tw)
                k.wait("tensor", c.sW, tw)
                ph = c.ps[ft % 2]
                k.wait("tensor", c.sSC, ftok.get(f"g{ft % 2}", 0))
                k.wait("tensor", c.sMM, ftok.get(f"hs{ft}", 0))
                for dt in range(8):
                    mm(k, ph[:, :], c.wbg[bs][:, dt * 128:(dt + 1) * 128], slsl(dt),
                       start=(dt == 0), stop=(dt == 7))
                ftok[f"w{bs}"] = k.inc("tensor", c.sMM)
                k.wait("scalar", c.sMM, ftok[f"w{bs}"])
                tg = act_op(k, hslot2(ft), ph[:, :], ACT.Gelu, sem=c.sSC)
                ftok[f"g{ft % 2}"] = tg
            thp = k.inc("scalar", c.sSC)
            k.wait("tensor", c.sSC, thp)
            for ft in range(16):
                bs = ft % 2
                k.wait("sync", c.sMM, ftok.get(f"w{bs}", 0))
                tw = k.dma("sync", c.wbg[bs][:, 0:1024],
                           c.l2w2b[fh * 2048 + ft * 128: fh * 2048 + (ft + 1) * 128, :],
                           c.sW)
                k.wait("sync", c.sW, tw)
                k.wait("tensor", c.sW, tw)
                if ft == 0:
                    for cb in range(7):
                        ent = ftok.get(f"pa{cb}")
                        if ent is not None:
                            k.wait("tensor", ent[0], ent[1])
                for cb in range(7):
                    nf, dh = cb // 2, cb % 2
                    mm(k, c.ps[cb][:, :], hslot2(ft)[:, nf * 128:(nf + 1) * 128],
                       c.wbg[bs][:, dh * 512:(dh + 1) * 512],
                       start=(ft == 0), stop=(ft == 15))
                ftok[f"w{bs}"] = k.inc("tensor", c.sMM)
            tpr = c.sMM.n
            for cb in range(7):
                nf, dh = cb // 2, cb % 2
                accsl = (c.G32[:, cb * 512:(cb + 1) * 512] if cb < 4
                         else c.F32[:, (cb - 4) * 512:(cb - 3) * 512])
                if fh == 0:
                    k.wait("scalar", c.sMM, tpr)
                    tev = copy_op(k, "scalar", accsl, c.ps[cb][:, :], sem=c.sSC)
                    ftok[f"pa{cb}"] = (c.sSC, tev)
                else:
                    k.wait("vector", c.sMM, tpr)
                    k.wait("vector", c.sW, ftok.get("pw", 0))
                    tev = tt_op(k, "vector", c.aVtm[:, 1024:1536], c.ps[cb][:, :],
                                accsl, ALU.add, sem=c.sVE)
                    ftok[f"pa{cb}"] = (c.sVE, tev)
                    k.wait("sync", c.sVE, tev)
                    tpw = k.dma("sync",
                                c.procb[n4 * 512 + nf * 128: n4 * 512 + (nf + 1) * 128,
                                        dh * 512:(dh + 1) * 512],
                                c.aVtm[:, 1024:1536], c.sW)
                    k.wait("sync", c.sW, tpw)
                    ftok["pw"] = tpw
            for ft in range(16):
                bs = ft % 2
                k.wait("sync", c.sMM, ftok.get(f"w{bs}", 0))
                tw = k.dma("sync", c.wbg[bs][:, 0:512],
                           c.l2w2b[fh * 2048 + ft * 128: fh * 2048 + (ft + 1) * 128,
                                   512:1024], c.sW)
                k.wait("sync", c.sW, tw)
                k.wait("tensor", c.sW, tw)
                if ft == 0:
                    for key in ("pa7", "pa0"):
                        ent = ftok.get(key)
                        if ent is not None:
                            k.wait("tensor", ent[0], ent[1])
                    k.wait("tensor", c.sSC, ftok.get("pa7s", 0))
                mm(k, c.ps[0][:, :], hslot2(ft)[:, 384:512], c.wbg[bs][:, 0:512],
                   start=(ft == 0), stop=(ft == 15))
                ftok[f"w{bs}"] = k.inc("tensor", c.sMM)
            tp7 = c.sMM.n
            if fh == 0:
                k.wait("scalar", c.sMM, tp7)
                tev = copy_op(k, "scalar", c.F32[:, 1536:2048], c.ps[0][:, :],
                              sem=c.sSC)
                ftok["pa7"] = (c.sSC, tev)
                ftok["pa7s"] = tev
            else:
                k.wait("vector", c.sMM, tp7)
                k.wait("vector", c.sW, ftok.get("pw", 0))
                tev = tt_op(k, "vector", c.aVtm[:, 1024:1536], c.ps[0][:, :],
                            c.F32[:, 1536:2048], ALU.add, sem=c.sVE)
                ftok["pa7"] = (c.sVE, tev)
                k.wait("sync", c.sVE, tev)
                tpw = k.dma("sync",
                            c.procb[n4 * 512 + 384: n4 * 512 + 512, 512:1024],
                            c.aVtm[:, 1024:1536], c.sW)
                k.wait("sync", c.sW, tpw)
                ftok["pw"] = tpw
            ftok["hs0"] = 0
        ftok["sl"] = c.sMM.n
        c.pump(1)
    c.tok_proc = ftok.get("pw", 0)
    ctok = {}
    for tc in range(4):
        k.wait("sync", c.sMM, ctok.get("crw", 0))
        tcr = k.dma("sync", c.crow5[0:1, 0:512].rearrange("p (a b) -> p a b", b=128),
                    c.Tst[16 + 4 * tc:16 + 4 * (tc + 1), 0:128], c.sW)
        k.wait("sync", c.sW, tcr)
        k.wait("tensor", c.sW, tcr)
        for nt4 in range(8):
            k.wait("sync", c.sMM, ctok.get("sen", 0))
            for dt in range(8):
                tsn = k.dma("sync", sslot(dt),
                            c.senTb[dt * 128:(dt + 1) * 128, nt4 * 512:(nt4 + 1) * 512],
                            c.sW)
            k.wait("sync", c.sW, tsn)
            k.wait("tensor", c.sW, tsn)
            for dt in range(8):
                bs = dt % 2
                k.wait("sync", c.sMM, ctok.get(f"x{bs}", 0))
                tx = k.dma("sync", c.wbg[bs][:, 0:512],
                           c.qfms[dt * 128:(dt + 1) * 128, tc * 512:(tc + 1) * 512],
                           c.sW)
                k.wait("sync", c.sW, tx)
                k.wait("tensor", c.sW, tx)
                if dt == 0:
                    for ni in range(4):
                        k.wait("tensor", c.sSC, ctok.get(f"es{ni}", 0))
                for ni in range(4):
                    mm(k, c.ps[ni][:, :], sslot(dt)[:, ni * 128:(ni + 1) * 128],
                       c.wbg[bs][:, 0:512], start=(dt == 0), stop=False)
                ctok[f"x{bs}"] = k.inc("tensor", c.sMM)
            for ni in range(4):
                mm(k, c.ps[ni][:, :], c.cONESROW[0:1, 0:128], c.crow5[0:1, 0:512],
                   start=False, stop=True)
            tlg = k.inc("tensor", c.sMM)
            k.wait("scalar", c.sMM, tlg)
            for ni in range(4):
                k.wait("scalar", c.sMM, ctok.get(f"ed{ni}", 0))
                k.wait("scalar", c.sW, ctok.get(f"ew{ni}", 0))
                tex = act_op(k, c.hA[:, ni * 512:(ni + 1) * 512], c.ps[ni][:, :],
                             ACT.Exp, sem=c.sSC)
                ctok[f"es{ni}"] = tex
                k.wait("tensor", c.sSC, tex)
                k.wait("sync", c.sSC, tex)
                mm(k, c.ps[4][0:1, :], c.cONESCOLB[:], c.hA[:, ni * 512:(ni + 1) * 512],
                   start=(nt4 == 0 and ni == 0), stop=(nt4 == 7 and ni == 3))
                ctok[f"ed{ni}"] = k.inc("tensor", c.sMM)
                ctok[f"e{ni}"] = ctok[f"ed{ni}"]
                tew = k.dma("sync",
                            c.expb[(nt4 * 4 + ni) * 128:(nt4 * 4 + ni + 1) * 128,
                                   tc * 512:(tc + 1) * 512],
                            c.hA[:, ni * 512:(ni + 1) * 512], c.sW)
            k.wait("sync", c.sW, tew)
            for ni in range(4):
                ctok[f"ew{ni}"] = tew
            ctok["sen"] = c.sMM.n
            c.pump(1)
        tdn = c.sMM.n
        k.wait("scalar", c.sMM, tdn)
        tce = act_op(k, c.G32[0:1, tc * 512:(tc + 1) * 512], c.ps[4][0:1, :],
                     ACT.Copy, sem=c.sSC)
        k.wait("sync", c.sSC, tce)
        k.wait("vector", c.sSC, tce)
        ctok["crw"] = tdn
    c.tok_expb = c.sW.n
    k.wait("vector", c.sSC, c.tok_l2s_sc)
    dtok = {}
    for d2t in range(8):
        for nt in range(32):
            bs = nt % 2
            k.wait("sync", c.sMM, dtok.get(f"p{bs}", 0))
            k.dma("sync", c.bfT[:, bs * 128:(bs + 1) * 128],
                  c.procb[nt * 128:(nt + 1) * 128, d2t * 128:(d2t + 1) * 128], c.sW)
            tpe = k.dma("sync", c.hA[:, bs * 2048:(bs + 1) * 2048],
                        c.expb[nt * 128:(nt + 1) * 128, :], c.sW)
            k.wait("sync", c.sW, tpe)
            k.wait("tensor", c.sW, tpe)
            if nt == 0:
                for tc in range(4):
                    k.wait("tensor", c.sVE, dtok.get(f"fin{tc}", 0))
            for tc in range(4):
                mm(k, c.ps[tc][:, :], c.bfT[:, bs * 128:(bs + 1) * 128],
                   c.hA[:, bs * 2048 + tc * 512: bs * 2048 + (tc + 1) * 512],
                   start=(nt == 0), stop=(nt == 31))
            dtok[f"p{bs}"] = k.inc("tensor", c.sMM)
        tnm = c.sMM.n
        for tc in range(4):
            k.wait("vector", c.sMM, dtok.get("rcm", 0))
            k.raw("vector", lambda e, o=c.crow5[0:1, 0:512],
                  i=c.G32[0:1, tc * 512:(tc + 1) * 512]: e.reciprocal(o, i))
            trc = k.inc("vector", c.sVE)
            k.wait("tensor", c.sVE, trc)
            k.wait("tensor", c.sSC, dtok.get("rev", 0))
            mm(k, c.ps[5][:, :], c.cONESROW[0:1, 0:128], c.crow5[0:1, 0:512],
               start=True, stop=True)
            trp = k.inc("tensor", c.sMM)
            dtok["rcm"] = trp
            k.wait("scalar", c.sMM, trp)
            k.wait("scalar", c.sMM, tnm)
            k.wait("scalar", c.sVE, dtok.get("fv", 0))
            trv = copy_op(k, "scalar", c.F32[:, 512:1024], c.ps[5][:, :], sem=c.sSC)
            dtok["rev"] = trv
            k.wait("vector", c.sSC, trv)
            tmu = tt_op(k, "vector", c.F32[:, 1024:1536], c.ps[tc][:, :],
                        c.F32[:, 512:1024], ALU.mult, sem=c.sVE)
            dtok[f"fin{tc}"] = tmu
            k.wait("scalar", c.sVE, tmu)
            tsi = act_op(k, c.hA[:, 4096 + (tc % 2) * 512: 4096 + (tc % 2 + 1) * 512],
                         c.F32[:, 1024:1536], ACT.Silu, sem=c.sSC)
            k.wait("vector", c.sSC, tsi)
            sl = slice(d2t * T + tc * 512, d2t * T + (tc + 1) * 512)
            tt_op(k, "vector", c.F32[:, 1536:2048], c.sres[:, sl],
                  c.hA[:, 4096 + (tc % 2) * 512: 4096 + (tc % 2 + 1) * 512], ALU.mult)
            k.drain("vector")
            tcx = tt_op(k, "vector", c.xres[:, sl], c.xres[:, sl],
                        c.F32[:, 1536:2048], ALU.add, sem=c.sVE)
            dtok["fv"] = tcx
        c.pump(1)
    c.tok_l2 = k.inc("vector", c.sVE)



def l3_cast_gen(k, c):
    t = None
    for dst, wsrc in ((c.l3wi, c.L3WinT), (c.l3wo, c.L3WoutT), (c.l3wx, c.L3WxT),
                      (c.l3wdt, c.L3WdtT), (c.l3gw, c.L3gwT),
                      (c.l3sg, c.L3sgT), (c.l3su, c.L3suT), (c.l3sd, c.L3sdT)):
        t = k.dma("gpsimd", dst, wsrc, c.sD2)
        yield
    k.wait("gpsimd", c.sD2, t)
    c.tok_l3pre = c.sD2.n
    yield
    for e in range(32):
        k.dma("gpsimd", c.l3eg[e], c.L3egT[e], c.sD2)
        k.dma("gpsimd", c.l3eu[e], c.L3euT[e], c.sD2)
        t = k.dma("gpsimd", c.l3ed[e], c.L3edT[e], c.sD2)
        k.wait("gpsimd", c.sD2, t)
        setattr(c, f"tok_l3e{e}", c.sD2.n)
        yield
    while True:
        yield


def l3_pre(k, c):
    pump_until(c, "tok_l3pre")
    k.wait("sync", c.sD2, c.tok_l3pre)
    k.wait("sync", c.sVE, c.tok_ln)
    k.wait("tensor", c.sVE, c.tok_ln)
    tcl = k.dma("sync", c.colsC[:, 0:64], c.L3cw, c.sW)
    k.dma("sync", c.colsC[:, 64:80], c.L3D, c.sW)
    k.dma("sync", c.colsB[:, 0:16], c.L3bdt, c.sW)
    tcb = k.dma("sync", c.colsB[:, 16:32], c.L3cb, c.sW)
    k.wait("sync", c.sW, tcb)
    k.wait("scalar", c.sW, tcb)
    k.wait("vector", c.sW, tcb)
    wtok = {}
    tsc_prev = 0
    twq_prev = 0
    evac = lambda i: c.aQ[:, i * 512:(i + 1) * 512]
    wiv = c.l3wi.rearrange("d (a b) -> d a b", b=128)
    for tc in range(4):
        for ow in range(8):
            for dt in range(8):
                bsel = dt % 2
                k.wait("sync", c.sMM, wtok.get(bsel, 0))
                tw = k.dma("sync", c.wbf[bsel][:, 0:512],
                           c.l3wi[dt * 128:(dt + 1) * 128, ow * 512:(ow + 1) * 512],
                           c.sW)
                k.wait("sync", c.sW, tw)
                k.wait("tensor", c.sW, tw)
                if dt == 0:
                    k.wait("tensor", c.sSC, tsc_prev)
                for oi in range(4):
                    mm(k, c.ps[oi][:, :], c.wbf[bsel][:, oi * 128:(oi + 1) * 128],
                       c.xnbf[:, dt * T + tc * 512: dt * T + (tc + 1) * 512],
                       start=(dt == 0), stop=(dt == 7))
                wtok[bsel] = k.inc("tensor", c.sMM)
            tmm = c.sMM.n
            k.wait("scalar", c.sMM, tmm)
            k.wait("scalar", c.sW, twq_prev)
            for oi in range(4):
                copy_op(k, "scalar", evac(oi), c.ps[oi][:, :], sem=c.sSC)
            tsc_prev = c.sSC.n
            k.wait("sync", c.sSC, tsc_prev)
            for oi in range(4):
                k.dma("sync", c.uzb[(ow * 4 + oi) * 128:(ow * 4 + oi + 1) * 128,
                                    tc * 512:(tc + 1) * 512], evac(oi), c.sW)
            twq_prev = c.sW.n
            k.wait("sync", c.sW, twq_prev)
            c.pump(1)
    k.wait("tensor", c.sSC, tsc_prev)
    k.wait("gpsimd", c.sW, twq_prev)
    ctok = {}
    for cht in range(16):
        k.wait("sync", c.sMM, ctok.get("ucmm", 0))
        twx = k.dma("sync", c.mwb[:, 0:96], c.l3wx[cht * 128:(cht + 1) * 128, :],
                    c.sW)
        k.wait("sync", c.sW, twx)
        k.wait("tensor", c.sW, twx)
        for tc in range(4):
            k.wait("gpsimd", c.sVE, ctok.get("pad", 0))
            if tc == 0:
                k.raw("vector", lambda e: e.memset(c.G32[:, 0:3], 0.0))
                tmz = k.inc("vector", c.sVE)
                k.wait("gpsimd", c.sVE, tmz)
                tpd = k.dma("gpsimd", c.G32[:, 3:515],
                            c.uzb[cht * 128:(cht + 1) * 128, 0:512], c.sD3)
            else:
                tpd = k.dma("gpsimd", c.G32[:, 0:515],
                            c.uzb[cht * 128:(cht + 1) * 128, tc * 512 - 3:(tc + 1) * 512],
                            c.sD3)
            k.wait("gpsimd", c.sD3, tpd)
            tpg = k.inc("gpsimd", c.sGP)
            k.wait("vector", c.sGP, tpg)
            k.wait("vector", c.sSC, ctok.get("silu", 0))
            ts_op(k, "vector", c.F32[:, 0:512], c.G32[:, 0:512],
                  c.colsC[:, cht * 4:cht * 4 + 1], ALU.mult)
            k.drain("vector")
            for kk in range(1, 4):
                stt_op(k, "vector", c.F32[:, 0:512], c.G32[:, kk:kk + 512],
                       c.colsC[:, cht * 4 + kk:cht * 4 + kk + 1], ALU.mult,
                       c.F32[:, 0:512], ALU.add)
                k.drain("vector")
            tcv = k.inc("vector", c.sVE)
            ctok["pad"] = tcv
            k.wait("scalar", c.sVE, tcv)
            k.wait("scalar", c.sMM, ctok.get("ucmm", 0))
            k.wait("scalar", c.sW, ctok.get("ucw", 0))
            tsl = act_op(k, c.wbf[0][:, 0:512], c.F32[:, 0:512], ACT.Silu,
                         bias=c.colsB[:, 16 + cht:17 + cht], sem=c.sSC)
            ctok["silu"] = tsl
            k.wait("sync", c.sSC, tsl)
            k.wait("tensor", c.sSC, tsl)
            ctok["ucw"] = k.dma("sync", c.ucfb[cht * 128:(cht + 1) * 128,
                                               tc * 512:(tc + 1) * 512],
                                c.wbf[0][:, 0:512], c.sW)
            k.wait("sync", c.sW, ctok["ucw"])
            mm(k, c.ps[tc][0:96, :], c.mwb[:, 0:96], c.wbf[0][:, 0:512],
               start=(cht == 0), stop=(cht == 15))
            ctok["ucmm"] = k.inc("tensor", c.sMM)
    tdbl = c.sMM.n
    k.wait("scalar", c.sMM, tdbl)
    k.wait("sync", c.sW, ctok["ucw"])
    for tc in range(4):
        copy_op(k, "scalar", c.aK[0:64, tc * 512:(tc + 1) * 512],
                c.ps[tc][0:64, :])
        tbe = copy_op(k, "scalar", c.F32[64:96, tc * 512:(tc + 1) * 512],
                      c.ps[tc][64:96, :], sem=c.sSC)
    k.wait("sync", c.sSC, tbe)
    tbc = k.dma("sync", c.bcd, c.F32[64:96, 0:2048], c.sW)
    k.wait("sync", c.sW, tbc)
    k.wait("tensor", c.sSC, tbe)
    c.tok_l3bc = tbc
    c.tok_l3dtr = tbe
    c.tok_l3w = c.sW.n


def l3_mamba_gen(k, c):
    g = {}
    k.wait("gpsimd", c.sSC, c.tok_l3dtr)
    k.wait("gpsimd", c.sW, c.tok_l3w)
    for cht in range(16):
        k.wait("gpsimd", c.sVE, g.get("dtu", 0))
        k.wait("gpsimd", c.sVE, g.get("y3v", 0))
        k.wait("gpsimd", c.sMM, g.get("rep", 0))
        k.wait("gpsimd", c.sSC, g.get("lastda", 0))
        tuc = k.dma("gpsimd", c.aVtm[:, 0:2048],
                    c.ucfb[cht * 128:(cht + 1) * 128, :], c.sD3)
        k.dma("gpsimd", c.cA[:, 0:16], c.L3AL[:, cht * 16:(cht + 1) * 16], c.sD3)
        tal = k.dma("gpsimd", c.mwb[0:64, 96:224],
                    c.l3wdt[0:64, cht * 128:(cht + 1) * 128], c.sD3)
        k.wait("gpsimd", c.sD3, tal)
        tgb = k.inc("gpsimd", c.sGP)
        k.wait("scalar", c.sGP, tgb)
        k.wait("scalar", c.sVE, g.get("negA", 0))
        tae = act_op(k, c.cA[:, 16:32], c.cA[:, 0:16], ACT.Exp, sem=c.sSC)
        k.wait("vector", c.sSC, tae)
        tan = ts_op(k, "vector", c.cA[:, 16:32], c.cA[:, 16:32], -1.0, ALU.mult,
                    sem=c.sVE)
        g["negA"] = tan
        k.wait("scalar", c.sVE, tan)
        k.wait("tensor", c.sGP, tgb)
        yield
        for tc in range(4):
            k.wait("tensor", c.sVE, g.get("dtmm", 0))
            k.wait("tensor", c.sVE, g.get("wtt", 0))
            k.wait("tensor", c.sVE, g.get("scan", 0))
            k.wait("tensor", c.sSC, g.get("dtsc", 0))
            mm(k, c.ps[5][:, :], c.mwb[0:64, 96:224],
               c.aK[0:64, tc * 512:(tc + 1) * 512], start=True, stop=True)
            tdm = k.inc("tensor", c.sMM)
            k.wait("scalar", c.sMM, tdm)
            k.wait("scalar", c.sVE, g.get("t1", 0))
            fsl = c.F32[:, tc * 512:(tc + 1) * 512]
            act_op(k, fsl, c.ps[5][:, :], ACT.Relu, bias=c.colsB[:, cht:cht + 1])
            k.drain("scalar")
            act_op(k, c.M32[:, 0:512], c.ps[5][:, :], ACT.Abs,
                   bias=c.colsB[:, cht:cht + 1])
            k.drain("scalar")
            act_op(k, c.M32[:, 512:1024], c.M32[:, 0:512], ACT.Exp, scale=-1.0)
            k.drain("scalar")
            tsp = act_op(k, c.M32[:, 0:512], c.M32[:, 512:1024], ACT.Ln,
                         bias=1.0, sem=c.sSC)
            g["dtsc"] = tsp
            k.wait("vector", c.sSC, tsp)
            k.wait("vector", c.sMM, tdm)
            tt_op(k, "vector", fsl, fsl, c.M32[:, 0:512], ALU.add)
            k.drain("vector")
            tdt = tt_op(k, "vector", c.aQ[:, tc * 512:(tc + 1) * 512], fsl,
                        c.aVtm[:, tc * 512:(tc + 1) * 512], ALU.mult, sem=c.sVE)
            g["dtmm"] = tdt
            g["dtu"] = tdt
            k.wait("scalar", c.sVE, tdt)
        k.wait("gpsimd", c.sVE, g.get("dtu", 0))
        if c.debug:
            k.dma("gpsimd", c.dbg_dt[cht * 128:(cht + 1) * 128, :],
                  c.F32[:, 0:2048], c.sD3)
        yield
        for s in range(16):
            k.wait("gpsimd", c.sMM, g.get("rep", 0))
            tbh = k.dma("gpsimd", c.brow[0:1, 0:1024], c.bcd[s, 0:1024], c.sD3)
            tch_ = k.dma("gpsimd", c.crow[0:1, 0:1024], c.bcd[16 + s, 0:1024], c.sD3)
            k.wait("gpsimd", c.sD3, tch_)
            tgb = k.inc("gpsimd", c.sGP)
            k.wait("tensor", c.sGP, tgb)
            for tc in range(4):
                if tc == 2:
                    k.wait("gpsimd", c.sMM, g.get("rep", 0))
                    tbh = k.dma("gpsimd", c.brow[0:1, 0:1024], c.bcd[s, 1024:2048],
                                c.sD3)
                    tch_ = k.dma("gpsimd", c.crow[0:1, 0:1024],
                                 c.bcd[16 + s, 1024:2048], c.sD3)
                    k.wait("gpsimd", c.sD3, tch_)
                    tgb = k.inc("gpsimd", c.sGP)
                    k.wait("tensor", c.sGP, tgb)
                k.wait("tensor", c.sVE, g.get("wtt", 0))
                if s == 0 and tc == 0:
                    k.wait("tensor", c.sSC, g.get("dtsc", 0))
                mm(k, c.ps[5][:, :], c.cONESROW[0:1, 0:128],
                   c.brow[0:1, (tc % 2) * 512:(tc % 2) * 512 + 512],
                   start=True, stop=True)
                k.wait("tensor", c.sVE, g.get("t1", 0))
                mm(k, c.ps[6][:, :], c.cONESROW[0:1, 0:128],
                   c.crow[0:1, (tc % 2) * 512:(tc % 2) * 512 + 512],
                   start=True, stop=True)
                trep = k.inc("tensor", c.sMM)
                g["rep"] = trep
                k.wait("scalar", c.sVE, g.get("scan", 0))
                k.wait("scalar", c.sVE, g.get("t1", 0))
                tda = act_op(k, c.Tst[:, 0:512], c.F32[:, tc * 512:(tc + 1) * 512],
                             ACT.Exp, scale=c.cA[:, 16 + s:17 + s], sem=c.sSC)
                g["lastda"] = tda
                if s == 0:
                    k.wait("gpsimd", c.sSC, tda)
                    if c.debug:
                        tdbg = k.dma("gpsimd", c.dbg_da[cht * 128:(cht + 1) * 128,
                                                        tc * 512:(tc + 1) * 512],
                                     c.Tst[:, 0:512], c.sD3)
                        k.wait("gpsimd", c.sD3, tdbg)
                    tgd = k.inc("gpsimd", c.sGP)
                    k.wait("vector", c.sGP, tgd)
                k.wait("vector", c.sMM, trep)
                twt = tt_op(k, "vector", c.M32[:, 0:512],
                            c.aQ[:, tc * 512:(tc + 1) * 512], c.ps[5][:, :],
                            ALU.mult, sem=c.sVE)
                g["wtt"] = twt
                k.wait("vector", c.sSC, tda)
                init = 0.0 if tc == 0 else c.M32[:, 1024:1025]
                k.raw("vector", lambda e, o=c.M32[:, 512:1024], d0=c.Tst[:, 0:512],
                      d1=c.M32[:, 0:512], init=init:
                      e.tensor_tensor_scan(o, d0, d1, init, ALU.mult, ALU.add))
                k.drain("vector")
                copy_op(k, "vector", c.M32[:, 1024:1025], c.M32[:, 1023:1024])
                k.drain("vector")
                tsc2 = k.inc("vector", c.sVE)
                g["scan"] = tsc2
                tt_op(k, "vector", c.Tst[:, 0:512], c.M32[:, 512:1024],
                      c.ps[6][:, :], ALU.mult)
                k.drain("vector")
                gy = c.G32[:, tc * 512:(tc + 1) * 512]
                if s == 0:
                    t1 = copy_op(k, "vector", gy, c.Tst[:, 0:512], sem=c.sVE)
                else:
                    t1 = tt_op(k, "vector", gy, gy, c.Tst[:, 0:512], ALU.add,
                               sem=c.sVE)
                g["t1"] = t1
                g["t1v"] = t1
            yield
        k.wait("gpsimd", c.sVE, g.get("t1", 0))
        if c.debug:
            k.dma("gpsimd", c.dbg_ys[cht * 128:(cht + 1) * 128, :],
                  c.G32[:, 0:2048], c.sD3)
        for tc in range(4):
            k.wait("gpsimd", c.sSC, g.get("zsl", 0))
            tzg = k.dma("gpsimd", c.M32[:, 0:512],
                        c.uzb[2048 + cht * 128: 2048 + (cht + 1) * 128,
                              tc * 512:(tc + 1) * 512], c.sD3)
            k.wait("gpsimd", c.sD3, tzg)
            tgz = k.inc("gpsimd", c.sGP)
            k.wait("scalar", c.sGP, tgz)
            k.wait("scalar", c.sVE, g.get("t1", 0))
            k.wait("scalar", c.sVE, g.get("y3v", 0))
            tzs = act_op(k, c.M32[:, 512:1024], c.M32[:, 0:512], ACT.Silu,
                         sem=c.sSC)
            g["zsl"] = tzs
            k.wait("vector", c.sSC, tzs)
            gy = c.G32[:, tc * 512:(tc + 1) * 512]
            stt_op(k, "vector", gy, c.aVtm[:, tc * 512:(tc + 1) * 512],
                   c.colsC[:, 64 + cht:65 + cht], ALU.mult, gy, ALU.add)
            k.drain("vector")
            k.wait("vector", c.sW, g.get("y3w", 0))
            ty3 = tt_op(k, "vector", c.aKtm[:, 0:512], gy, c.M32[:, 512:1024],
                        ALU.mult, sem=c.sVE)
            g["y3v"] = ty3
            k.wait("sync", c.sVE, ty3)
            g["y3w"] = k.dma("sync", c.y3b[cht * 128:(cht + 1) * 128,
                                           tc * 512:(tc + 1) * 512],
                             c.aKtm[:, 0:512], c.sW)
            k.wait("sync", c.sW, g["y3w"])
        yield
    c.tok_mamba = (c.sVE.n, c.sW.n, c.sGP.n)
    while True:
        yield


def l3_router(k, c):
    tgw = k.dma("sync", c.gwb[:, 0:256].rearrange("p (a b) -> p a b", b=32),
                c.l3gw.rearrange("(a p) e -> p a e", p=128), c.sW)
    k.wait("sync", c.sW, tgw)
    k.wait("tensor", c.sW, tgw)
    tdma_prev = 0
    ttr_prev = 0
    for tt in range(16):
        for dt in range(8):
            mm(k, c.ps[4][:, 0:32],
               c.xnbf[:, dt * T + tt * 128:dt * T + (tt + 1) * 128],
               c.gwb[:, dt * 32:(dt + 1) * 32], start=(dt == 0), stop=(dt == 7))
        tlg = k.inc("tensor", c.sMM)
        k.wait("scalar", c.sMM, tlg)
        k.wait("scalar", c.sVE, ttr_prev)
        tsg = act_op(k, c.F32[:, 0:32], c.ps[4][:, 0:32], ACT.Sigmoid, sem=c.sSC)
        k.wait("vector", c.sSC, tsg)
        k.raw("vector", lambda e, o=c.F32[:, 32:40].rearrange("p (g o) -> p g o", o=1),
              i=c.F32[:, 0:32].rearrange("p (g e) -> p g e", e=4):
              e.reduce_sum(o, i, axis=AX.X))
        k.drain("vector")
        copy_op(k, "vector", c.F32[:, 40:48], c.F32[:, 32:40])
        k.drain("vector")
        for i in range(3):
            k.raw("vector", lambda e, o=c.F32[:, 48:49], i2=c.F32[:, 40:48]:
                  e.reduce_max(o, i2, axis=AX.X))
            k.drain("vector")
            stt_op(k, "vector", c.F32[:, 40:48], c.F32[:, 40:48], c.F32[:, 48:49],
                   ALU.is_lt, c.F32[:, 40:48], ALU.mult)
            k.drain("vector")
        k.raw("vector", lambda e, o=c.F32[:, 48:49], i2=c.F32[:, 40:48]:
              e.reduce_max(o, i2, axis=AX.X))
        k.drain("vector")
        ts_op(k, "vector", c.F32[:, 40:48], c.F32[:, 32:40], c.F32[:, 48:49],
              ALU.is_ge)
        k.drain("vector")
        tt_op(k, "vector", c.F32[:, 64:96].rearrange("p (g e) -> p g e", e=4),
              c.F32[:, 0:32].rearrange("p (g e) -> p g e", e=4),
              c.F32[:, 40:48].rearrange("p (g o) -> p g o", o=1).broadcast_to((128, 8, 4)),
              ALU.mult)
        k.drain("vector")
        copy_op(k, "vector", c.F32[:, 96:128], c.F32[:, 64:96])
        k.drain("vector")
        for i in range(7):
            k.raw("vector", lambda e, o=c.F32[:, 48:49], i2=c.F32[:, 96:128]:
                  e.reduce_max(o, i2, axis=AX.X))
            k.drain("vector")
            stt_op(k, "vector", c.F32[:, 96:128], c.F32[:, 96:128], c.F32[:, 48:49],
                   ALU.is_lt, c.F32[:, 96:128], ALU.mult)
            k.drain("vector")
        k.raw("vector", lambda e, o=c.F32[:, 48:49], i2=c.F32[:, 96:128]:
              e.reduce_max(o, i2, axis=AX.X))
        k.drain("vector")
        ts_op(k, "vector", c.F32[:, 96:128], c.F32[:, 64:96], c.F32[:, 48:49],
              ALU.is_ge)
        k.drain("vector")
        tt_op(k, "vector", c.F32[:, 96:128], c.F32[:, 96:128], c.F32[:, 64:96],
              ALU.mult)
        k.drain("vector")
        k.raw("vector", lambda e, o=c.F32[:, 48:49].rearrange("p (g o) -> p g o", o=1),
              i=c.F32[:, 96:128].rearrange("p (g e) -> p g e", e=32):
              e.reduce_sum(o, i, axis=AX.X))
        k.drain("vector")
        ts_op(k, "vector", c.F32[:, 48:49], c.F32[:, 48:49], 1e-12, ALU.max)
        k.drain("vector")
        k.raw("vector", lambda e, o=c.F32[:, 49:50], i2=c.F32[:, 48:49]:
              e.reciprocal(o, i2))
        k.drain("vector")
        tcg = ts_op(k, "vector", c.bfS[:, 0:32], c.F32[:, 96:128], c.F32[:, 49:50],
                    ALU.mult, sem=c.sVE)
        ttr_prev = tcg
        k.wait("tensor", c.sVE, tcg)
        tr2 = tr_op(k, c.ps7b[0:32, 0:128], c.bfS[:, 0:32], c.cIDTB[:], sem=c.sMM)
        k.wait("scalar", c.sMM, tr2)
        k.wait("scalar", c.sW, tdma_prev)
        tev = copy_op(k, "scalar", c.F32[0:32, 128:256], c.ps7b[0:32, 0:128],
                      sem=c.sSC)
        k.wait("sync", c.sSC, tev)
        tdma_prev = k.dma("sync", c.ctr32[tt], c.F32[0:32, 128:256], c.sW)
        k.wait("sync", c.sW, tdma_prev)
    c.tok_router3 = tdma_prev


def l3_moe(k, c):
    egv = c.l3eg.rearrange("e (a p) f -> e p a f", p=128)
    euv = c.l3eu.rearrange("e (a p) f -> e p a f", p=128)
    edv = c.l3ed.rearrange("e (a p) o -> e p a o", p=128)
    sgv = c.l3sg.rearrange("(a p) f -> p a f", p=128)
    suv = c.l3su.rearrange("(a p) f -> p a f", p=128)
    sdv = c.l3sd.rearrange("(a p) o -> p a o", p=128)
    ytok = {}
    wtok = {}
    gtok = {}
    stok = 0
    for e in range(33):
        shared = (e == 32)
        if not shared:
            pump_until(c, f"tok_l3e{e}")
            k.wait("sync", c.sD2, getattr(c, f"tok_l3e{e}"))
        for tf in range(4):
            tfs = slice(tf * 512, (tf + 1) * 512)
            if not shared:
                k.wait("sync", c.sMM, gtok.get("crow", 0))
                tcr = k.dma("sync",
                            c.crow5[0:1, 0:512].rearrange("p (a b) -> p a b", b=128),
                            c.ctr32[tf * 4:(tf + 1) * 4, e:e + 1, :], c.sW)
                k.wait("sync", c.sW, tcr)
                k.wait("tensor", c.sW, tcr)
                k.wait("tensor", c.sSC, stok)
                mm(k, c.ps[4][:, :], c.cONESROW[0:1, 0:128], c.crow5[0:1, 0:512],
                   start=True, stop=True)
                tcm = k.inc("tensor", c.sMM)
                gtok["crow"] = tcm
                k.wait("scalar", c.sMM, tcm)
                k.wait("scalar", c.sGP, gtok.get("h2", 0))
                stok = copy_op(k, "scalar", c.MF[:, 512:1024], c.ps[4][:, :],
                               sem=c.sSC)
                k.wait("gpsimd", c.sSC, stok)
            for ft in range(4):
                bsel = ft % 2
                k.wait("sync", c.sMM, wtok.get(f"w{bsel}", 0))
                if shared:
                    tw1 = k.dma("sync", c.wbf[bsel][:, 0:1024].rearrange(
                        "p (a b) -> p a b", b=128),
                        sgv[:, :, ft * 128:(ft + 1) * 128], c.sW)
                    tw2 = k.dma("sync", c.wbg[bsel][:, 0:1024].rearrange(
                        "p (a b) -> p a b", b=128),
                        suv[:, :, ft * 128:(ft + 1) * 128], c.sW)
                else:
                    tw1 = k.dma("sync", c.wbf[bsel][:, 0:1024].rearrange(
                        "p (a b) -> p a b", b=128),
                        egv[e][:, :, ft * 128:(ft + 1) * 128], c.sW)
                    tw2 = k.dma("sync", c.wbg[bsel][:, 0:1024].rearrange(
                        "p (a b) -> p a b", b=128),
                        euv[e][:, :, ft * 128:(ft + 1) * 128], c.sW)
                k.wait("sync", c.sW, tw2)
                k.wait("tensor", c.sW, tw2)
                pb1 = c.ps[ft % 2]
                pb2 = c.ps[2 + ft % 2]
                ent = gtok.get(ft % 2)
                if ent is not None:
                    k.wait("tensor", c.sVE, ent[0])
                    k.wait("tensor", c.sSC, ent[1])
                entp = ytok.get(f"pb{ft % 2}")
                if entp is not None:
                    k.wait("tensor", entp[0], entp[1])
                k.wait("tensor", c.sGP, gtok.get(f"hg{ft % 2}", 0))
                k.wait("tensor", c.sMM, ytok.get(f"hs{ft}", 0))
                for kt in range(8):
                    ksl = slice(kt * T + tf * 512, kt * T + (tf + 1) * 512)
                    mm(k, pb1[:, :], c.wbf[bsel][:, kt * 128:(kt + 1) * 128],
                       c.xnbf[:, ksl], start=(kt == 0), stop=(kt == 7))
                    mm(k, pb2[:, :], c.wbg[bsel][:, kt * 128:(kt + 1) * 128],
                       c.xnbf[:, ksl], start=(kt == 0), stop=(kt == 7))
                th = k.inc("tensor", c.sMM)
                wtok[f"w{bsel}"] = th
                k.wait("scalar", c.sMM, th)
                k.wait("scalar", c.sGP, gtok.get(f"hg{ft % 2}", 0))
                tg1 = act_op(k, c.hC[:, ft * 512:(ft + 1) * 512], pb1[:, :],
                             ACT.Silu, sem=c.sSC)
                k.wait("vector", c.sSC, tg1)
                k.wait("vector", c.sMM, th)
                k.wait("vector", c.sGP, gtok.get("h2", 0))
                thh = tt_op(k, "vector", c.MF[:, 0:512],
                            c.hC[:, ft * 512:(ft + 1) * 512], pb2[:, :],
                            ALU.mult, sem=c.sVE)
                gtok[ft % 2] = (thh, tg1)
                k.wait("gpsimd", c.sVE, thh)
                if shared:
                    th2 = copy_op(k, "gpsimd", c.hC[:, ft * 512:(ft + 1) * 512],
                                  c.MF[:, 0:512], sem=c.sGP)
                else:
                    th2 = tt_op(k, "gpsimd", c.hC[:, ft * 512:(ft + 1) * 512],
                                c.MF[:, 0:512], c.MF[:, 512:1024], ALU.mult,
                                sem=c.sGP)
                gtok[f"hg{ft % 2}"] = th2
                gtok["h2"] = th2
                c.pump(1)
                c.pump2(1)
            thp = k.inc("gpsimd", c.sGP)
            k.wait("tensor", c.sGP, thp)
            for ot in range(8):
                rk = f"y{ot}_{tf}"
                if e > 0:
                    k.wait("sync", c.sW, ytok.get(rk + "w", 0))
                    k.wait("sync", c.sVE, ytok.get("yin", 0))
                    tyi = k.dma("sync", c.MF[:, 1024:1536], c.yacc[ot][:, tfs],
                                c.sW)
                    k.wait("sync", c.sW, tyi)
                    k.wait("vector", c.sW, tyi)
                pb = c.ps[ot % 2]
                entp = ytok.get(f"pb{ot % 2}")
                if entp is not None:
                    k.wait("tensor", entp[0], entp[1])
                k.wait("tensor", c.sSC, gtok.get("lastg", 0))
                bsel = ot % 2
                k.wait("sync", c.sMM, wtok.get(f"w{bsel}", 0))
                if shared:
                    tw = k.dma("sync", c.wbf[bsel][:, 0:512].rearrange(
                        "p (a b) -> p a b", b=128),
                        sdv[:, :, ot * 128:(ot + 1) * 128], c.sW)
                else:
                    tw = k.dma("sync", c.wbf[bsel][:, 0:512].rearrange(
                        "p (a b) -> p a b", b=128),
                        edv[e][:, :, ot * 128:(ot + 1) * 128], c.sW)
                k.wait("sync", c.sW, tw)
                k.wait("tensor", c.sW, tw)
                for ft in range(4):
                    mm(k, pb[:, :], c.wbf[bsel][:, ft * 128:(ft + 1) * 128],
                       c.hC[:, ft * 512:(ft + 1) * 512],
                       start=(ft == 0), stop=(ft == 3))
                tym = k.inc("tensor", c.sMM)
                wtok[f"w{bsel}"] = tym
                for ft in range(4):
                    ytok[f"hs{ft}"] = tym
                if e == 0:
                    k.wait("scalar", c.sMM, tym)
                    k.wait("scalar", c.sW, ytok.get(f"evw{ot % 2}", 0))
                    tev = copy_op(k, "scalar", c.MF[:, 1024:1536], pb[:, :],
                                  sem=c.sSC)
                    ytok[f"pb{ot % 2}"] = (c.sSC, tev)
                    k.wait("sync", c.sSC, tev)
                else:
                    k.wait("vector", c.sMM, tym)
                    tev = tt_op(k, "vector", c.MF[:, 1024:1536],
                                c.MF[:, 1024:1536], pb[:, :], ALU.add, sem=c.sVE)
                    ytok["yin"] = tev
                    ytok[f"pb{ot % 2}"] = (c.sVE, tev)
                    k.wait("sync", c.sVE, tev)
                tyw = k.dma("sync", c.yacc[ot][:, tfs], c.MF[:, 1024:1536], c.sW)
                k.wait("sync", c.sW, tyw)
                ytok[rk + "w"] = tyw
                ytok[f"evw{ot % 2}"] = tyw
                c.pump(1)
                c.pump2(1)
            gtok["lastg"] = stok
    c.tok_moey = ytok


def l3_wout_combine(k, c):
    while True:
        if hasattr(c, "tok_mamba"):
            break
        c.pump2(1)
    tve, tsw, tgp = c.tok_mamba
    k.wait("sync", c.sW, tsw)
    k.wait("sync", c.sVE, tve)
    wtok = {}
    tsc_prev = 0
    tve_prev = 0
    for tc in range(4):
        for og in range(2):
            for cht in range(16):
                bsel = cht % 2
                k.wait("sync", c.sMM, wtok.get(f"s{bsel}", 0))
                tw = k.dma("sync", c.wbf[bsel][:, 0:512],
                           c.l3wo[cht * 128:(cht + 1) * 128, og * 512:(og + 1) * 512],
                           c.sW)
                k.wait("sync", c.sMM, wtok.get(f"m{bsel}", 0))
                tmv = k.dma("sync", c.wbg[bsel][:, 0:512],
                            c.y3b[cht * 128:(cht + 1) * 128, tc * 512:(tc + 1) * 512],
                            c.sW)
                k.wait("sync", c.sW, tmv)
                k.wait("tensor", c.sW, tmv)
                if cht == 0:
                    k.wait("tensor", c.sSC, tsc_prev)
                    k.wait("tensor", c.sVE, tve_prev)
                for oo in range(4):
                    mm(k, c.ps[oo][:, :], c.wbf[bsel][:, oo * 128:(oo + 1) * 128],
                       c.wbg[bsel][:, 0:512], start=(cht == 0), stop=(cht == 15))
                tmm = k.inc("tensor", c.sMM)
                wtok[f"s{bsel}"] = tmm
                wtok[f"m{bsel}"] = tmm
            tmm = c.sMM.n
            k.wait("scalar", c.sMM, tmm)
            for oo in range(4):
                copy_op(k, "scalar",
                        c.sres[:, (og * 4 + oo) * T + tc * 512:
                               (og * 4 + oo) * T + (tc + 1) * 512],
                        c.ps[oo][:, :])
            tsc_prev = k.inc("scalar", c.sSC)
            k.wait("tensor", c.sSC, tsc_prev)
    tsf = tsc_prev
    k.wait("vector", c.sSC, tsf)
    ytok = c.tok_moey
    ct = {}
    for tf in range(4):
        tfs = slice(tf * 512, (tf + 1) * 512)
        for ot in range(8):
            k.wait("sync", c.sW, ytok.get(f"y{ot}_{tf}w", 0))
            k.wait("sync", c.sSC, ct.get("cmb", 0))
            tyi = k.dma("sync", c.F32[:, 1536:2048], c.yacc[ot][:, tfs], c.sW)
            k.wait("sync", c.sW, tyi)
            k.wait("scalar", c.sW, tyi)
            k.wait("scalar", c.sVE, ct.get("cmbv", 0))
            tsi = act_op(k, c.hC[:, 0:512], c.F32[:, 1536:2048], ACT.Silu,
                         sem=c.sSC)
            ct["cmb"] = tsi
            k.wait("vector", c.sSC, tsi)
            sl = slice(ot * T + tf * 512, ot * T + (tf + 1) * 512)
            tt_op(k, "vector", c.F32[:, 512:1024], c.sres[:, sl], c.hC[:, 0:512],
                  ALU.mult)
            k.drain("vector")
            tcx = tt_op(k, "vector", c.xres[:, sl], c.xres[:, sl],
                        c.F32[:, 512:1024], ALU.add, sem=c.sVE)
            ct["cmbv"] = tcx
    c.tok_l3 = k.inc("vector", c.sVE)

def stage_inputs(inputs):
    g = {}
    f32 = np.float32
    t = lambda a: np.ascontiguousarray(np.asarray(a, f32).T)
    import ml_dtypes
    g["c_ident"] = np.eye(16, dtype=f32)
    g["c_identb"] = np.eye(128).astype(ml_dtypes.bfloat16)
    onr = np.zeros((65, 128), dtype=f32); onr[0] = 1.0; onr[64] = 1.0
    g["c_onesrow"] = onr
    g["c_onescolb"] = np.ones((128, 1)).astype(ml_dtypes.bfloat16)
    jj, ii = np.meshgrid(np.arange(128), np.arange(128), indexing="ij")
    g["c_mbig"] = np.where(ii >= jj, 0.0, -1e30).astype(f32)
    repl = np.zeros((16, 8 * 128), dtype=f32)
    for dt in range(8):
        for m in range(128):
            repl[2 * dt + m // 64, dt * 128 + m] = 1.0
    g["c_repl2"] = repl
    import math as _m
    ac = np.zeros((128, 20), dtype=f32)
    for i, v in enumerate(ACT_CONSTS):
        ac[:, i] = v
    g["c_actc"] = ac
    g["L0WqT"] = t(inputs["L0_attn_Wq"]); g["L0WkT"] = t(inputs["L0_attn_Wk"])
    g["L0WvT"] = t(inputs["L0_attn_Wv"]); g["L0WoT"] = t(inputs["L0_attn_Wo"])
    g["L0WwT"] = t(inputs["L0_attn_Ww"]); g["L0WeT"] = t(inputs["L0_attn_We"])
    g["L0bw"] = np.asarray(inputs["L0_attn_bw"], f32).reshape(16, 1)
    g["L0be"] = np.asarray(inputs["L0_attn_be"], f32).reshape(16, 1)
    g["L0W1T"] = t(inputs["L0_ffn_W1"]); g["L0W2T"] = t(inputs["L0_ffn_W2"])
    g["L0b1"] = np.ascontiguousarray(np.asarray(inputs["L0_ffn_b1"], f32).reshape(32, 128).T)
    g["L0b2"] = np.ascontiguousarray(np.asarray(inputs["L0_ffn_b2"], f32).reshape(8, 128).T)
    g["L1Wq"] = np.ascontiguousarray(np.asarray(inputs["L1_attn_Wq"], f32))
    g["L1Wk"] = np.ascontiguousarray(np.asarray(inputs["L1_attn_Wk"], f32))
    g["L1Wv"] = np.ascontiguousarray(np.asarray(inputs["L1_attn_Wv"], f32))
    omgt = np.asarray(inputs["L1_attn_omega"], f32).T
    omg2 = np.zeros((128, 256), dtype=f32)
    omg2[0:64] = omgt; omg2[64:128] = omgt
    g["c_omgt2"] = omg2.astype(ml_dtypes.bfloat16)
    g["c_negh"] = np.full((128, 1), -0.5, dtype=f32)
    g["L1rw"] = np.ascontiguousarray(np.asarray(inputs["L1_ffn_router_w"], f32))
    g["L1w1"] = np.ascontiguousarray(np.asarray(inputs["L1_ffn_w1"], f32))
    g["L1v1"] = np.ascontiguousarray(np.asarray(inputs["L1_ffn_v1"], f32))
    g["L1w2"] = np.ascontiguousarray(np.asarray(inputs["L1_ffn_w2"], f32))
    g["L1bias"] = np.ascontiguousarray(np.asarray(inputs["L1_ffn_bias"], f32).reshape(8, 128).T)
    g["L2WqT"] = t(inputs["L2_attn_Wq"]); g["L2WkT"] = t(inputs["L2_attn_Wk"])
    g["L2WvT"] = t(inputs["L2_attn_Wv"]); g["L2WuT"] = t(inputs["L2_attn_Wu"])
    g["L2se"] = np.ascontiguousarray(np.asarray(inputs["L2_ffn_slot_embeds"], f32).reshape(4096, D))
    g["L2w1"] = np.ascontiguousarray(np.asarray(inputs["L2_ffn_w1"], f32))
    g["L2w2"] = np.ascontiguousarray(np.asarray(inputs["L2_ffn_w2"], f32))
    g["L2gx"] = np.ascontiguousarray(np.asarray(inputs["L2_ffn_gamma_x"], f32).reshape(8, 128).T)
    g["L2bu"] = np.ascontiguousarray(np.asarray(inputs["L2_attn_bu"], f32).reshape(8, 128).T)
    g["L2gs"] = np.ascontiguousarray(np.asarray(inputs["L2_ffn_gamma_slot"], f32).reshape(1, D))
    g["c_identf"] = np.eye(128, dtype=f32)
    g["L3WinT"] = t(inputs["L3_attn_W_in"])
    g["L3WoutT"] = t(inputs["L3_attn_W_out"])
    g["L3WxT"] = t(inputs["L3_attn_W_x"])
    g["L3WdtT"] = t(inputs["L3_attn_W_dt"])
    cw = np.asarray(inputs["L3_attn_conv_w"], f32)
    g["L3cw"] = np.ascontiguousarray(
        cw.reshape(16, 128, 4).transpose(1, 0, 2).reshape(128, 64))
    g["L3cb"] = np.ascontiguousarray(np.asarray(inputs["L3_attn_conv_b"], f32).reshape(16, 128).T)
    g["L3D"] = np.ascontiguousarray(np.asarray(inputs["L3_attn_D"], f32).reshape(16, 128).T)
    g["L3AL"] = np.ascontiguousarray(
        np.asarray(inputs["L3_attn_A_log"], f32).reshape(16, 128, 16).transpose(1, 0, 2).reshape(128, 256))
    g["L3bdt"] = np.ascontiguousarray(np.asarray(inputs["L3_attn_b_dt"], f32).reshape(16, 128).T)
    g["L3gwT"] = t(inputs["L3_ffn_gate_w"])
    g["L3egT"] = np.ascontiguousarray(np.transpose(np.asarray(inputs["L3_ffn_expert_gate"], f32), (0, 2, 1)))
    g["L3euT"] = np.ascontiguousarray(np.transpose(np.asarray(inputs["L3_ffn_expert_up"], f32), (0, 2, 1)))
    g["L3edT"] = np.ascontiguousarray(np.transpose(np.asarray(inputs["L3_ffn_expert_down"], f32), (0, 2, 1)))
    g["L3sgT"] = t(inputs["L3_ffn_shared_gate_w"])
    g["L3suT"] = t(inputs["L3_ffn_shared_up_w"])
    g["L3sdT"] = t(inputs["L3_ffn_shared_down_w"])
    return g


def _np_threefry2x32(k1, k2, x0, x1):
    k1 = np.uint32(k1); k2 = np.uint32(k2)
    x0 = x0.astype(np.uint32).copy(); x1 = x1.astype(np.uint32).copy()
    ks = [k1, k2, np.uint32(k1 ^ k2 ^ np.uint32(0x1BD11BDA))]
    rot = [(13, 15, 26, 6), (17, 29, 16, 24)]
    x0 += ks[0]; x1 += ks[1]
    for i in range(5):
        for r in rot[i % 2]:
            x0 += x1
            x1 = ((x1 << np.uint32(r)) | (x1 >> np.uint32(32 - r))).astype(np.uint32)
            x1 ^= x0
        x0 += ks[(i + 1) % 3]
        x1 += ks[(i + 2) % 3] + np.uint32(i + 1)
    return x0, x1


def _dropout_masks():
    import ml_dtypes
    n = B * T * D
    idx = np.arange(n, dtype=np.uint64)
    c1 = (idx >> np.uint64(32)).astype(np.uint32)
    c2 = (idx & np.uint64(0xFFFFFFFF)).astype(np.uint32)
    b1, b2 = _np_threefry2x32(0, 42, c1, c2)
    if os.environ.get("BASS_SIM_FP64") == "1":
        bits = (b1.astype(np.uint64) << np.uint64(32)) | b2.astype(np.uint64)
        u = ((bits >> np.uint64(12)) | np.uint64(0x3FF0000000000000)).view(np.float64) - 1.0
        keep = (u < 0.8).reshape(B, T, D)
    else:
        bits = b1 ^ b2
        u = (((bits >> np.uint32(9)) | np.uint32(0x3F800000)).view(np.float32)
             - np.float32(1.0))
        keep = (u < np.float32(0.8)).reshape(B, T, D)
    scale = np.where(keep, 1.25, 0.0).astype(ml_dtypes.bfloat16)
    return [np.ascontiguousarray(scale[b].T) for b in range(B)]


def kernel(**inputs):
    x = np.asarray(inputs["L0_attn_x"], np.float32)
    shared = stage_inputs(inputs)
    drops = _dropout_masks()
    nc = build_program()
    in_maps = []
    for b in range(B):
        m = dict(shared)
        m["x_fm"] = np.ascontiguousarray(x[b].T)
        m["dropscale"] = drops[b]
        in_maps.append(m)
    res = run_bass_kernel_spmd(nc, in_maps, core_ids=list(range(B)))
    out = np.empty((B, T, D), dtype=np.float32)
    for b in range(B):
        out[b] = res.results[b]["out_fm"].T
    return out
